# version 02
# baseline (speedup 1.0000x reference)
import contextlib
import math
import numpy as np
import ml_dtypes
import concourse.bass as bass
import concourse.mybir as mybir
from concourse.bass_utils import run_bass_kernel_spmd

F32 = mybir.dt.float32
BF16 = mybir.dt.bfloat16
I32 = mybir.dt.int32
AF = mybir.ActivationFunctionType
ALU = mybir.AluOpType
AX = mybir.AxisListType
BF = ml_dtypes.bfloat16

B, S, D = 2, 4096, 2048
NK, NV, DK, DV, KTAP = 16, 32, 128, 128, 4
KEY_DIM, VAL_DIM, CONV_DIM = NK * DK, NV * DV, 2 * NK * DK + NV * DV
INTER = 5632
H1, DH1 = 16, 64
LAMBDA_INIT = 0.8
C = 128


class Ctx:
    def __init__(self, nc, es):
        self.nc, self.es = nc, es
        self.sems, self.semv = {}, {}

    def sem(self, name):
        if name not in self.sems:
            self.sems[name] = self.es.enter_context(self.nc.semaphore(name))
            self.semv[name] = 0
        return self.sems[name]

    def inc(self, inst, name, n=1):
        s = self.sem(name)
        self.semv[name] += n
        inst.then_inc(s, n)

    def wait(self, eng, *names):
        for n in names:
            eng.wait_ge(self.sem(n), self.semv[n])

    def vsync(self, v, inst):
        s = self.sem("vself")
        self.semv["vself"] += 1
        inst.then_inc(s, 1)
        v.wait_ge(s, self.semv["vself"])

    def sb(self, name, shape, dt):
        return self.es.enter_context(self.nc.sbuf_tensor(name, shape, dt))

    def ps(self, name, shape, dt=F32):
        return self.es.enter_context(self.nc.psum_tensor(name, shape, dt))


def _consts():
    i_ = np.arange(128)
    c = {}
    c["ident"] = np.eye(128, dtype=np.float32)
    c["allones"] = np.ones((128, 128), np.float32)
    up_s = (i_[:, None] < i_[None, :]).astype(np.float32)
    lo_s = (i_[None, :] < i_[:, None]).astype(np.float32)
    up_i = (i_[:, None] <= i_[None, :]).astype(np.float32)
    c["mus8"] = np.broadcast_to(up_s[:, None, :], (128, 8, 128)).copy()
    c["mls8"] = np.broadcast_to(lo_s[:, None, :], (128, 8, 128)).copy()
    c["mui8"] = np.broadcast_to(up_i[:, None, :], (128, 8, 128)).copy()
    c["trilones"] = up_i.copy()
    oh8 = np.zeros((128, 8, 8), np.float32)
    for j in range(8):
        oh8[:, j, j] = 1.0
    c["onehot8"] = oh8
    return c


def build_l0attn(dbg=False):
    NKH, NVH = 4, 8
    QC, VC = NKH * DK, NVH * DV
    SEG = 512
    NSEG = S // SEG
    CPS = SEG // C
    LN2I8 = -0.5 * math.log(128.0)

    nc = bass.Bass("TRN2", target_bir_lowering=False)
    x_in = nc.dram_tensor("x", [D, S], F32, kind="ExternalInput").ap()
    wqT = nc.dram_tensor("wqT", [D, QC], F32, kind="ExternalInput").ap()
    wkT = nc.dram_tensor("wkT", [D, QC], F32, kind="ExternalInput").ap()
    wvT = nc.dram_tensor("wvT", [D, VC], F32, kind="ExternalInput").ap()
    wzT = nc.dram_tensor("wzT", [D, VC], F32, kind="ExternalInput").ap()
    wbaT = nc.dram_tensor("wbaT", [D, 16], F32, kind="ExternalInput").ap()
    cwT = nc.dram_tensor("cwT", [128, KTAP, 16], F32, kind="ExternalInput").ap()
    nwrep = nc.dram_tensor("nwrep", [1, VC], F32, kind="ExternalInput").ap()
    alog = nc.dram_tensor("alog", [1, NVH], F32, kind="ExternalInput").ap()
    dtb = nc.dram_tensor("dtb", [1, NVH], F32, kind="ExternalInput").ap()
    ident = nc.dram_tensor("ident", [128, 128], F32, kind="ExternalInput").ap()
    allones = nc.dram_tensor("allones", [128, 128], F32, kind="ExternalInput").ap()
    mus8 = nc.dram_tensor("mus8", [128, 8, 128], F32, kind="ExternalInput").ap()
    mui8 = nc.dram_tensor("mui8", [128, 8, 128], F32, kind="ExternalInput").ap()
    tril = nc.dram_tensor("trilones", [128, 128], F32, kind="ExternalInput").ap()
    oh8 = nc.dram_tensor("onehot8", [128, 8, 8], F32, kind="ExternalInput").ap()
    og_out = nc.dram_tensor("og", [S, VC], BF16, kind="ExternalOutput").ap()

    dbg_t = {}
    if dbg:
        for nm, shp, dt_ in [
            ("lnx", [128, 16, 512], BF16), ("qkv", [128, 16, 512], BF16),
            ("zt", [128, 4, 1024], BF16), ("gba", [128, 4, 16], F32),
            ("lrqk", [8, 512], F32), ("combo", [128, 32], F32),
            ("lgam", [128, 8], F32), ("glog", [128, 8], F32),
            ("negb", [128, 8], F32), ("e8a", [128, 8, 128], BF16),
            ("ap", [128, 8, 128], BF16), ("au", [128, 8, 128], BF16),
            ("m8", [128, 8, 128], BF16), ("kk8", [128, 8, 128], BF16),
            ("vtm", [128, 8, 128], BF16), ("ktm", [128, 4, 128], BF16),
            ("rhs", [128, 8, 128], BF16), ("uu", [128, 8, 128], BF16),
            ("u3", [128, 8, 128], BF16), ("osum", [128, 8], F32),
            ("osc", [128, 8], F32), ("ogt", [128, 8, 128], BF16),
            ("og1", [128, 1024], BF16), ("stf", [128, 8, 128], F32),
            ("erowm", [128, 8], F32), ("eu3", [128, 8], F32),
            ("nrhs", [128, 8], F32), ("rows8", [1, 2048], F32),
            ("lrtm", [128, 8], F32), ("lke", [128, 8], F32), ("lqe", [128, 8], F32),
        ]:
            dbg_t[nm] = nc.dram_tensor("dbg_" + nm, shp, dt_, kind="ExternalOutput").ap()

    es = contextlib.ExitStack()
    with es:
        ctx = Ctx(nc, es)

        def dump(blk_, name, tile, wait_name):
            if not dbg:
                return
            @blk_.sync
            def _(s):
                ctx.wait(s, wait_name)
                ctx.inc(s.dma_start(out=dbg_t[name], in_=tile), "dbg_st", 16)

        wq_s = ctx.sb("wq_s", [128, 16, QC], BF16)
        wk_s = ctx.sb("wk_s", [128, 16, QC], BF16)
        wv_s = ctx.sb("wv_s", [128, 16, VC], BF16)
        wzh = ctx.sb("wzh", [128, 16, 512], BF16)
        wba_s = ctx.sb("wba_s", [128, 16, 16], BF16)
        cw_s = ctx.sb("cw_s", [128, KTAP, 16], F32)
        cvg = ctx.sb("cvg", [128, 4, 4, 128], BF16)
        id_s = ctx.sb("id_s", [128, 128], BF16)
        idf_s = ctx.sb("idf_s", [128, 128], F32)
        ao_s = ctx.sb("ao_s", [128, 128], BF16)
        aof_s = ctx.sb("aof_s", [128, 128], F32)
        trif_s = ctx.sb("trif_s", [128, 128], F32)
        mus_s = ctx.sb("mus_s", [128, 8, 128], BF16)
        mui_s = ctx.sb("mui_s", [128, 8, 128], BF16)
        tri_s = ctx.sb("tri_s", [128, 128], BF16)
        oh8_s = ctx.sb("oh8_s", [128, 8, 8], BF16)
        alog_s = ctx.sb("alog_s", [1, NVH], F32)
        dtb_s = ctx.sb("dtb_s", [1, NVH], F32)
        nwr_s = ctx.sb("nwr_s", [1, VC], BF16)
        dtb_b = ctx.sb("dtb_b", [128, NVH], F32)
        nega_b = ctx.sb("nega_b", [128, NVH], F32)
        nw_b = ctx.sb("nw_b", [128, VC], BF16)
        xb = ctx.sb("xb", [128, 16, SEG], BF16)
        sqb = ctx.sb("sqb", [128, 4, SEG], BF16)
        mu_t = ctx.sb("mu_t", [128, SEG], F32)
        var_t = ctx.sb("var_t", [128, SEG], F32)
        mixr = ctx.sb("mixr", [128, 16, 3 + SEG], BF16)
        qkv = ctx.sb("qkv", [128, 16, SEG], BF16)
        zt = ctx.sb("zt", [128, CPS, VC], BF16)
        gba4 = ctx.sb("gba4", [128, CPS, 16], F32)
        lrqk = ctx.sb("lrqk", [8, SEG], F32)
        og1 = ctx.sb("og1", [128, VC], BF16)
        combo = ctx.sb("combo", [128, 32], F32)
        rows8 = ctx.sb("rows8", [1, 2048], F32)
        negb = ctx.sb("negb", [128, NVH], F32)
        glog = ctx.sb("glog", [128, NVH], F32)
        lgam = ctx.sb("lgam", [128, NVH], F32)
        e_u3 = ctx.sb("e_u3", [128, NVH], F32)
        erowM = ctx.sb("erowM", [128, NVH], F32)
        lgCb = ctx.sb("lgCb", [128, NVH], F32)
        nrhs = ctx.sb("nrhs", [128, NVH], F32)
        gamC = ctx.sb("gamC", [128, NVH], F32)
        osum = ctx.sb("osum", [128, NVH], F32)
        osc = ctx.sb("osc", [128, NVH], F32)
        t8a = ctx.sb("t8a", [128, NVH], F32)
        t8b = ctx.sb("t8b", [128, NVH], F32)
        lr_tm = ctx.sb("lr_tm", [128, 8], F32)
        lk_e = ctx.sb("lk_e", [128, NVH], F32)
        lq_e = ctx.sb("lq_e", [128, NVH], F32)
        kk8 = ctx.sb("kk8", [128, 8, 128], BF16)
        m8 = ctx.sb("m8", [128, 8, 128], BF16)
        ktm = ctx.sb("ktm", [128, NKH, 128], BF16)
        vtm = ctx.sb("vtm", [128, NVH, 128], BF16)
        e8a = ctx.sb("e8a", [128, 8, 128], BF16)
        ap_t = ctx.sb("ap_t", [128, 8, 128], BF16)
        au_t = ctx.sb("au_t", [128, 8, 128], BF16)
        pp_t = ctx.sb("pp_t", [128, 8, 128], BF16)
        pu_t = ctx.sb("pu_t", [128, 8, 128], BF16)
        u_t = ctx.sb("u_t", [128, 8, 128], BF16)
        u3_t = ctx.sb("u3_t", [128, 8, 128], BF16)
        st_hi = ctx.sb("st_hi", [128, 8, 128], BF16)
        st_lo = ctx.sb("st_lo", [128, 8, 128], BF16)
        st_f = ctx.sb("st_f", [128, 8, 128], F32)
        ogt = ctx.sb("ogt", [128, 8, 128], BF16)
        pb = [ctx.ps(f"pb{i}", [128, 512]) for i in range(6)]
        pvt = ctx.ps("pvt", [128, 16, 128], BF16)

        blk = es.enter_context(nc.Block())

        @blk.gpsimd
        def _(g):
            for dst, src_ in [
                (id_s, ident), (idf_s, ident), (ao_s, allones), (aof_s, allones),
                (mus_s, mus8),
                (mui_s, mui8), (tri_s, tril), (trif_s, tril), (oh8_s, oh8),
                (nwr_s, nwrep), (alog_s, alog), (dtb_s, dtb), (cw_s, cwT),
                (wq_s, wqT.rearrange("(c p) q -> p c q", p=128)),
                (wk_s, wkT.rearrange("(c p) q -> p c q", p=128)),
                (wv_s, wvT.rearrange("(c p) q -> p c q", p=128)),
                (wba_s, wbaT.rearrange("(c p) q -> p c q", p=128)),
            ]:
                ctx.inc(g.dma_start(out=dst[:], in_=src_), "c_ld", 16)

        @blk.scalar
        def _(sc):
            ctx.wait(sc, "c_ld")
            ctx.inc(sc.activation(alog_s[:], alog_s[:], AF.Exp), "i_s", 1)

        @blk.vector
        def _(v):
            ctx.wait(v, "i_s")
            v.tensor_scalar_mul(alog_s[:], alog_s[:], -1.0)
            v.memset(st_hi[:], 0.0)
            v.memset(st_lo[:], 0.0)
            v.memset(st_f[:], 0.0)
            ctx.inc(v.memset(mixr[:, :, 0:3], 0.0), "i_v", 1)

        @blk.tensor
        def _(t):
            ctx.wait(t, "i_v")
            t.matmul(pb[0][:, 0:NVH], aof_s[0:1, :], dtb_s[:], start=True, stop=True)
            t.matmul(pb[0][:, 16:16 + NVH], aof_s[0:1, :], alog_s[:], start=True, stop=True)
            t.matmul(pb[1][:, :], ao_s[0:1, :], nwr_s[:, 0:512], start=True, stop=True)
            ctx.inc(t.matmul(pb[2][:, :], ao_s[0:1, :], nwr_s[:, 512:1024], start=True, stop=True), "i_t", 1)

        @blk.vector
        def _(v):
            ctx.wait(v, "i_t")
            v.tensor_copy(dtb_b[:], pb[0][:, 0:NVH])
            v.tensor_copy(nega_b[:], pb[0][:, 16:16 + NVH])
            v.tensor_copy(nw_b[:, 0:512], pb[1][:, :])
            ctx.inc(v.tensor_copy(nw_b[:, 512:1024], pb[2][:, :]), "i_v2", 1)

        for seg in range(NSEG):
            t0 = seg * SEG

            @blk.gpsimd
            def _(g, t0=t0, seg=seg):
                ctx.wait(g, "i_v2")
                if seg > 0:
                    ctx.wait(g, "og_st")
                ctx.inc(g.dma_start(
                    out=xb[:], in_=x_in.rearrange("(c p) s -> p c s", p=128)[:, :, t0:t0 + SEG]),
                    "x_ld", 16)

            for grp in range(4):
                @blk.scalar
                def _(sc, grp=grp):
                    if grp == 0:
                        ctx.wait(sc, "x_ld", "i_v2")
                    else:
                        ctx.wait(sc, "ln_t")
                    for i in range(4):
                        last = sc.activation(sqb[:, i, :], xb[:, grp * 4 + i, :], AF.Square)
                    ctx.inc(last, "ln_s", 1)

                @blk.tensor
                def _(t, grp=grp):
                    ctx.wait(t, "ln_s")
                    for i in range(4):
                        k = grp * 4 + i
                        t.matmul(pb[0][:, :], ao_s[:], xb[:, k, :], start=(k == 0), stop=(k == 15))
                        mm = t.matmul(pb[1][:, :], ao_s[:], sqb[:, i, :], start=(k == 0), stop=(k == 15))
                    ctx.inc(mm, "ln_t", 1)

            @blk.vector
            def _(v):
                ctx.wait(v, "ln_t")
                v.tensor_scalar_mul(mu_t[:], pb[0][:, :], 1.0 / D)
                v.tensor_scalar(var_t[:], pb[1][:, :], 1.0 / D, 1e-5, ALU.mult, ALU.add)
                inst = v.tensor_tensor(pb[0][:, :], mu_t[:], mu_t[:], ALU.mult)
                ctx.vsync(v, inst)
                ctx.inc(v.tensor_tensor(var_t[:], var_t[:], pb[0][:, :], ALU.subtract), "ph_v", 1)

            @blk.scalar
            def _(sc):
                ctx.wait(sc, "ph_v")
                sc.activation(var_t[:], var_t[:], AF.Ln)
                ctx.inc(sc.activation(var_t[:], var_t[:], AF.Exp, scale=-0.5), "ph_s", 1)

            @blk.vector
            def _(v, seg=seg):
                ctx.wait(v, "ph_s")
                if seg > 0:
                    v.tensor_copy(mixr[:, :, 0:3], mixr[:, :, SEG:SEG + 3])
                for k in range(16):
                    v.tensor_tensor(xb[:, k, :], xb[:, k, :], mu_t[:], ALU.subtract)
                for k in range(16):
                    last = v.tensor_tensor(xb[:, k, :], xb[:, k, :], var_t[:], ALU.mult)
                ctx.inc(last, "ph_v2", 1)

            if dbg and seg == 0:
                dump(blk, "lnx", xb[:], "ph_v2")

            WSEL = [(wq_s, m, m) for m in range(4)] + \
                   [(wk_s, m, 4 + m) for m in range(4)] + \
                   [(wv_s, m, 8 + m) for m in range(8)]
            for grp in range(4):
                @blk.tensor
                def _(t, grp=grp):
                    ctx.wait(t, "ph_v2")
                    if grp > 0:
                        ctx.wait(t, "pj_v")
                    for i in range(4):
                        w, mb, dst = WSEL[grp * 4 + i]
                        for k in range(16):
                            mm = t.matmul(pb[i][:, :], w[:, k, mb * 128:(mb + 1) * 128],
                                          xb[:, k, :], start=(k == 0), stop=(k == 15))
                    ctx.inc(mm, "pj_t", 1)

                @blk.vector
                def _(v, grp=grp):
                    ctx.wait(v, "pj_t")
                    for i in range(4):
                        _, _, dst = WSEL[grp * 4 + i]
                        last = v.tensor_copy(mixr[:, dst, 3:3 + SEG], pb[i][:, :])
                    ctx.inc(last, "pj_v", 1)

            for grp in range(4):
                @blk.vector
                def _(v, grp=grp):
                    ctx.wait(v, "pj_v")
                    if grp > 0:
                        ctx.wait(v, "cv_t")
                    for i in range(4):
                        cc = grp * 4 + i
                        for j in range(KTAP):
                            last = v.tensor_scalar_mul(cvg[:, j, i, :], id_s[:], cw_s[:, j, cc:cc + 1])
                    ctx.inc(last, "cv_v", 1)

                @blk.tensor
                def _(t, grp=grp):
                    ctx.wait(t, "cv_v")
                    if grp > 0:
                        ctx.wait(t, "cv_s")
                    for i in range(4):
                        cc = grp * 4 + i
                        for j in range(KTAP):
                            mm = t.matmul(pb[i][:, :], cvg[:, j, i, :],
                                          mixr[:, cc, j:j + SEG], start=(j == 0), stop=(j == 3))
                    ctx.inc(mm, "cv_t", 1)

                @blk.scalar
                def _(sc, grp=grp):
                    ctx.wait(sc, "cv_t")
                    for i in range(4):
                        cc = grp * 4 + i
                        last = sc.activation(qkv[:, cc, :], pb[i][:, :], AF.Silu)
                    ctx.inc(last, "cv_s", 1)

            if dbg and seg == 0:
                dump(blk, "qkv", qkv[:], "cv_s")

            for half in range(2):
                @blk.gpsimd
                def _(g, half=half):
                    ctx.wait(g, "cv_s" if half == 0 else "z_t")
                    ctx.inc(g.dma_start(
                        out=wzh[:],
                        in_=wzT.rearrange("(c p) q -> p c q", p=128)[:, :, half * 512:(half + 1) * 512]),
                        "wz_ld", 16)

                for tc in range(CPS):
                    @blk.tensor
                    def _(t, half=half, tc=tc):
                        ctx.wait(t, "wz_ld")
                        if not (half == 0 and tc == 0):
                            ctx.wait(t, "z_sv")
                        for k in range(16):
                            mm = t.matmul(pb[tc % 2][:, :], xb[:, k, tc * 128:(tc + 1) * 128],
                                          wzh[:, k, :], start=(k == 0), stop=(k == 15))
                        if half == 0:
                            for k in range(16):
                                mm = t.matmul(pb[2][:, 0:16], xb[:, k, tc * 128:(tc + 1) * 128],
                                              wba_s[:, k, :], start=(k == 0), stop=(k == 15))
                        ctx.inc(mm, "z_t", 1)

                    @blk.scalar
                    def _(sc, half=half, tc=tc):
                        ctx.wait(sc, "z_t")
                        ctx.inc(sc.activation(zt[:, tc, half * 512:(half + 1) * 512],
                                              pb[tc % 2][:, :], AF.Silu), "z_s", 1)

                    @blk.vector
                    def _(v, half=half, tc=tc):
                        ctx.wait(v, "z_s")
                        if half == 0:
                            v.tensor_copy(gba4[:, tc, :], pb[2][:, 0:16])
                        ctx.inc(v.tensor_tensor(zt[:, tc, half * 512:(half + 1) * 512],
                                                zt[:, tc, half * 512:(half + 1) * 512],
                                                nw_b[:, half * 512:(half + 1) * 512], ALU.mult), "z_sv", 1)

            if dbg and seg == 0:
                dump(blk, "zt", zt[:], "z_sv")
                dump(blk, "gba", gba4[:], "z_sv")

            for grp in range(2):
                @blk.scalar
                def _(sc, grp=grp):
                    ctx.wait(sc, "z_sv" if grp == 0 else "lr_t")
                    for i in range(4):
                        last = sc.activation(sqb[:, i, :], qkv[:, grp * 4 + i, :], AF.Square)
                    ctx.inc(last, "lr_s", 1)

                @blk.tensor
                def _(t, grp=grp):
                    ctx.wait(t, "lr_s")
                    for i in range(4):
                        j = grp * 4 + i
                        mm = t.matmul(pb[0][0:8, :], oh8_s[:, j, :], sqb[:, i, :],
                                      start=(j == 0), stop=(j == 7))
                    ctx.inc(mm, "lr_t", 1)

            @blk.vector
            def _(v):
                ctx.wait(v, "lr_t")
                ctx.inc(v.tensor_scalar_add(lrqk[:], pb[0][0:8, :], 1e-6), "lr_v", 1)

            @blk.scalar
            def _(sc):
                ctx.wait(sc, "lr_v")
                ctx.inc(sc.activation(lrqk[:], lrqk[:], AF.Ln), "lr_s2", 1)

            if dbg and seg == 0:
                dump(blk, "lrqk", lrqk[:], "lr_s2")

            for tc in range(CPS):
                @blk.scalar
                def _(sc, tc=tc, seg=seg):
                    ctx.wait(sc, "lr_s2")
                    if not (seg == 0 and tc == 0):
                        ctx.wait(sc, "og_v")
                    sc.activation(t8a[:], gba4[:, tc, 0:8], AF.Exp, scale=-1.0)
                    ctx.inc(sc.activation(negb[:], t8a[:], AF.Ln, bias=1.0), "c0_s", 1)

                @blk.vector
                def _(v, tc=tc):
                    ctx.wait(v, "c0_s")
                    ctx.inc(v.tensor_tensor(t8b[:], gba4[:, tc, 8:16], dtb_b[:], ALU.add), "c0_v", 1)

                @blk.scalar
                def _(sc):
                    ctx.wait(sc, "c0_v")
                    sc.activation(t8b[:], t8b[:], AF.Exp)
                    ctx.inc(sc.activation(t8b[:], t8b[:], AF.Ln, bias=1.0), "c0_s2", 1)

                @blk.vector
                def _(v):
                    ctx.wait(v, "c0_s2")
                    ctx.inc(v.tensor_tensor(glog[:], t8b[:], nega_b[:], ALU.mult), "c0_v2", 1)

                @blk.tensor
                def _(t, tc=tc):
                    ctx.wait(t, "c0_v2")
                    t.matmul(pb[0][:, 0:8], trif_s[:], glog[:], start=True, stop=True)
                    t.matmul(pb[2][:, 0:8], aof_s[:], glog[:], start=True, stop=True)
                    ctx.inc(t.transpose(pb[1][0:128, 0:8], lrqk[:, tc * 128:(tc + 1) * 128], idf_s[0:8, 0:8]), "c0_t", 1)

                @blk.vector
                def _(v, tc=tc):
                    ctx.wait(v, "c0_t")
                    v.tensor_copy(lr_tm[:], pb[1][0:128, 0:8])
                    v.tensor_copy(lgam[:], pb[0][:, 0:8])
                    v.tensor_copy(lgCb[:], pb[2][:, 0:8])
                    v.tensor_copy(lk_e[:, 0::2], lr_tm[:, 4:8])
                    v.tensor_copy(lk_e[:, 1::2], lr_tm[:, 4:8])
                    v.tensor_copy(lq_e[:, 0::2], lr_tm[:, 0:4])
                    v.tensor_copy(lq_e[:, 1::2], lr_tm[:, 0:4])
                    v.scalar_tensor_tensor(t8a[:], lk_e[:], -0.5, negb[:], ALU.mult, ALU.subtract)
                    v.scalar_tensor_tensor(combo[:, 16:24], lq_e[:], -0.5, lgam[:], ALU.mult, ALU.add)
                    v.scalar_tensor_tensor(combo[:, 8:16], lk_e[:], -0.5, lgam[:], ALU.mult, ALU.add)
                    v.tensor_tensor(combo[:, 0:8], t8a[:], lgam[:], ALU.subtract)
                    ctx.inc(v.tensor_scalar_add(combo[:, 16:24], combo[:, 16:24], LN2I8), "c1_v", 1)

                if dbg and seg == 0 and tc == 0:
                    dump(blk, "lrtm", lr_tm[:], "c1_v")
                    dump(blk, "lke", lk_e[:], "c1_v")
                    dump(blk, "lqe", lq_e[:], "c1_v")
                    dump(blk, "combo", combo[:], "c1_v")
                    dump(blk, "lgam", lgam[:], "c1_v")
                    dump(blk, "glog", glog[:], "c1_v")
                    dump(blk, "negb", negb[:], "c1_v")

                @blk.vector
                def _(v):
                    ctx.wait(v, "c1_v")
                    ctx.inc(v.tensor_tensor(e_u3[:], combo[:, 0:8], lgCb[:], ALU.add), "c1_v2", 1)

                @blk.scalar
                def _(sc):
                    ctx.wait(sc, "c1_v2")
                    sc.activation(e_u3[:], e_u3[:], AF.Exp)
                    sc.activation(gamC[:], lgCb[:], AF.Exp)
                    sc.activation(erowM[:], combo[:, 16:24], AF.Exp)
                    ctx.inc(sc.activation(nrhs[:], combo[:, 8:16], AF.Exp), "c1_s", 1)

                @blk.tensor
                def _(t):
                    ctx.wait(t, "c1_s")
                    for h in range(16):
                        bank = pb[2 + h // 4]
                        mm = t.transpose(bank[0:1, (h % 4) * 128:(h % 4 + 1) * 128],
                                         combo[:, 8 + h:9 + h], idf_s[:])
                    ctx.inc(mm, "c1_t", 1)

                @blk.vector
                def _(v):
                    ctx.wait(v, "c1_t")
                    for h in range(16):
                        bank = pb[2 + h // 4]
                        v.tensor_copy(rows8[0:1, h * 128:(h + 1) * 128],
                                      bank[0:1, (h % 4) * 128:(h % 4 + 1) * 128])
                    ctx.inc(v.tensor_scalar_mul(nrhs[:], nrhs[:], -1.0), "c2_s", 1)

                if dbg and seg == 0 and tc == 0:
                    dump(blk, "nrhs", nrhs[:], "c2_s")
                    dump(blk, "erowm", erowM[:], "c2_s")
                    dump(blk, "eu3", e_u3[:], "c2_s")
                    dump(blk, "rows8", rows8[:], "c2_s")

                @blk.tensor
                def _(t, tc=tc):
                    ctx.wait(t, "c2_s")
                    sl = slice(tc * 128, (tc + 1) * 128)
                    for kh in range(4):
                        t.matmul(pb[0][:, kh * 128:(kh + 1) * 128], qkv[:, 4 + kh, sl], qkv[:, 4 + kh, sl], start=True, stop=True)
                        t.matmul(pb[1][:, kh * 128:(kh + 1) * 128], qkv[:, 4 + kh, sl], qkv[:, kh, sl], start=True, stop=True)
                        t.transpose(pvt[:, kh, :], qkv[:, 4 + kh, sl], id_s[:])
                    for h in range(8):
                        t.transpose(pvt[:, 4 + h, :], qkv[:, 8 + h, sl], id_s[:])
                    for h in range(16):
                        bank = pb[2 + h // 4]
                        mm = t.matmul(bank[:, (h % 4) * 128:(h % 4 + 1) * 128], aof_s[0:1, :],
                                      rows8[0:1, h * 128:(h + 1) * 128], start=True, stop=True)
                    ctx.inc(mm, "d1_t", 1)

                @blk.vector
                def _(v):
                    ctx.wait(v, "d1_t")
                    for h in range(8):
                        bank = pb[2 + h // 4]
                        dst = mu_t if h < 4 else var_t
                        v.tensor_scalar(dst[:, (h % 4) * 128:(h % 4 + 1) * 128],
                                        bank[:, (h % 4) * 128:(h % 4 + 1) * 128],
                                        combo[:, h:h + 1], 30.0, ALU.add, ALU.min)
                    ctx.inc(v.tensor_copy(kk8[:, 0::2, :], pb[0][:, :].rearrange("p (k c) -> p k c", c=128)), "d1_v", 1)

                @blk.scalar
                def _(sc):
                    ctx.wait(sc, "d1_v")
                    sc.activation(e8a[:, 0:4, :].rearrange("p k c -> p (k c)"), mu_t[:], AF.Exp)
                    ctx.inc(sc.activation(e8a[:, 4:8, :].rearrange("p k c -> p (k c)"), var_t[:], AF.Exp), "d1_s", 1)

                if dbg and seg == 0 and tc == 0:
                    dump(blk, "e8a", e8a[:], "d1_s")

                @blk.vector
                def _(v):
                    ctx.wait(v, "d1_s")
                    for h in range(8):
                        bank = pb[4 + h // 4]
                        dst = mu_t if h < 4 else var_t
                        v.tensor_scalar(dst[:, (h % 4) * 128:(h % 4 + 1) * 128],
                                        bank[:, (h % 4) * 128:(h % 4 + 1) * 128],
                                        combo[:, h:h + 1], 30.0, ALU.add, ALU.min)
                    v.tensor_copy(kk8[:, 1::2, :], pb[0][:, :].rearrange("p (k c) -> p k c", c=128))
                    v.tensor_copy(m8[:, 0::2, :], pb[1][:, :].rearrange("p (k c) -> p k c", c=128))
                    v.tensor_copy(m8[:, 1::2, :], pb[1][:, :].rearrange("p (k c) -> p k c", c=128))
                    v.tensor_tensor(ap_t[:], e8a[:], mus_s[:], ALU.mult)
                    v.tensor_copy(ktm[:], pvt[:, 0:4, :])
                    v.tensor_copy(vtm[:], pvt[:, 4:12, :])
                    ctx.inc(v.tensor_tensor(ap_t[:], ap_t[:], kk8[:], ALU.mult), "d1_v2", 1)

                @blk.scalar
                def _(sc):
                    ctx.wait(sc, "d1_v2")
                    sc.activation(e8a[:, 0:4, :].rearrange("p k c -> p (k c)"), mu_t[:], AF.Exp)
                    ctx.inc(sc.activation(e8a[:, 4:8, :].rearrange("p k c -> p (k c)"), var_t[:], AF.Exp), "d1_s2", 1)

                @blk.vector
                def _(v):
                    ctx.wait(v, "d1_s2")
                    inst = v.tensor_tensor(m8[:], m8[:], e8a[:], ALU.mult)
                    ctx.vsync(v, inst)
                    ctx.inc(v.tensor_tensor(m8[:], m8[:], mui_s[:], ALU.mult), "d2_v", 1)

                if dbg and seg == 0 and tc == 0:
                    dump(blk, "ap", ap_t[:], "d2_v")
                    dump(blk, "m8", m8[:], "d2_v")
                    dump(blk, "kk8", kk8[:], "d2_v")
                    dump(blk, "vtm", vtm[:], "d2_v")
                    dump(blk, "ktm", ktm[:], "d2_v")

                @blk.tensor
                def _(t):
                    ctx.wait(t, "d2_v")
                    for h in range(8):
                        mm = t.transpose(pvt[:, h, :], ap_t[:, h, :], id_s[:])
                    ctx.inc(mm, "d2_t", 1)

                @blk.vector
                def _(v):
                    ctx.wait(v, "d2_t")
                    ctx.inc(v.tensor_copy(au_t[:], pvt[:, 0:8, :]), "d2_v2", 1)

                if dbg and seg == 0 and tc == 0:
                    dump(blk, "au", au_t[:], "d2_v2")

                @blk.tensor
                def _(t, tc=tc):
                    ctx.wait(t, "d2_v2")
                    sl = slice(tc * 128, (tc + 1) * 128)
                    for h in range(8):
                        bank = pb[h // 4]
                        off = (h % 4) * 128
                        t.matmul(bank[:, off:off + 128], qkv[:, 4 + h // 2, sl], st_hi[:, h, :], start=True, stop=False)
                        mm = t.matmul(bank[:, off:off + 128], qkv[:, 4 + h // 2, sl], st_lo[:, h, :], start=False, stop=True)
                    ctx.inc(mm, "d3_t", 1)

                @blk.vector
                def _(v):
                    ctx.wait(v, "d3_t")
                    for h in range(8):
                        bank = pb[h // 4]
                        off = (h % 4) * 128
                        last = v.scalar_tensor_tensor(u_t[:, h, :], bank[:, off:off + 128],
                                                      nrhs[:, h:h + 1], vtm[:, h, :], ALU.mult, ALU.add)
                    ctx.inc(last, "d3_v", 1)

                if dbg and seg == 0 and tc == 0:
                    dump(blk, "rhs", u_t[:], "d3_v")

                for stage in range(7):
                    @blk.tensor
                    def _(t, stage=stage):
                        ctx.wait(t, "d3_v" if stage == 0 else "d4_v")
                        src_p = ap_t if stage == 0 else pp_t
                        src_u = au_t if stage == 0 else pu_t
                        for h in range(8):
                            bank = pb[h // 4]
                            off = (h % 4) * 128
                            mm = t.matmul(bank[:, off:off + 128], src_p[:, h, :], u_t[:, h, :], start=True, stop=True)
                        if stage < 6:
                            for h in range(8):
                                bank = pb[2 + h // 4]
                                off = (h % 4) * 128
                                t.matmul(bank[:, off:off + 128], src_p[:, h, :], src_u[:, h, :], start=True, stop=True)
                            for h in range(8):
                                bank = pb[4 + h // 4]
                                off = (h % 4) * 128
                                mm = t.matmul(bank[:, off:off + 128], src_u[:, h, :], src_p[:, h, :], start=True, stop=True)
                        ctx.inc(mm, "d4_t", 1)

                    @blk.vector
                    def _(v, stage=stage):
                        ctx.wait(v, "d4_t")
                        op = ALU.subtract if stage == 0 else ALU.add
                        v.tensor_tensor(u_t[:, 0:4, :], u_t[:, 0:4, :], pb[0][:, :].rearrange("p (k c) -> p k c", c=128), op)
                        last = v.tensor_tensor(u_t[:, 4:8, :], u_t[:, 4:8, :], pb[1][:, :].rearrange("p (k c) -> p k c", c=128), op)
                        if stage < 6:
                            v.tensor_copy(pu_t[:, 0:4, :], pb[2][:, :].rearrange("p (k c) -> p k c", c=128))
                            v.tensor_copy(pu_t[:, 4:8, :], pb[3][:, :].rearrange("p (k c) -> p k c", c=128))
                            v.tensor_copy(pp_t[:, 0:4, :], pb[4][:, :].rearrange("p (k c) -> p k c", c=128))
                            last = v.tensor_copy(pp_t[:, 4:8, :], pb[5][:, :].rearrange("p (k c) -> p k c", c=128))
                        ctx.inc(last, "d4_v", 1)

                if dbg and seg == 0 and tc == 0:
                    dump(blk, "uu", u_t[:], "d4_v")

                @blk.vector
                def _(v):
                    ctx.wait(v, "d4_v")
                    inst = v.memset(t8a[:], 0.0)
                    ctx.vsync(v, inst)
                    for h in range(8):
                        last = v.tensor_scalar_mul(u3_t[:, h, :], u_t[:, h, :], e_u3[:, h:h + 1])
                    ctx.inc(last, "d5_v", 1)

                if dbg and seg == 0 and tc == 0:
                    dump(blk, "u3", u3_t[:], "d5_v")

                @blk.tensor
                def _(t, tc=tc):
                    ctx.wait(t, "d5_v")
                    sl = slice(tc * 128, (tc + 1) * 128)
                    for h in range(8):
                        off = (h % 4) * 128
                        t.matmul(pb[h // 4][:, off:off + 128], m8[:, h, :], u_t[:, h, :], start=True, stop=True)
                        t.matmul(pb[2 + h // 4][:, off:off + 128], qkv[:, h // 2, sl], st_hi[:, h, :], start=True, stop=False)
                        t.matmul(pb[2 + h // 4][:, off:off + 128], qkv[:, h // 2, sl], st_lo[:, h, :], start=False, stop=True)
                        mm = t.matmul(pb[4 + h // 4][:, off:off + 128], ktm[:, h // 2, :], u3_t[:, h, :], start=True, stop=True)
                    ctx.inc(mm, "d6_t", 1)

                @blk.vector
                def _(v):
                    ctx.wait(v, "d6_t")
                    for h in range(8):
                        off = (h % 4) * 128
                        v.tensor_copy(ogt[:, h, :], pb[h // 4][:, off:off + 128])
                    for h in range(8):
                        off = (h % 4) * 128
                        last = v.scalar_tensor_tensor(ogt[:, h, :], pb[2 + h // 4][:, off:off + 128],
                                                      erowM[:, h:h + 1], ogt[:, h, :], ALU.mult, ALU.add)
                    ctx.inc(last, "d6_v", 1)

                @blk.scalar
                def _(sc):
                    ctx.wait(sc, "d6_v")
                    for h in range(8):
                        last = sc.activation(u3_t[:, h, :], ogt[:, h, :], AF.Square,
                                             accum_out=osum[:, h:h + 1])
                    ctx.inc(last, "d7_s", 1)

                if dbg and seg == 0 and tc == 0:
                    dump(blk, "osum", osum[:], "d7_s")

                @blk.vector
                def _(v):
                    ctx.wait(v, "d7_s")
                    ctx.inc(v.tensor_scalar(t8b[:], osum[:], 1.0 / 128.0, 1e-6, ALU.mult, ALU.add), "d7_v", 1)

                @blk.scalar
                def _(sc):
                    ctx.wait(sc, "d7_v")
                    sc.activation(t8b[:], t8b[:], AF.Ln)
                    ctx.inc(sc.activation(osc[:], t8b[:], AF.Exp, scale=-0.5), "d7_s3", 1)

                @blk.vector
                def _(v, tc=tc):
                    ctx.wait(v, "d7_s3")
                    for h in range(8):
                        v.tensor_scalar_mul(ogt[:, h, :], ogt[:, h, :], osc[:, h:h + 1])
                    for h in range(8):
                        bank = pb[4 + h // 4]
                        off = (h % 4) * 128
                        v.scalar_tensor_tensor(st_f[:, h, :], st_f[:, h, :], gamC[:, h:h + 1],
                                               bank[:, off:off + 128], ALU.mult, ALU.add)
                    ctx.inc(v.tensor_tensor(og1[:], ogt[:].rearrange("p k c -> p (k c)"), zt[:, tc, :], ALU.mult), "d8_v", 1)

                if dbg and seg == 0 and tc == 0:
                    dump(blk, "osc", osc[:], "d7_s3")
                    dump(blk, "ogt", ogt[:], "d8_v")
                    dump(blk, "og1", og1[:], "d8_v")
                    dump(blk, "stf", st_f[:], "d8_v")

                @blk.scalar
                def _(sc):
                    ctx.wait(sc, "d8_v")
                    ctx.inc(sc.copy(st_hi[:], st_f[:]), "d8_s", 1)

                @blk.vector
                def _(v):
                    ctx.wait(v, "d8_s")
                    if dbg:
                        ctx.wait(v, "dbg_st")
                    ctx.inc(v.tensor_tensor(st_lo[:], st_f[:], st_hi[:], ALU.subtract), "og_v", 1)

                @blk.sync
                def _(s, t0=t0, tc=tc):
                    ctx.wait(s, "og_v")
                    ctx.inc(s.dma_start(out=og_out[t0 + tc * 128:t0 + (tc + 1) * 128, :],
                                        in_=og1[:]), "og_st", 16)

        @blk.gpsimd
        def _(g):
            ctx.wait(g, "og_st")

    return nc



def build_wsum():
    RW1 = INTER // 8
    RW3 = D // 8
    NCOL = 6144
    nc = bass.Bass("TRN2", target_bir_lowering=False)
    w1 = nc.dram_tensor("w1", [RW1, D], F32, kind="ExternalInput").ap()
    w2 = nc.dram_tensor("w2", [RW1, D], F32, kind="ExternalInput").ap()
    w3 = nc.dram_tensor("w3", [RW3, INTER], F32, kind="ExternalInput").ap()
    allones = nc.dram_tensor("allones", [128, 128], F32, kind="ExternalInput").ap()
    out = nc.dram_tensor("wsums", [1, 3], F32, kind="ExternalOutput").ap()
    es = contextlib.ExitStack()
    with es:
        ctx = Ctx(nc, es)
        buf = [ctx.sb(f"buf{i}", [128, NCOL], F32) for i in range(2)]
        ao = ctx.sb("ao", [128, 128], F32)
        acc = ctx.sb("acc", [128, 3], F32)
        red = ctx.sb("red", [128, 2], F32)
        res = ctx.sb("res", [1, 3], F32)
        ps = ctx.ps("ps", [128, 4])
        blk = es.enter_context(nc.Block())

        @blk.gpsimd
        def _(g):
            ctx.inc(g.dma_start(out=ao[:], in_=allones), "ld0", 16)

        @blk.vector
        def _(v):
            ctx.wait(v, "ld0")
            ctx.inc(v.memset(acc[:], 0.0), "acc_v", 1)

        plans = [(w1, RW1 * D // 128, 0), (w2, RW1 * D // 128, 1), (w3, RW3 * INTER // 128, 2)]
        ib = 0
        for wt, total, idx2 in plans:
            nblk = (total + NCOL - 1) // NCOL
            for b in range(nblk):
                cols = min(NCOL, total - b * NCOL)
                bb = ib % 2

                @blk.sync
                def _(s, wt=wt, b=b, cols=cols, bb=bb, ib=ib):
                    if ib >= 2:
                        ctx.wait(s, "acc_v")
                    ctx.inc(s.dma_start(
                        out=buf[bb][:, 0:cols],
                        in_=wt.rearrange("a b -> (a b)").rearrange("(p x) -> p x", p=128)[:, b * NCOL:b * NCOL + cols]),
                        "w_ld", 16)

                @blk.vector
                def _(v, cols=cols, bb=bb, idx2=idx2):
                    ctx.wait(v, "w_ld")
                    inst = v.tensor_reduce(red[:, bb:bb + 1], buf[bb][:, 0:cols], AX.X, ALU.add,
                                           apply_absolute_value=True)
                    ctx.vsync(v, inst)
                    ctx.inc(v.tensor_tensor(acc[:, idx2:idx2 + 1], acc[:, idx2:idx2 + 1],
                                            red[:, bb:bb + 1], ALU.add), "acc_v", 1)
                ib += 1

        @blk.tensor
        def _(t):
            ctx.wait(t, "acc_v")
            ctx.inc(t.matmul(ps[0:1, 0:3], ao[:, 0:1], acc[:], start=True, stop=True), "fin_t", 1)

        @blk.vector
        def _(v):
            ctx.wait(v, "fin_t")
            ctx.inc(v.tensor_copy(res[:], ps[0:1, 0:3]), "fin_v", 1)

        @blk.sync
        def _(s):
            ctx.wait(s, "fin_v")
            ctx.inc(s.dma_start(out=out, in_=res[:]), "fin_st", 16)

        @blk.gpsimd
        def _(g):
            ctx.wait(g, "fin_st")
    return nc


def build_wtern():
    RW1 = INTER // 8
    RW3 = D // 8
    NCOL = 6144
    NELEM = float(INTER * D)
    nc = bass.Bass("TRN2", target_bir_lowering=False)
    w1 = nc.dram_tensor("w1", [RW1, D], F32, kind="ExternalInput").ap()
    w2 = nc.dram_tensor("w2", [RW1, D], F32, kind="ExternalInput").ap()
    w3 = nc.dram_tensor("w3", [RW3, INTER], F32, kind="ExternalInput").ap()
    wsums = nc.dram_tensor("wsums", [8, 3], F32, kind="ExternalInput").ap()
    allones = nc.dram_tensor("allones", [128, 128], F32, kind="ExternalInput").ap()
    t1o = nc.dram_tensor("t1", [RW1, D], BF16, kind="ExternalOutput").ap()
    t2o = nc.dram_tensor("t2", [RW1, D], BF16, kind="ExternalOutput").ap()
    t3o = nc.dram_tensor("t3", [RW3, INTER], BF16, kind="ExternalOutput").ap()
    es = contextlib.ExitStack()
    with es:
        ctx = Ctx(nc, es)
        buf = [ctx.sb(f"buf{i}", [128, NCOL], F32) for i in range(2)]
        tb = [ctx.sb(f"tb{i}", [128, NCOL], BF16) for i in range(2)]
        g1 = ctx.sb("g1", [128, NCOL], F32)
        g2 = ctx.sb("g2", [128, NCOL], F32)
        ao = ctx.sb("ao", [128, 128], F32)
        ws = ctx.sb("ws", [8, 3], F32)
        thr = ctx.sb("thr", [1, 8], F32)
        thrb = ctx.sb("thrb", [128, 8], F32)
        ps = ctx.ps("ps", [128, 16])
        blk = es.enter_context(nc.Block())

        @blk.gpsimd
        def _(g):
            ctx.inc(g.dma_start(out=ao[:], in_=allones), "ld0", 16)
            ctx.inc(g.dma_start(out=ws[:], in_=wsums), "ld0", 16)

        @blk.tensor
        def _(t):
            ctx.wait(t, "ld0")
            ctx.inc(t.matmul(ps[0:1, 0:3], ao[0:8, 0:1], ws[:], start=True, stop=True), "t0", 1)

        @blk.vector
        def _(v):
            ctx.wait(v, "t0")
            v.tensor_scalar(thr[0:1, 0:3], ps[0:1, 0:3], 1.0 / NELEM, 1e-5, ALU.mult, ALU.max)
            inst = v.memset(thr[0:1, 3:8], 0.0)
            ctx.vsync(v, inst)
            v.tensor_scalar_mul(thr[0:1, 0:3], thr[0:1, 0:3], 0.5)
            inst = v.memset(g1[:, 0:1], 0.0)
            ctx.vsync(v, inst)
            ctx.inc(v.tensor_scalar_mul(thr[0:1, 3:6], thr[0:1, 0:3], -1.0), "t1v", 1)

        @blk.tensor
        def _(t):
            ctx.wait(t, "t1v")
            ctx.inc(t.matmul(ps[:, 8:16], ao[0:1, :], thr[0:1, 0:8], start=True, stop=True), "t2t", 1)

        @blk.vector
        def _(v):
            ctx.wait(v, "t2t")
            ctx.inc(v.tensor_copy(thrb[:], ps[:, 8:16]), "t3v", 1)

        plans = [(w1, t1o, RW1 * D // 128, 0), (w2, t2o, RW1 * D // 128, 1),
                 (w3, t3o, RW3 * INTER // 128, 2)]
        ib = 0
        for wt, wo, total, idx2 in plans:
            nblk = (total + NCOL - 1) // NCOL
            for b in range(nblk):
                cols = min(NCOL, total - b * NCOL)
                bb = ib % 2

                @blk.sync
                def _(s, wt=wt, b=b, cols=cols, bb=bb, ib=ib):
                    ctx.wait(s, "t3v")
                    if ib >= 2:
                        ctx.wait(s, "st_d")
                    ctx.inc(s.dma_start(
                        out=buf[bb][:, 0:cols],
                        in_=wt.rearrange("a b -> (a b)").rearrange("(p x) -> p x", p=128)[:, b * NCOL:b * NCOL + cols]),
                        "w_ld", 16)

                @blk.vector
                def _(v, cols=cols, bb=bb, idx2=idx2):
                    ctx.wait(v, "w_ld")
                    v.tensor_single_scalar(g1[:, 0:cols], buf[bb][:, 0:cols], thrb[:, idx2:idx2 + 1], ALU.is_gt)
                    inst = v.tensor_single_scalar(g2[:, 0:cols], buf[bb][:, 0:cols], thrb[:, 3 + idx2:4 + idx2], ALU.is_lt)
                    ctx.vsync(v, inst)
                    ctx.inc(v.tensor_tensor(tb[bb][:, 0:cols], g2[:, 0:cols], g1[:, 0:cols], ALU.subtract), "tern_v", 1)

                @blk.sync
                def _(s, wo=wo, b=b, cols=cols, bb=bb):
                    ctx.wait(s, "tern_v")
                    ctx.inc(s.dma_start(
                        out=wo.rearrange("a b -> (a b)").rearrange("(p x) -> p x", p=128)[:, b * NCOL:b * NCOL + cols],
                        in_=tb[bb][:, 0:cols]), "st_d", 16)
                ib += 1

        @blk.gpsimd
        def _(g):
            ctx.wait(g, "st_d")
    return nc



def build_l2():
    TOK = 1024
    ST = 512
    NST = TOK // ST
    nc = bass.Bass("TRN2", target_bir_lowering=False)
    og = nc.dram_tensor("og", [VAL_DIM, TOK], BF16, kind="ExternalInput").ap()
    x0 = nc.dram_tensor("x0", [D, TOK], F32, kind="ExternalInput").ap()
    woT = nc.dram_tensor("woT", [VAL_DIM, D], F32, kind="ExternalInput").ap()
    gwT = nc.dram_tensor("gwT", [D, INTER], F32, kind="ExternalInput").ap()
    uwT = nc.dram_tensor("uwT", [D, INTER], F32, kind="ExternalInput").ap()
    dwT = nc.dram_tensor("dwT", [INTER, D], F32, kind="ExternalInput").ap()
    gsc = nc.dram_tensor("gsc", [128, 44], F32, kind="ExternalInput").ap()
    usc = nc.dram_tensor("usc", [128, 44], F32, kind="ExternalInput").ap()
    dsc = nc.dram_tensor("dsc", [128, 16], F32, kind="ExternalInput").ap()
    allones = nc.dram_tensor("allones", [128, 128], F32, kind="ExternalInput").ap()
    x2_o = nc.dram_tensor("x2", [D, TOK], F32, kind="ExternalOutput").ap()
    ln2_o = nc.dram_tensor("ln2", [D, TOK], BF16, kind="ExternalOutput").ap()

    es = contextlib.ExitStack()
    with es:
        ctx = Ctx(nc, es)
        og_s = ctx.sb("og_s", [128, 32, ST], BF16)
        x_s = ctx.sb("x_s", [128, 16, ST], F32)
        ln_s = ctx.sb("ln_s", [128, 16, ST], BF16)
        h_s = ctx.sb("h_s", [128, 44, ST], BF16)
        gt_s = ctx.sb("gt_s", [128, ST], BF16)
        ut_s = ctx.sb("ut_s", [128, ST], BF16)
        wo_b = [ctx.sb(f"wo_b{i}", [128, 32, 128], BF16) for i in range(2)]
        wg_b = [ctx.sb(f"wg_b{i}", [128, 16, 128], BF16) for i in range(2)]
        wu_b = [ctx.sb(f"wu_b{i}", [128, 16, 128], BF16) for i in range(2)]
        wd_b = [ctx.sb(f"wd_b{i}", [128, 44, 128], BF16) for i in range(2)]
        gsc_s = ctx.sb("gsc_s", [128, 44], F32)
        usc_s = ctx.sb("usc_s", [128, 44], F32)
        dsc_s = ctx.sb("dsc_s", [128, 16], F32)
        aof = ctx.sb("aof", [128, 128], F32)
        sqf = ctx.sb("sqf", [128, 4, ST], F32)
        mu_t = ctx.sb("mu_t", [128, ST], F32)
        var_t = ctx.sb("var_t", [128, ST], F32)
        pb = [ctx.ps(f"pb{i}", [128, 512]) for i in range(6)]
        blk = es.enter_context(nc.Block())

        @blk.gpsimd
        def _(g):
            for dst, src_ in [(gsc_s, gsc), (usc_s, usc), (dsc_s, dsc), (aof, allones)]:
                ctx.inc(g.dma_start(out=dst[:], in_=src_), "c_ld", 16)

        def emit_ln(tag, dst_bf):
            for grp in range(4):
                @blk.scalar
                def _(sc, grp=grp, tag=tag):
                    ctx.wait(sc, f"{tag}_pre")
                    if grp > 0:
                        ctx.wait(sc, f"{tag}_t")
                    for i in range(4):
                        last = sc.activation(sqf[:, i, :], x_s[:, grp * 4 + i, :], AF.Square)
                    ctx.inc(last, f"{tag}_s", 1)

                @blk.tensor
                def _(t, grp=grp, tag=tag):
                    ctx.wait(t, f"{tag}_s")
                    for i in range(4):
                        k = grp * 4 + i
                        t.matmul(pb[4][:, :], aof[:], x_s[:, k, :], start=(k == 0), stop=(k == 15))
                        mm = t.matmul(pb[5][:, :], aof[:], sqf[:, i, :], start=(k == 0), stop=(k == 15))
                    ctx.inc(mm, f"{tag}_t", 1)

            @blk.vector
            def _(v, tag=tag):
                ctx.wait(v, f"{tag}_t")
                v.tensor_scalar_mul(mu_t[:], pb[4][:, :], 1.0 / D)
                v.tensor_scalar(var_t[:], pb[5][:, :], 1.0 / D, 1e-5, ALU.mult, ALU.add)
                inst = v.tensor_tensor(pb[4][:, :], mu_t[:], mu_t[:], ALU.mult)
                ctx.vsync(v, inst)
                ctx.inc(v.tensor_tensor(var_t[:], var_t[:], pb[4][:, :], ALU.subtract), f"{tag}_v", 1)

            @blk.scalar
            def _(sc, tag=tag):
                ctx.wait(sc, f"{tag}_v")
                sc.activation(var_t[:], var_t[:], AF.Ln)
                ctx.inc(sc.activation(var_t[:], var_t[:], AF.Exp, scale=-0.5), f"{tag}_s2", 1)

            @blk.vector
            def _(v, tag=tag, dst_bf=dst_bf):
                ctx.wait(v, f"{tag}_s2")
                for k in range(16):
                    v.tensor_tensor(ln_s[:, k, :] if dst_bf is None else dst_bf[:, k, :],
                                    x_s[:, k, :], mu_t[:], ALU.subtract)
                dd = ln_s if dst_bf is None else dst_bf
                for k in range(16):
                    last = v.tensor_tensor(dd[:, k, :], dd[:, k, :], var_t[:], ALU.mult)
                ctx.inc(last, f"{tag}_v2", 1)

        for st in range(NST):
            t0 = st * ST

            @blk.gpsimd
            def _(g, t0=t0, st=st):
                ctx.wait(g, "c_ld")
                if st > 0:
                    ctx.wait(g, "out_st")
                ctx.inc(g.dma_start(out=og_s[:], in_=og.rearrange("(c p) s -> p c s", p=128)[:, :, t0:t0 + ST]), "i_ld", 16)
                ctx.inc(g.dma_start(out=x_s[:], in_=x0.rearrange("(c p) s -> p c s", p=128)[:, :, t0:t0 + ST]), "i_ld", 16)

            for m in range(16):
                @blk.gpsimd
                def _(g, m=m):
                    ctx.wait(g, "i_ld")
                    if m >= 2:
                        ctx.wait(g, "wo_t")
                    ctx.inc(g.dma_start(out=wo_b[m % 2][:], in_=woT.rearrange("(c p) q -> p c q", p=128)[:, :, m * 128:(m + 1) * 128]), "wo_ld", 16)

                @blk.tensor
                def _(t, m=m):
                    ctx.wait(t, "wo_ld")
                    if m >= 2:
                        ctx.wait(t, "wo_v")
                    for k in range(32):
                        mm = t.matmul(pb[m % 2][:, :], wo_b[m % 2][:, k, :], og_s[:, k, :],
                                      start=(k == 0), stop=(k == 31))
                    ctx.inc(mm, "wo_t", 1)

                @blk.vector
                def _(v, m=m):
                    ctx.wait(v, "wo_t")
                    ctx.inc(v.tensor_tensor(x_s[:, m, :], x_s[:, m, :], pb[m % 2][:, :], ALU.add), "wo_v", 1)

            @blk.vector
            def _(v):
                ctx.wait(v, "wo_v")
                ctx.inc(v.memset(sqf[:, 0, 0:1], 0.0), "ln1_pre", 1)
            emit_ln("ln1", None)

            for m in range(44):
                @blk.gpsimd
                def _(g, m=m):
                    ctx.wait(g, "ln1_v2")
                    if m >= 2:
                        ctx.wait(g, "gu_t")
                    ctx.inc(g.dma_start(out=wg_b[m % 2][:], in_=gwT.rearrange("(c p) q -> p c q", p=128)[:, :, m * 128:(m + 1) * 128]), "gu_ld", 16)
                    ctx.inc(g.dma_start(out=wu_b[m % 2][:], in_=uwT.rearrange("(c p) q -> p c q", p=128)[:, :, m * 128:(m + 1) * 128]), "gu_ld", 16)

                @blk.tensor
                def _(t, m=m):
                    ctx.wait(t, "gu_ld")
                    if m >= 1:
                        ctx.wait(t, "h_v")
                    for k in range(16):
                        t.matmul(pb[2][:, :], wg_b[m % 2][:, k, :], ln_s[:, k, :], start=(k == 0), stop=(k == 15))
                    for k in range(16):
                        mm = t.matmul(pb[3][:, :], wu_b[m % 2][:, k, :], ln_s[:, k, :], start=(k == 0), stop=(k == 15))
                    ctx.inc(mm, "gu_t", 1)

                @blk.scalar
                def _(sc, m=m):
                    ctx.wait(sc, "gu_t")
                    sc.activation(gt_s[:], pb[2][:, :], AF.Silu, scale=gsc_s[:, m:m + 1])
                    ctx.inc(sc.activation(ut_s[:], pb[3][:, :], AF.Copy, scale=usc_s[:, m:m + 1]), "gu_s", 1)

                @blk.vector
                def _(v, m=m):
                    ctx.wait(v, "gu_s")
                    ctx.inc(v.tensor_tensor(h_s[:, m, :], gt_s[:], ut_s[:], ALU.mult), "h_v", 1)

            for m in range(16):
                @blk.gpsimd
                def _(g, m=m):
                    ctx.wait(g, "h_v")
                    if m >= 2:
                        ctx.wait(g, "dn_t")
                    ctx.inc(g.dma_start(out=wd_b[m % 2][:], in_=dwT.rearrange("(c p) q -> p c q", p=128)[:, :, m * 128:(m + 1) * 128]), "dn_ld", 16)

                @blk.tensor
                def _(t, m=m):
                    ctx.wait(t, "dn_ld")
                    if m >= 2:
                        ctx.wait(t, "dn_v")
                    for k in range(44):
                        mm = t.matmul(pb[m % 2][:, :], wd_b[m % 2][:, k, :], h_s[:, k, :],
                                      start=(k == 0), stop=(k == 43))
                    ctx.inc(mm, "dn_t", 1)

                @blk.vector
                def _(v, m=m):
                    ctx.wait(v, "dn_t")
                    ctx.inc(v.scalar_tensor_tensor(x_s[:, m, :], pb[m % 2][:, :], dsc_s[:, m:m + 1],
                                                   x_s[:, m, :], ALU.mult, ALU.add), "dn_v", 1)

            @blk.vector
            def _(v):
                ctx.wait(v, "dn_v")
                ctx.inc(v.memset(sqf[:, 0, 0:1], 0.0), "ln2_pre", 1)
            emit_ln("ln2", ln_s)

            @blk.sync
            def _(s, t0=t0):
                ctx.wait(s, "ln2_v2")
                ctx.inc(s.dma_start(out=x2_o.rearrange("(c p) s -> p c s", p=128)[:, :, t0:t0 + ST], in_=x_s[:]), "out_st", 16)
                ctx.inc(s.dma_start(out=ln2_o.rearrange("(c p) s -> p c s", p=128)[:, :, t0:t0 + ST], in_=ln_s[:]), "out_st", 16)

        @blk.gpsimd
        def _(g):
            ctx.wait(g, "out_st")
    return nc



def build_l3():
    HC = 512
    NKT = S // 128
    NQB = S // 512
    LN8 = math.log(8.0)
    nc = bass.Bass("TRN2", target_bir_lowering=False)
    ln2 = nc.dram_tensor("ln2", [D, S], BF16, kind="ExternalInput").ap()
    wq_c = nc.dram_tensor("wq_c", [D, HC], F32, kind="ExternalInput").ap()
    wk_c = nc.dram_tensor("wk_c", [D, HC], F32, kind="ExternalInput").ap()
    wv_c = nc.dram_tensor("wv_c", [D, HC], F32, kind="ExternalInput").ap()
    lq1 = nc.dram_tensor("lq1", [1, DH1], F32, kind="ExternalInput").ap()
    lk1 = nc.dram_tensor("lk1", [1, DH1], F32, kind="ExternalInput").ap()
    lq2 = nc.dram_tensor("lq2", [1, DH1], F32, kind="ExternalInput").ap()
    lk2 = nc.dram_tensor("lk2", [1, DH1], F32, kind="ExternalInput").ap()
    gnw = nc.dram_tensor("gnw", [128, 4, 128], F32, kind="ExternalInput").ap()
    gnb = nc.dram_tensor("gnb", [128, 4, 128], F32, kind="ExternalInput").ap()
    allones = nc.dram_tensor("allones", [128, 128], F32, kind="ExternalInput").ap()
    ident = nc.dram_tensor("ident", [128, 128], F32, kind="ExternalInput").ap()
    att_o = nc.dram_tensor("att", [S, HC], BF16, kind="ExternalOutput").ap()

    es = contextlib.ExitStack()
    with es:
        ctx = Ctx(nc, es)
        wq_s = ctx.sb("wq_s", [128, 16, HC], BF16)
        wk_s = ctx.sb("wk_s", [128, 16, HC], BF16)
        wv_s = ctx.sb("wv_s", [128, 16, HC], BF16)
        lt = ctx.sb("lt", [1, 4 * DH1], F32)
        aof = ctx.sb("aof", [128, 128], F32)
        idf_s = ctx.sb("idf_s", [128, 128], F32)
        gnw_s = ctx.sb("gnw_s", [128, 4, 128], F32)
        gnb_s = ctx.sb("gnb_s", [128, 4, 128], F32)
        xt = ctx.sb("xt", [128, 16, 512], BF16)
        qf = ctx.sb("qf", [128, 4, S], BF16)
        kf = ctx.sb("kf", [128, 4, S], BF16)
        vt = ctx.sb("vt", [128, 4, 32, 132], BF16)
        nrm = ctx.sb("nrm", [1, 64], F32)
        negc = ctx.sb("negc", [128, 4], F32)
        lamv = ctx.sb("lamv", [128, 1], F32)
        p_sb = ctx.sb("p_sb", [128, 2, 512], BF16)
        att1 = ctx.sb("att1", [128, 4, 128], F32)
        att2 = ctx.sb("att2", [128, 4, 128], F32)
        attc = ctx.sb("attc", [128, 4, 128], BF16)
        stat = ctx.sb("stat", [128, 16], F32)
        st2 = ctx.sb("st2", [128, 16], F32)
        sdump = ctx.sb("sdump", [128, 512], BF16)
        ps_s = [ctx.ps(f"ps_s{i}", [128, 512]) for i in range(2)]
        ps_o = [ctx.ps(f"ps_o{i}", [128, 2, 132]) for i in range(4)]
        ps_m = ctx.ps("ps_m", [128, 512])
        blk = es.enter_context(nc.Block())

        @blk.gpsimd
        def _(g):
            for dst, src_ in [(aof, allones), (idf_s, ident), (gnw_s, gnw), (gnb_s, gnb)]:
                ctx.inc(g.dma_start(out=dst[:], in_=src_), "c_ld", 16)
            for i, src_ in enumerate([lq1, lk1, lq2, lk2]):
                ctx.inc(g.dma_start(out=lt[0:1, i * DH1:(i + 1) * DH1], in_=src_), "c_ld", 16)
            ctx.inc(g.dma_start(out=wq_s[:], in_=wq_c.rearrange("(c p) q -> p c q", p=128)), "c_ld", 16)
            ctx.inc(g.dma_start(out=wk_s[:], in_=wk_c.rearrange("(c p) q -> p c q", p=128)), "c_ld", 16)
            ctx.inc(g.dma_start(out=wv_s[:], in_=wv_c.rearrange("(c p) q -> p c q", p=128)), "c_ld", 16)

        @blk.vector
        def _(v):
            ctx.wait(v, "c_ld")
            v.tensor_tensor(lt[0:1, 0:DH1], lt[0:1, 0:DH1], lt[0:1, DH1:2 * DH1], ALU.mult)
            inst = v.tensor_tensor(lt[0:1, 2 * DH1:3 * DH1], lt[0:1, 2 * DH1:3 * DH1], lt[0:1, 3 * DH1:4 * DH1], ALU.mult)
            ctx.vsync(v, inst)
            v.tensor_reduce(nrm[0:1, 0:1], lt[0:1, 0:DH1], AX.X, ALU.add)
            inst = v.tensor_reduce(nrm[0:1, 1:2], lt[0:1, 2 * DH1:3 * DH1], AX.X, ALU.add)
            ctx.vsync(v, inst)
            v.tensor_scalar_mul(gnw_s[:], gnw_s[:], 1.0 - LAMBDA_INIT)
            ctx.inc(v.tensor_scalar_mul(gnb_s[:], gnb_s[:], 1.0 - LAMBDA_INIT), "l_v", 1)

        @blk.scalar
        def _(sc):
            ctx.wait(sc, "l_v")
            ctx.inc(sc.activation(nrm[0:1, 0:2], nrm[0:1, 0:2], AF.Exp), "l_s", 1)

        @blk.vector
        def _(v):
            ctx.wait(v, "l_s")
            inst = v.tensor_tensor(nrm[0:1, 2:3], nrm[0:1, 0:1], nrm[0:1, 1:2], ALU.subtract)
            ctx.vsync(v, inst)
            inst = v.tensor_scalar(nrm[0:1, 3:4], nrm[0:1, 2:3], LAMBDA_INIT, -1.0, ALU.add, ALU.mult)
            ctx.vsync(v, inst)
            ctx.inc(v.memset(vt[:, :, :, 128:132], 0.0), "l_v2", 1)

        @blk.tensor
        def _(t):
            ctx.wait(t, "l_v2")
            ctx.inc(t.matmul(ps_m[:, 0:1], aof[0:1, :], nrm[0:1, 3:4], start=True, stop=True), "l_t", 1)

        @blk.vector
        def _(v):
            ctx.wait(v, "l_t")
            v.tensor_copy(lamv[:], ps_m[:, 0:1])
            ctx.inc(v.memset(vt[:, :, :, 128:129], 1.0), "l_v3", 1)

        for n in range(NQB):
            @blk.gpsimd
            def _(g, n=n):
                ctx.wait(g, "l_v3")
                if n > 0:
                    ctx.wait(g, "pv_t")
                ctx.inc(g.dma_start(out=xt[:], in_=ln2.rearrange("(c p) s -> p c s", p=128)[:, :, n * 512:(n + 1) * 512]), "x_ld", 16)

            outs = [("q", hh) for hh in range(4)] + [("k", hh) for hh in range(4)]
            for j, (kind, hh) in enumerate(outs):
                @blk.tensor
                def _(t, j=j, kind=kind, hh=hh):
                    ctx.wait(t, "x_ld")
                    if ctx.semv.get("pj_v", 0) >= 2:
                        ctx.wait(t, "pj_v")
                    w = wq_s if kind == "q" else wk_s
                    for k in range(16):
                        mm = t.matmul(ps_s[j % 2][:, :], w[:, k, hh * 128:(hh + 1) * 128], xt[:, k, :],
                                      start=(k == 0), stop=(k == 15))
                    ctx.inc(mm, "pj_t", 1)

                @blk.vector
                def _(v, j=j, kind=kind, hh=hh, n=n):
                    ctx.wait(v, "pj_t")
                    dst = qf if kind == "q" else kf
                    ctx.inc(v.tensor_copy(dst[:, hh, n * 512:(n + 1) * 512], ps_s[j % 2][:, :]), "pj_v", 1)

            for ts in range(4):
                @blk.tensor
                def _(t, ts=ts):
                    ctx.wait(t, "pj_v")
                    if ts >= 1:
                        ctx.wait(t, "pv_v")
                    for k in range(16):
                        mm = t.matmul(ps_m[:, :], xt[:, k, ts * 128:(ts + 1) * 128], wv_s[:, k, :],
                                      start=(k == 0), stop=(k == 15))
                    ctx.inc(mm, "pv_t", 1)

                @blk.vector
                def _(v, ts=ts, n=n):
                    ctx.wait(v, "pv_t")
                    for hh in range(4):
                        last = v.tensor_copy(vt[:, hh, n * 4 + ts, 0:128], ps_m[:, hh * 128:(hh + 1) * 128])
                    ctx.inc(last, "pv_v", 1)

        @blk.vector
        def _(v):
            ctx.wait(v, "pv_v")
            for hh in range(4):
                v.tensor_reduce(stat[:, hh:hh + 1], qf[:, hh, :], AX.X, ALU.max, apply_absolute_value=True)
            for hh in range(4):
                last = v.tensor_reduce(stat[:, 4 + hh:5 + hh], kf[:, hh, :], AX.X, ALU.max, apply_absolute_value=True)
            ctx.inc(last, "m_v", 1)

        @blk.tensor
        def _(t):
            ctx.wait(t, "m_v")
            for c in range(8):
                bank = ps_s[c // 4]
                mm = t.transpose(bank[0:1, (c % 4) * 128:(c % 4 + 1) * 128], stat[:, c:c + 1], idf_s[:])
            ctx.inc(mm, "m_t", 1)

        @blk.vector
        def _(v):
            ctx.wait(v, "m_t")
            for c in range(8):
                bank = ps_s[c // 4]
                last = v.tensor_reduce(nrm[0:1, 8 + c:9 + c], bank[0:1, (c % 4) * 128:(c % 4 + 1) * 128], AX.X, ALU.max)
            ctx.inc(last, "m_v2", 1)

        @blk.scalar
        def _(sc):
            ctx.wait(sc, "m_v2")
            ctx.inc(sc.activation(nrm[0:1, 8:16], nrm[0:1, 8:16], AF.Ln), "m_s", 1)

        @blk.vector
        def _(v):
            ctx.wait(v, "m_s")
            inst = v.tensor_tensor(nrm[0:1, 16:20], nrm[0:1, 8:12], nrm[0:1, 12:16], ALU.add)
            ctx.vsync(v, inst)
            ctx.inc(v.tensor_scalar(nrm[0:1, 20:24], nrm[0:1, 16:20], LN8 + math.log(64.0), -1.0, ALU.add, ALU.mult), "m_v3", 1)

        @blk.tensor
        def _(t):
            ctx.wait(t, "m_v3")
            ctx.inc(t.matmul(ps_m[:, 0:4], aof[0:1, :], nrm[0:1, 20:24], start=True, stop=True), "m_t2", 1)

        @blk.vector
        def _(v):
            ctx.wait(v, "m_v4" if False else "m_t2")
            ctx.inc(v.tensor_copy(negc[:], ps_m[:, 0:4]), "m_v4", 1)

        for hh in range(4):
            for qb in range(NQB):
                for kt in range(NKT):
                    @blk.tensor
                    def _(t, hh=hh, qb=qb, kt=kt):
                        if hh == 0 and qb == 0 and kt == 0:
                            ctx.wait(t, "m_v4")
                        else:
                            ctx.wait(t, "a_s")
                            if kt == 0:
                                ctx.wait(t, "o_v")
                        ksl = slice(kt * 128, (kt + 1) * 128)
                        qsl = slice(qb * 512, (qb + 1) * 512)
                        t.matmul(ps_s[0][:, :], kf[0:64, hh, ksl], qf[0:64, hh, qsl],
                                 start=True, stop=True, tile_position=(0, 0))
                        ctx.inc(t.matmul(ps_s[1][:, :], kf[64:128, hh, ksl], qf[64:128, hh, qsl],
                                         start=True, stop=True, tile_position=(64, 0)), "a_t", 1)

                    @blk.scalar
                    def _(sc, hh=hh):
                        ctx.wait(sc, "a_t")
                        sc.activation(p_sb[:, 0, :], ps_s[0][:, :], AF.Exp, scale=0.125, bias=negc[:, hh:hh + 1])
                        ctx.inc(sc.activation(p_sb[:, 1, :], ps_s[1][:, :], AF.Exp, scale=0.125, bias=negc[:, hh:hh + 1]), "a_s", 1)

                    @blk.tensor
                    def _(t, hh=hh, kt=kt):
                        ctx.wait(t, "a_s")
                        for sl in range(4):
                            for a in range(2):
                                mm = t.matmul(ps_o[sl][:, a, 0:129], p_sb[:, a, sl * 128:(sl + 1) * 128],
                                              vt[:, hh, kt, 0:129], start=(kt == 0), stop=(kt == NKT - 1))
                        ctx.inc(mm, "a_t2", 1)

                @blk.vector
                def _(v):
                    ctx.wait(v, "a_t2")
                    for sl in range(4):
                        v.reciprocal(stat[:, sl:sl + 1], ps_o[sl][:, 0, 128:129])
                        v.reciprocal(stat[:, 4 + sl:5 + sl], ps_o[sl][:, 1, 128:129])
                    for sl in range(4):
                        v.tensor_scalar_mul(att1[:, sl, :], ps_o[sl][:, 0, 0:128], stat[:, sl:sl + 1])
                        v.tensor_scalar_mul(att2[:, sl, :], ps_o[sl][:, 1, 0:128], stat[:, 4 + sl:5 + sl])
                    for sl in range(4):
                        last = v.scalar_tensor_tensor(attc[:, sl, :], att2[:, sl, :], lamv[:, 0:1],
                                                      att1[:, sl, :], ALU.mult, ALU.add)
                    ctx.inc(last, "g_v", 1)

                @blk.vector
                def _(v):
                    ctx.wait(v, "g_v")
                    for sl in range(4):
                        last = v.tensor_reduce(stat[:, 8 + sl:9 + sl], attc[:, sl, :], AX.X, ALU.add)
                    ctx.inc(last, "g_v2", 1)

                @blk.scalar
                def _(sc):
                    ctx.wait(sc, "g_v2")
                    for sl in range(4):
                        last = sc.activation(sdump[:, sl * 128:(sl + 1) * 128], attc[:, sl, :], AF.Square,
                                             accum_out=stat[:, 12 + sl:13 + sl])
                    ctx.inc(last, "g_s", 1)

                @blk.vector
                def _(v):
                    ctx.wait(v, "g_s")
                    v.tensor_scalar_mul(st2[:, 0:4], stat[:, 8:12], 1.0 / 128.0)
                    inst = v.tensor_scalar(st2[:, 4:8], stat[:, 12:16], 1.0 / 128.0, 1e-5, ALU.mult, ALU.add)
                    ctx.vsync(v, inst)
                    inst = v.tensor_tensor(st2[:, 8:12], st2[:, 0:4], st2[:, 0:4], ALU.mult)
                    ctx.vsync(v, inst)
                    ctx.inc(v.tensor_tensor(st2[:, 4:8], st2[:, 4:8], st2[:, 8:12], ALU.subtract), "g_v3", 1)

                @blk.scalar
                def _(sc):
                    ctx.wait(sc, "g_v3")
                    sc.activation(st2[:, 4:8], st2[:, 4:8], AF.Ln)
                    ctx.inc(sc.activation(st2[:, 4:8], st2[:, 4:8], AF.Exp, scale=-0.5), "g_s2", 1)

                @blk.vector
                def _(v, hh=hh):
                    ctx.wait(v, "g_s2")
                    for sl in range(4):
                        v.tensor_single_scalar(attc[:, sl, :], attc[:, sl, :], st2[:, sl:sl + 1], ALU.subtract)
                    for sl in range(4):
                        v.tensor_scalar_mul(attc[:, sl, :], attc[:, sl, :], st2[:, 4 + sl:5 + sl])
                    for sl in range(4):
                        v.tensor_tensor(attc[:, sl, :], attc[:, sl, :], gnw_s[:, hh, :], ALU.mult)
                    for sl in range(4):
                        last = v.tensor_tensor(attc[:, sl, :], attc[:, sl, :], gnb_s[:, hh, :], ALU.add)
                    ctx.inc(last, "g_v4", 1)

                @blk.sync
                def _(s, hh=hh, qb=qb):
                    ctx.wait(s, "g_v4")
                    ctx.inc(s.dma_start(
                        out=att_o[qb * 512:(qb + 1) * 512, hh * 128:(hh + 1) * 128].rearrange("(c p) v -> p c v", p=128),
                        in_=attc[:]), "o_st", 16)

                @blk.vector
                def _(v):
                    ctx.wait(v, "o_st")
                    ctx.inc(v.memset(stat[:, 0:1], 0.0), "o_v", 1)

        @blk.gpsimd
        def _(g):
            ctx.wait(g, "o_st")
    return nc



def build_l4():
    TOK = 1024
    ST = 512
    NST = TOK // ST
    NELEM = float(INTER * D)
    nc = bass.Bass("TRN2", target_bir_lowering=False)
    att = nc.dram_tensor("att", [D, TOK], BF16, kind="ExternalInput").ap()
    x2 = nc.dram_tensor("x2", [D, TOK], F32, kind="ExternalInput").ap()
    wo = nc.dram_tensor("wo", [D, D], F32, kind="ExternalInput").ap()
    t1T = nc.dram_tensor("t1T", [D, INTER], BF16, kind="ExternalInput").ap()
    t2T = nc.dram_tensor("t2T", [D, INTER], BF16, kind="ExternalInput").ap()
    t3T = nc.dram_tensor("t3T", [INTER, D], BF16, kind="ExternalInput").ap()
    wsums = nc.dram_tensor("wsums", [8, 3], F32, kind="ExternalInput").ap()
    allones = nc.dram_tensor("allones", [128, 128], F32, kind="ExternalInput").ap()
    ident = nc.dram_tensor("ident", [128, 128], F32, kind="ExternalInput").ap()
    x4_o = nc.dram_tensor("x4", [D, TOK], F32, kind="ExternalOutput").ap()

    es = contextlib.ExitStack()
    with es:
        ctx = Ctx(nc, es)
        att_s = ctx.sb("att_s", [128, 16, ST], BF16)
        x_s = ctx.sb("x_s", [128, 16, ST], F32)
        kq_s = ctx.sb("kq_s", [128, 16, ST], BF16)
        h_s = ctx.sb("h_s", [128, 44, ST], BF16)
        xa_s = ctx.sb("xa_s", [128, ST], BF16)
        gt_s = ctx.sb("gt_s", [128, ST], BF16)
        ut_s = ctx.sb("ut_s", [128, ST], BF16)
        wo_b = [ctx.sb(f"wo_b{i}", [128, 16, 128], BF16) for i in range(2)]
        wt_b = [ctx.sb(f"wt_b{i}", [128, 16, 128], BF16) for i in range(2)]
        wu_b = [ctx.sb(f"wu_b{i}", [128, 16, 128], BF16) for i in range(2)]
        wd_b = [ctx.sb(f"wd_b{i}", [128, 44, 128], BF16) for i in range(2)]
        id_b = ctx.sb("id_b", [128, 128], BF16)
        idf_s = ctx.sb("idf_s", [128, 128], F32)
        aof = ctx.sb("aof", [128, 128], F32)
        ao_b = ctx.sb("ao_b", [128, 128], BF16)
        ws_s = ctx.sb("ws_s", [8, 3], F32)
        mrow = ctx.sb("mrow", [1, 16], F32)
        mcol = ctx.sb("mcol", [128, 3], F32)
        sqf = ctx.sb("sqf", [128, 4, ST], F32)
        rstd_t = ctx.sb("rstd_t", [128, ST], F32)
        s127_t = ctx.sb("s127_t", [128, ST], F32)
        dq_t = ctx.sb("dq_t", [128, ST], F32)
        amax = ctx.sb("amax", [128, 4], F32)
        tmax = ctx.sb("tmax", [128, 4], F32)
        rowa = ctx.sb("rowa", [1, ST], F32)
        t1f = ctx.sb("t1f", [128, 2, ST], F32)
        kib = ctx.sb("kib", [128, 2, ST], I32)
        kfb = ctx.sb("kfb", [128, 2, ST], F32)
        cbf = ctx.sb("cbf", [128, 2, ST], F32)
        pb = [ctx.ps(f"pb{i}", [128, 512]) for i in range(6)]
        pvt = ctx.ps("pvt", [128, 8, 128], BF16)
        blk = es.enter_context(nc.Block())

        @blk.gpsimd
        def _(g):
            for dst, src_ in [(aof, allones), (ao_b, allones), (id_b, ident),
                              (idf_s, ident), (ws_s, wsums)]:
                ctx.inc(g.dma_start(out=dst[:], in_=src_), "c_ld", 16)

        @blk.tensor
        def _(t):
            ctx.wait(t, "c_ld")
            ctx.inc(t.matmul(pb[0][0:1, 0:3], aof[0:8, 0:1], ws_s[:], start=True, stop=True), "i_t", 1)

        @blk.vector
        def _(v):
            ctx.wait(v, "i_t")
            ctx.inc(v.tensor_scalar_mul(mrow[0:1, 0:3], pb[0][0:1, 0:3], 1.0 / NELEM), "i_v", 1)

        @blk.tensor
        def _(t):
            ctx.wait(t, "i_v")
            ctx.inc(t.matmul(pb[1][:, 0:3], aof[0:1, :], mrow[0:1, 0:3], start=True, stop=True), "i_t2", 1)

        @blk.vector
        def _(v):
            ctx.wait(v, "i_t2")
            ctx.inc(v.tensor_copy(mcol[:], pb[1][:, 0:3]), "i_v2", 1)

        def emit_quant(tag, nch, src_s, dst_s):
            for pr in range((nch + 1) // 2):
                @blk.vector
                def _(v, pr=pr, tag=tag, nch=nch):
                    ctx.wait(v, f"{tag}_pre")
                    ids = [2 * pr + i for i in range(2) if 2 * pr + i < nch]
                    for i, kk in enumerate(ids):
                        v.tensor_tensor(t1f[:, i, :], src_s[:, kk, :], s127_t[:], ALU.mult)
                    for i in range(len(ids)):
                        v.tensor_copy(kib[:, i, :], t1f[:, i, :])
                    for i in range(len(ids)):
                        v.tensor_copy(kfb[:, i, :], kib[:, i, :])
                    for i in range(len(ids)):
                        v.tensor_tensor(t1f[:, i, :], t1f[:, i, :], kfb[:, i, :], ALU.subtract)
                    for i in range(len(ids)):
                        v.scalar_tensor_tensor(cbf[:, i, :], t1f[:, i, :], 0.5, kfb[:, i, :], ALU.is_gt, ALU.add)
                    for i in range(len(ids)):
                        v.scalar_tensor_tensor(kfb[:, i, :], t1f[:, i, :], -0.5, cbf[:, i, :], ALU.is_lt, ALU.subtract)
                    for i, kk in enumerate(ids):
                        last = v.tensor_tensor(dst_s[:, kk, :], kfb[:, i, :], dq_t[:], ALU.mult)
                    ctx.inc(last, f"{tag}_q", 1)

        def emit_norm_amax(tag, nch, src_s, isrms):
            ao_use = ao_b if src_s.dtype == BF16 else aof
            for grp in range((nch + 3) // 4):
                @blk.scalar
                def _(sc, grp=grp, tag=tag, nch=nch):
                    ctx.wait(sc, f"{tag}_pre0")
                    if grp > 0:
                        ctx.wait(sc, f"{tag}_t")
                    ids = [grp * 4 + i for i in range(4) if grp * 4 + i < nch]
                    for i, kk in enumerate(ids):
                        last = sc.activation(sqf[:, i, :], src_s[:, kk, :], AF.Square)
                    ctx.inc(last, f"{tag}_s", 1)

                @blk.tensor
                def _(t, grp=grp, tag=tag, nch=nch):
                    ctx.wait(t, f"{tag}_s")
                    ids = [grp * 4 + i for i in range(4) if grp * 4 + i < nch]
                    for i, kk in enumerate(ids):
                        mm = t.matmul(pb[5][:, :], aof[:], sqf[:, i, :], start=(kk == 0), stop=(kk == nch - 1))
                    ctx.inc(mm, f"{tag}_t", 1)

            @blk.vector
            def _(v, tag=tag):
                ctx.wait(v, f"{tag}_t")
                ctx.inc(v.tensor_scalar(rstd_t[:], pb[5][:, :], 1.0 / (nch * 128.0), 1e-6, ALU.mult, ALU.add), f"{tag}_v", 1)

            @blk.scalar
            def _(sc, tag=tag):
                ctx.wait(sc, f"{tag}_v")
                sc.activation(rstd_t[:], rstd_t[:], AF.Ln)
                ctx.inc(sc.activation(rstd_t[:], rstd_t[:], AF.Exp, scale=-0.5), f"{tag}_s2", 1)

            @blk.vector
            def _(v, tag=tag):
                ctx.wait(v, f"{tag}_s2")
                ctx.inc(v.memset(amax[:], 0.0), f"{tag}_v2", 1)

            for kk in range(nch):
                @blk.scalar
                def _(sc, kk=kk, tag=tag):
                    ctx.wait(sc, f"{tag}_v2")
                    if kk > 0:
                        ctx.wait(sc, f"{tag}_tt")
                    ctx.inc(sc.activation(xa_s[:], src_s[:, kk, :], AF.Abs), f"{tag}_as", 1)

                @blk.tensor
                def _(t, kk=kk, tag=tag):
                    ctx.wait(t, f"{tag}_as")
                    if kk > 0:
                        ctx.wait(t, f"{tag}_av")
                    for ts in range(4):
                        mm = t.transpose(pvt[:, ts, :], xa_s[:, ts * 128:(ts + 1) * 128], id_b[:])
                    ctx.inc(mm, f"{tag}_tt", 1)

                @blk.vector
                def _(v, kk=kk, tag=tag):
                    ctx.wait(v, f"{tag}_tt")
                    for ts in range(4):
                        v.tensor_reduce(tmax[:, ts:ts + 1], pvt[:, ts, :], AX.X, ALU.max)
                    for ts in range(4):
                        last = v.tensor_tensor(amax[:, ts:ts + 1], amax[:, ts:ts + 1], tmax[:, ts:ts + 1], ALU.max)
                    ctx.inc(last, f"{tag}_av", 1)

            @blk.tensor
            def _(t, tag=tag):
                ctx.wait(t, f"{tag}_av")
                for ts in range(4):
                    mm = t.transpose(pb[4][0:1, ts * 128:(ts + 1) * 128], amax[:, ts:ts + 1], idf_s[:])
                ctx.inc(mm, f"{tag}_t2", 1)

            @blk.vector
            def _(v, tag=tag):
                ctx.wait(v, f"{tag}_t2")
                inst = v.tensor_scalar_max(rowa[0:1, :], pb[4][0:1, 0:512], 1e-5)
                ctx.vsync(v, inst)
                ctx.inc(v.reciprocal(rowa[0:1, :], rowa[0:1, :]), f"{tag}_v3", 1)

            @blk.tensor
            def _(t, tag=tag):
                ctx.wait(t, f"{tag}_v3")
                ctx.inc(t.matmul(pb[4][:, :], aof[0:1, :], rowa[0:1, :], start=True, stop=True), f"{tag}_t3", 1)

            @blk.vector
            def _(v, tag=tag):
                ctx.wait(v, f"{tag}_t3")
                inst = v.tensor_scalar_mul(s127_t[:], pb[4][:, :], 127.0)
                ctx.vsync(v, inst)
                inst = v.reciprocal(t1f[:, 0, :], s127_t[:])
                ctx.vsync(v, inst)
                ctx.inc(v.tensor_tensor(dq_t[:], rstd_t[:], t1f[:, 0, :], ALU.mult), f"{tag}_pre", 1)

        for st in range(NST):
            t0 = st * ST

            @blk.gpsimd
            def _(g, t0=t0, st=st):
                ctx.wait(g, "i_v2")
                if st > 0:
                    ctx.wait(g, "out_st")
                ctx.inc(g.dma_start(out=att_s[:], in_=att.rearrange("(c p) s -> p c s", p=128)[:, :, t0:t0 + ST]), "in_ld", 16)
                ctx.inc(g.dma_start(out=x_s[:], in_=x2.rearrange("(c p) s -> p c s", p=128)[:, :, t0:t0 + ST]), "in_ld", 16)

            for m in range(16):
                @blk.gpsimd
                def _(g, m=m):
                    ctx.wait(g, "in_ld")
                    if m >= 2:
                        ctx.wait(g, "woq_t")
                    ctx.inc(g.dma_start(out=wo_b[m % 2][:], in_=wo.rearrange("(c p) q -> p c q", p=128)[:, :, m * 128:(m + 1) * 128]), "woq_ld", 16)

                @blk.tensor
                def _(t, m=m):
                    ctx.wait(t, "woq_ld")
                    if m >= 2:
                        ctx.wait(t, "woq_v")
                    for k in range(16):
                        mm = t.matmul(pb[m % 2][:, :], wo_b[m % 2][:, k, :], att_s[:, k, :],
                                      start=(k == 0), stop=(k == 15))
                    ctx.inc(mm, "woq_t", 1)

                @blk.vector
                def _(v, m=m):
                    ctx.wait(v, "woq_t")
                    ctx.inc(v.tensor_tensor(x_s[:, m, :], x_s[:, m, :], pb[m % 2][:, :], ALU.add), "woq_v", 1)

            @blk.vector
            def _(v):
                ctx.wait(v, "woq_v")
                ctx.inc(v.memset(sqf[:, 0, 0:1], 0.0), "q1_pre0", 1)
            emit_norm_amax("q1", 16, x_s, True)
            emit_quant("q1", 16, x_s, kq_s)

            for m in range(44):
                @blk.sync
                def _(s, m=m):
                    ctx.wait(s, "q1_q")
                    if m >= 2:
                        ctx.wait(s, "gu_t")
                    ctx.inc(s.dma_start(out=wt_b[m % 2][:], in_=t1T.rearrange("(c p) q -> p c q", p=128)[:, :, m * 128:(m + 1) * 128]), "gu_ld", 16)
                    ctx.inc(s.dma_start(out=wu_b[m % 2][:], in_=t2T.rearrange("(c p) q -> p c q", p=128)[:, :, m * 128:(m + 1) * 128]), "gu_ld", 16)

                @blk.tensor
                def _(t, m=m):
                    ctx.wait(t, "gu_ld")
                    if m >= 1:
                        ctx.wait(t, "h_v")
                    for k in range(16):
                        t.matmul(pb[2][:, :], wt_b[m % 2][:, k, :], kq_s[:, k, :], start=(k == 0), stop=(k == 15))
                    for k in range(16):
                        mm = t.matmul(pb[3][:, :], wu_b[m % 2][:, k, :], kq_s[:, k, :], start=(k == 0), stop=(k == 15))
                    ctx.inc(mm, "gu_t", 1)

                @blk.scalar
                def _(sc, m=m):
                    ctx.wait(sc, "gu_t")
                    sc.activation(gt_s[:], pb[2][:, :], AF.Silu, scale=mcol[:, 0:1])
                    ctx.inc(sc.activation(ut_s[:], pb[3][:, :], AF.Copy, scale=mcol[:, 1:2]), "gu_s", 1)

                @blk.vector
                def _(v, m=m):
                    ctx.wait(v, "gu_s")
                    ctx.inc(v.tensor_tensor(h_s[:, m, :], gt_s[:], ut_s[:], ALU.mult), "h_v", 1)

            @blk.vector
            def _(v):
                ctx.wait(v, "h_v")
                ctx.inc(v.memset(sqf[:, 0, 0:1], 0.0), "q2_pre0", 1)
            emit_norm_amax("q2", 44, h_s, True)
            emit_quant("q2", 44, h_s, h_s)

            for m in range(16):
                @blk.sync
                def _(s, m=m):
                    ctx.wait(s, "q2_q")
                    if m >= 2:
                        ctx.wait(s, "dn_t")
                    ctx.inc(s.dma_start(out=wd_b[m % 2][:], in_=t3T.rearrange("(c p) q -> p c q", p=128)[:, :, m * 128:(m + 1) * 128]), "dn_ld", 16)

                @blk.tensor
                def _(t, m=m):
                    ctx.wait(t, "dn_ld")
                    if m >= 2:
                        ctx.wait(t, "dn_v")
                    for k in range(44):
                        mm = t.matmul(pb[m % 2][:, :], wd_b[m % 2][:, k, :], h_s[:, k, :],
                                      start=(k == 0), stop=(k == 43))
                    ctx.inc(mm, "dn_t", 1)

                @blk.vector
                def _(v, m=m):
                    ctx.wait(v, "dn_t")
                    ctx.inc(v.scalar_tensor_tensor(x_s[:, m, :], pb[m % 2][:, :], mcol[:, 2:3],
                                                   x_s[:, m, :], ALU.mult, ALU.add), "dn_v", 1)

            @blk.sync
            def _(s, t0=t0):
                ctx.wait(s, "dn_v")
                ctx.inc(s.dma_start(out=x4_o.rearrange("(c p) s -> p c s", p=128)[:, :, t0:t0 + ST], in_=x_s[:]), "out_st", 16)

        @blk.gpsimd
        def _(g):
            ctx.wait(g, "out_st")
    return nc


def prep_l0attn(inputs):
    cst = _consts()
    x = np.asarray(inputs["L0_attn_hidden_states"])
    wqkv = np.asarray(inputs["L0_attn_W_qkv"])
    wz = np.asarray(inputs["L0_attn_W_z"])
    wb = np.asarray(inputs["L0_attn_W_b"])
    wa = np.asarray(inputs["L0_attn_W_a"])
    cw = np.asarray(inputs["L0_attn_conv_w"])
    nw = np.asarray(inputs["L0_attn_norm_w"])
    alog = np.asarray(inputs["L0_attn_A_log"])
    dtb = np.asarray(inputs["L0_attn_dt_bias"])
    in_maps = []
    for c in range(8):
        b, q = c // 4, c % 4
        qs = slice(4 * q * 128, (4 * q + 4) * 128)
        vs = slice(8 * q * 128, (8 * q + 8) * 128)
        hv = slice(8 * q, 8 * q + 8)
        chan = np.concatenate([
            np.arange(qs.start, qs.stop),
            2048 + np.arange(qs.start, qs.stop),
            4096 + np.arange(vs.start, vs.stop)])
        cwT = np.ascontiguousarray(
            cw[chan, :].reshape(16, 128, KTAP).transpose(1, 2, 0)).astype(np.float32)
        m = {
            "x": np.ascontiguousarray(x[b].T),
            "wqT": np.ascontiguousarray(wqkv[qs, :].T),
            "wkT": np.ascontiguousarray(wqkv[2048 + qs.start:2048 + qs.stop, :].T),
            "wvT": np.ascontiguousarray(wqkv[4096 + vs.start:4096 + vs.stop, :].T),
            "wzT": np.ascontiguousarray(wz[vs, :].T),
            "wbaT": np.ascontiguousarray(
                np.concatenate([wb[hv, :], wa[hv, :]], axis=0).T),
            "cwT": cwT,
            "nwrep": np.tile(nw, 8)[None, :].astype(np.float32),
            "alog": np.ascontiguousarray(alog[hv])[None, :].astype(np.float32),
            "dtb": np.ascontiguousarray(dtb[hv])[None, :].astype(np.float32),
            "ident": cst["ident"], "allones": cst["allones"],
            "mus8": cst["mus8"], "mui8": cst["mui8"],
            "trilones": cst["trilones"], "onehot8": cst["onehot8"],
        }
        in_maps.append(m)
    return in_maps


def run_l0attn(inputs):
    nc = build_l0attn()
    in_maps = prep_l0attn(inputs)
    res = run_bass_kernel_spmd(nc, in_maps, core_ids=list(range(8)))
    og = np.zeros((B, S, VAL_DIM), BF)
    for c in range(8):
        b, q = c // 4, c % 4
        og[b, :, 8 * q * 128:(8 * q + 8) * 128] = res.results[c]["og"]
    return og


def prep_w(inputs):
    cst = _consts()
    w1 = np.asarray(inputs["L1_ffn_w1"])
    w2 = np.asarray(inputs["L1_ffn_w2"])
    w3 = np.asarray(inputs["L1_ffn_w3"])
    RW1, RW3 = INTER // 8, D // 8
    maps = []
    for c in range(8):
        maps.append({
            "w1": np.ascontiguousarray(w1[c * RW1:(c + 1) * RW1, :]),
            "w2": np.ascontiguousarray(w2[c * RW1:(c + 1) * RW1, :]),
            "w3": np.ascontiguousarray(w3[c * RW3:(c + 1) * RW3, :]),
            "allones": cst["allones"],
        })
    return maps


def run_pipeline(inputs):
    cst = _consts()
    ncw = build_wsum()
    mw = prep_w(inputs)
    rw = run_bass_kernel_spmd(ncw, mw, core_ids=list(range(8)))
    wsums = np.stack([rw.results[c]["wsums"][0] for c in range(8)])

    nct = build_wtern()
    for c in range(8):
        mw[c]["wsums"] = wsums
    rt = run_bass_kernel_spmd(nct, mw, core_ids=list(range(8)))
    t1 = np.concatenate([rt.results[c]["t1"] for c in range(8)], axis=0)
    t2 = np.concatenate([rt.results[c]["t2"] for c in range(8)], axis=0)
    t3 = np.concatenate([rt.results[c]["t3"] for c in range(8)], axis=0)

    nc1 = build_l0attn()
    m1 = prep_l0attn(inputs)
    r1 = run_bass_kernel_spmd(nc1, m1, core_ids=list(range(8)))
    og = [r1.results[c]["og"] for c in range(8)]

    x = np.asarray(inputs["L0_attn_hidden_states"])
    wout = np.asarray(inputs["L0_attn_W_out"])
    gw = np.asarray(inputs["L0_ffn_gate_w"]); uw = np.asarray(inputs["L0_ffn_up_w"])
    dw = np.asarray(inputs["L0_ffn_down_w"])
    gs = np.asarray(inputs["L0_ffn_gate_s"]); us = np.asarray(inputs["L0_ffn_up_s"])
    ds = np.asarray(inputs["L0_ffn_down_s"])
    nc2 = build_l2()
    m2 = []
    woT = np.ascontiguousarray(wout.T)
    gwT = np.ascontiguousarray(gw.T); uwT = np.ascontiguousarray(uw.T)
    dwT = np.ascontiguousarray(dw.T)
    gsc = np.ascontiguousarray(gs.reshape(44, 128).T).astype(np.float32)
    usc = np.ascontiguousarray(us.reshape(44, 128).T).astype(np.float32)
    dsc = np.ascontiguousarray(ds.reshape(16, 128).T).astype(np.float32)
    for c in range(8):
        b, q = c // 4, c % 4
        tsl = slice(q * 1024, (q + 1) * 1024)
        og_fm = np.concatenate([og[b * 4 + j][tsl, :].T for j in range(4)], axis=0)
        m2.append({
            "og": np.ascontiguousarray(og_fm),
            "x0": np.ascontiguousarray(x[b, tsl, :].T.astype(np.float32)),
            "woT": woT, "gwT": gwT, "uwT": uwT, "dwT": dwT,
            "gsc": gsc, "usc": usc, "dsc": dsc, "allones": cst["allones"],
        })
    r2 = run_bass_kernel_spmd(nc2, m2, core_ids=list(range(8)))
    x2 = [r2.results[c]["x2"] for c in range(8)]
    ln2 = [r2.results[c]["ln2"] for c in range(8)]

    wq1 = np.asarray(inputs["L1_attn_Wq"]); wk1 = np.asarray(inputs["L1_attn_Wk"])
    wv1 = np.asarray(inputs["L1_attn_Wv"])
    gnw = np.asarray(inputs["L1_attn_gn_weight"]); gnb = np.asarray(inputs["L1_attn_gn_bias"])
    nc3 = build_l3()
    m3 = []
    for c in range(8):
        b, q = c // 4, c % 4
        hsl = slice(q * 512, (q + 1) * 512)
        ln2f = np.concatenate([ln2[b * 4 + j] for j in range(4)], axis=1)
        m3.append({
            "ln2": np.ascontiguousarray(ln2f),
            "wq_c": np.ascontiguousarray(wq1[:, hsl].astype(np.float32)),
            "wk_c": np.ascontiguousarray(wk1[:, hsl].astype(np.float32)),
            "wv_c": np.ascontiguousarray(wv1[:, hsl].astype(np.float32)),
            "lq1": inputs["L1_attn_lambda_q1"][None, :].astype(np.float32),
            "lk1": inputs["L1_attn_lambda_k1"][None, :].astype(np.float32),
            "lq2": inputs["L1_attn_lambda_q2"][None, :].astype(np.float32),
            "lk2": inputs["L1_attn_lambda_k2"][None, :].astype(np.float32),
            "gnw": np.ascontiguousarray(np.broadcast_to(
                gnw[hsl].reshape(4, 128)[None, :, :], (128, 4, 128))).astype(np.float32),
            "gnb": np.ascontiguousarray(np.broadcast_to(
                gnb[hsl].reshape(4, 128)[None, :, :], (128, 4, 128))).astype(np.float32),
            "allones": cst["allones"], "ident": cst["ident"],
        })
    r3 = run_bass_kernel_spmd(nc3, m3, core_ids=list(range(8)))
    att = [r3.results[c]["att"] for c in range(8)]

    wo1 = np.asarray(inputs["L1_attn_Wo"])
    nc4 = build_l4()
    m4 = []
    t1T = np.ascontiguousarray(t1.T)
    t2T = np.ascontiguousarray(t2.T)
    t3T = np.ascontiguousarray(t3.T)
    for c in range(8):
        b, q = c // 4, c % 4
        tsl = slice(q * 1024, (q + 1) * 1024)
        att_fm = np.concatenate([att[b * 4 + j][tsl, :].T for j in range(4)], axis=0)
        m4.append({
            "att": np.ascontiguousarray(att_fm),
            "x2": x2[c],
            "wo": np.ascontiguousarray(wo1.astype(np.float32)),
            "t1T": t1T, "t2T": t2T, "t3T": t3T,
            "wsums": wsums,
            "allones": cst["allones"], "ident": cst["ident"],
        })
    r4 = run_bass_kernel_spmd(nc4, m4, core_ids=list(range(8)))
    out = np.zeros((B, S, D), np.float32)
    for c in range(8):
        b, q = c // 4, c % 4
        out[b, q * 1024:(q + 1) * 1024, :] = r4.results[c]["x4"].T
    return out


def kernel(**inputs):
    return run_pipeline(inputs)


# version 06
# speedup vs baseline: 1.2516x; 1.2516x over previous
import contextlib
import math
import numpy as np
import ml_dtypes
import concourse.bass as bass
import concourse.mybir as mybir
from concourse.bass_utils import run_bass_kernel_spmd

F32 = mybir.dt.float32
BF16 = mybir.dt.bfloat16
I32 = mybir.dt.int32
AF = mybir.ActivationFunctionType
ALU = mybir.AluOpType
AX = mybir.AxisListType
BF = ml_dtypes.bfloat16

B, S, D = 2, 4096, 2048
NK, NV, DK, DV, KTAP = 16, 32, 128, 128, 4
KEY_DIM, VAL_DIM, CONV_DIM = NK * DK, NV * DV, 2 * NK * DK + NV * DV
INTER = 5632
H1, DH1 = 16, 64
LAMBDA_INIT = 0.8
C = 128


class Ctx:
    def __init__(self, nc, es):
        self.nc, self.es = nc, es
        self.sems, self.semv = {}, {}

    def sem(self, name):
        if name not in self.sems:
            self.sems[name] = self.es.enter_context(self.nc.semaphore(name))
            self.semv[name] = 0
        return self.sems[name]

    def inc(self, inst, name, n=1):
        s = self.sem(name)
        self.semv[name] += n
        inst.then_inc(s, n)

    def wait(self, eng, *names):
        for n in names:
            eng.wait_ge(self.sem(n), self.semv[n])

    def wait_val(self, eng, name, val):
        if val > 0:
            eng.wait_ge(self.sem(name), val)

    def vsync(self, v, inst):
        s = self.sem("vself")
        self.semv["vself"] += 1
        inst.then_inc(s, 1)
        v.wait_ge(s, self.semv["vself"])

    def sb(self, name, shape, dt):
        return self.es.enter_context(self.nc.sbuf_tensor(name, shape, dt))

    def ps(self, name, shape, dt=F32):
        return self.es.enter_context(self.nc.psum_tensor(name, shape, dt))


def tile_fm(a, C, TS):
    a = np.ascontiguousarray(a)
    T = a.shape[1]
    return np.ascontiguousarray(
        a.reshape(C, 128, T).transpose(1, 0, 2).reshape(128, C, T // TS, TS)
        .transpose(2, 0, 1, 3))


def tile_w(w, C):
    w = np.ascontiguousarray(w)
    M = w.shape[1]
    return np.ascontiguousarray(
        w.reshape(C, 128, M).transpose(1, 0, 2).reshape(128, C, M // 128, 128)
        .transpose(2, 0, 1, 3))


def _consts():
    i_ = np.arange(128)
    c = {}
    c["ident"] = np.eye(128, dtype=np.float32)
    c["allones"] = np.ones((128, 128), np.float32)
    up_s = (i_[:, None] < i_[None, :]).astype(np.float32)
    lo_s = (i_[None, :] < i_[:, None]).astype(np.float32)
    up_i = (i_[:, None] <= i_[None, :]).astype(np.float32)
    c["mus8"] = np.broadcast_to(up_s[:, None, :], (128, 8, 128)).copy()
    c["mls8"] = np.broadcast_to(lo_s[:, None, :], (128, 8, 128)).copy()
    c["mui8"] = np.broadcast_to(up_i[:, None, :], (128, 8, 128)).copy()
    c["trilones"] = up_i.copy()
    oh8 = np.zeros((128, 8, 8), np.float32)
    for j in range(8):
        oh8[:, j, j] = 1.0
    c["onehot8"] = oh8
    return c


def build_l0attn(dbg=False):
    NKH, NVH = 4, 8
    QC, VC = NKH * DK, NVH * DV
    SEG = 512
    NSEG = S // SEG
    CPS = SEG // C
    LN2I8 = -0.5 * math.log(128.0)

    nc = bass.Bass("TRN2", target_bir_lowering=False)
    x_in = nc.dram_tensor("x", [NSEG, 128, 16, SEG], F32, kind="ExternalInput").ap()
    wqT = nc.dram_tensor("wqT", [128, 16, QC], F32, kind="ExternalInput").ap()
    wkT = nc.dram_tensor("wkT", [128, 16, QC], F32, kind="ExternalInput").ap()
    wvT = nc.dram_tensor("wvT", [128, 16, VC], F32, kind="ExternalInput").ap()
    wzT = nc.dram_tensor("wzT", [2, 128, 16, 512], F32, kind="ExternalInput").ap()
    wbaT = nc.dram_tensor("wbaT", [128, 16, 16], F32, kind="ExternalInput").ap()
    cwT = nc.dram_tensor("cwT", [128, KTAP, 16], F32, kind="ExternalInput").ap()
    nwrep = nc.dram_tensor("nwrep", [1, VC], F32, kind="ExternalInput").ap()
    alog = nc.dram_tensor("alog", [1, NVH], F32, kind="ExternalInput").ap()
    dtb = nc.dram_tensor("dtb", [1, NVH], F32, kind="ExternalInput").ap()
    ident = nc.dram_tensor("ident", [128, 128], F32, kind="ExternalInput").ap()
    allones = nc.dram_tensor("allones", [128, 128], F32, kind="ExternalInput").ap()
    mus8 = nc.dram_tensor("mus8", [128, 8, 128], F32, kind="ExternalInput").ap()
    mui8 = nc.dram_tensor("mui8", [128, 8, 128], F32, kind="ExternalInput").ap()
    tril = nc.dram_tensor("trilones", [128, 128], F32, kind="ExternalInput").ap()
    oh8 = nc.dram_tensor("onehot8", [128, 8, 8], F32, kind="ExternalInput").ap()
    og_out = nc.dram_tensor("og", [S, VC], BF16, kind="ExternalOutput").ap()

    dbg_t = {}
    if dbg:
        for nm, shp, dt_ in [
            ("lnx", [128, 16, 512], BF16), ("qkv", [128, 16, 512], BF16),
            ("zt", [128, 4, 1024], BF16), ("gba", [128, 4, 16], F32),
            ("lrqk", [8, 512], F32), ("combo", [128, 32], F32),
            ("lgam", [128, 8], F32), ("glog", [128, 8], F32),
            ("negb", [128, 8], F32), ("e8a", [128, 8, 128], BF16),
            ("ap", [128, 8, 128], BF16), ("au", [128, 8, 128], BF16),
            ("m8", [128, 8, 128], BF16), ("kk8", [128, 8, 128], BF16),
            ("vtm", [128, 8, 128], BF16), ("ktm", [128, 4, 128], BF16),
            ("rhs", [128, 8, 128], BF16), ("uu", [128, 8, 128], BF16),
            ("u3", [128, 8, 128], BF16), ("osum", [128, 8], F32),
            ("osc", [128, 8], F32), ("ogt", [128, 8, 128], BF16),
            ("og1", [128, 1024], BF16), ("stf", [128, 8, 128], F32),
            ("erowm", [128, 8], F32), ("eu3", [128, 8], F32),
            ("nrhs", [128, 8], F32), ("rows8", [1, 2048], F32),
            ("lrtm", [128, 8], F32), ("lke", [128, 8], F32), ("lqe", [128, 8], F32),
        ]:
            dbg_t[nm] = nc.dram_tensor("dbg_" + nm, shp, dt_, kind="ExternalOutput").ap()

    es = contextlib.ExitStack()
    with es:
        ctx = Ctx(nc, es)

        def dump(blk_, name, tile, wait_name):
            if not dbg:
                return
            @blk_.sync
            def _(s):
                ctx.wait(s, wait_name)
                ctx.inc(s.dma_start(out=dbg_t[name], in_=tile), "dbg_st", 16)

        wq_s = ctx.sb("wq_s", [128, 16, QC], BF16)
        wk_s = ctx.sb("wk_s", [128, 16, QC], BF16)
        wv_s = ctx.sb("wv_s", [128, 16, VC], BF16)
        wzh = ctx.sb("wzh", [128, 16, 512], BF16)
        wba_s = ctx.sb("wba_s", [128, 16, 16], BF16)
        cw_s = ctx.sb("cw_s", [128, KTAP, 16], F32)
        cvg = ctx.sb("cvg", [128, 4, 4, 128], BF16)
        id_s = ctx.sb("id_s", [128, 128], BF16)
        idf_s = ctx.sb("idf_s", [128, 128], F32)
        ao_s = ctx.sb("ao_s", [128, 128], BF16)
        aof_s = ctx.sb("aof_s", [128, 128], F32)
        trif_s = ctx.sb("trif_s", [128, 128], F32)
        mus_s = ctx.sb("mus_s", [128, 8, 128], BF16)
        mui_s = ctx.sb("mui_s", [128, 8, 128], BF16)
        tri_s = ctx.sb("tri_s", [128, 128], BF16)
        oh8_s = ctx.sb("oh8_s", [128, 8, 8], BF16)
        alog_s = ctx.sb("alog_s", [1, NVH], F32)
        dtb_s = ctx.sb("dtb_s", [1, NVH], F32)
        nwr_s = ctx.sb("nwr_s", [1, VC], BF16)
        dtb_b = ctx.sb("dtb_b", [128, NVH], F32)
        nega_b = ctx.sb("nega_b", [128, NVH], F32)
        nw_b = ctx.sb("nw_b", [128, VC], BF16)
        xb = ctx.sb("xb", [128, 16, SEG], BF16)
        sqb = ctx.sb("sqb", [128, 4, SEG], BF16)
        mu_t = ctx.sb("mu_t", [128, SEG], F32)
        var_t = ctx.sb("var_t", [128, SEG], F32)
        mixr = ctx.sb("mixr", [128, 16, 3 + SEG], BF16)
        qkv = ctx.sb("qkv", [128, 16, SEG], BF16)
        zt = ctx.sb("zt", [128, CPS, VC], BF16)
        gba4 = ctx.sb("gba4", [128, CPS, 16], F32)
        lrqk = ctx.sb("lrqk", [8, SEG], F32)
        og1 = ctx.sb("og1", [128, VC], BF16)
        combo = ctx.sb("combo", [128, 32], F32)
        rows8 = ctx.sb("rows8", [1, 2048], F32)
        negb = ctx.sb("negb", [128, NVH], F32)
        glog = ctx.sb("glog", [128, NVH], F32)
        lgam = ctx.sb("lgam", [128, NVH], F32)
        e_u3 = ctx.sb("e_u3", [128, NVH], F32)
        erowM = ctx.sb("erowM", [128, NVH], F32)
        lgCb = ctx.sb("lgCb", [128, NVH], F32)
        nrhs = ctx.sb("nrhs", [128, NVH], F32)
        gamC = ctx.sb("gamC", [128, NVH], F32)
        osum = ctx.sb("osum", [128, NVH], F32)
        osc = ctx.sb("osc", [128, NVH], F32)
        t8a = ctx.sb("t8a", [128, NVH], F32)
        t8b = ctx.sb("t8b", [128, NVH], F32)
        lr_tm = ctx.sb("lr_tm", [128, 8], F32)
        lk_e = ctx.sb("lk_e", [128, NVH], F32)
        lq_e = ctx.sb("lq_e", [128, NVH], F32)
        kk8 = ctx.sb("kk8", [128, 8, 128], BF16)
        m8 = ctx.sb("m8", [128, 8, 128], BF16)
        ktm = ctx.sb("ktm", [128, NKH, 128], BF16)
        vtm = ctx.sb("vtm", [128, NVH, 128], BF16)
        e8a = ctx.sb("e8a", [128, 8, 128], BF16)
        ap_t = ctx.sb("ap_t", [128, 8, 128], BF16)
        au_t = ctx.sb("au_t", [128, 8, 128], BF16)
        pp_t = ctx.sb("pp_t", [128, 8, 128], BF16)
        pu_t = ctx.sb("pu_t", [128, 8, 128], BF16)
        u_t = ctx.sb("u_t", [128, 8, 128], BF16)
        u3_t = ctx.sb("u3_t", [128, 8, 128], BF16)
        st_hi = ctx.sb("st_hi", [128, 8, 128], BF16)
        st_lo = ctx.sb("st_lo", [128, 8, 128], BF16)
        st_f = ctx.sb("st_f", [128, 8, 128], F32)
        ogt = ctx.sb("ogt", [128, 8, 128], BF16)
        pb = [ctx.ps(f"pb{i}", [128, 512]) for i in range(6)]
        pvt = ctx.ps("pvt", [128, 16, 128], BF16)

        blk = es.enter_context(nc.Block())

        @blk.gpsimd
        def _(g):
            for dst, src_ in [
                (id_s, ident), (idf_s, ident), (ao_s, allones), (aof_s, allones),
                (mus_s, mus8),
                (mui_s, mui8), (tri_s, tril), (trif_s, tril), (oh8_s, oh8),
                (nwr_s, nwrep), (alog_s, alog), (dtb_s, dtb), (cw_s, cwT),
                (wq_s, wqT), (wk_s, wkT), (wv_s, wvT), (wba_s, wbaT),
            ]:
                ctx.inc(g.dma_start(out=dst[:], in_=src_), "c_ld", 16)

        @blk.scalar
        def _(sc):
            ctx.wait(sc, "c_ld")
            ctx.inc(sc.activation(alog_s[:], alog_s[:], AF.Exp), "i_s", 1)

        @blk.vector
        def _(v):
            ctx.wait(v, "i_s")
            v.tensor_scalar_mul(alog_s[:], alog_s[:], -1.0)
            v.memset(st_hi[:], 0.0)
            v.memset(st_lo[:], 0.0)
            v.memset(st_f[:], 0.0)
            ctx.inc(v.memset(mixr[:, :, 0:3], 0.0), "i_v", 1)

        @blk.tensor
        def _(t):
            ctx.wait(t, "i_v")
            t.matmul(pb[0][:, 0:NVH], aof_s[0:1, :], dtb_s[:], start=True, stop=True)
            t.matmul(pb[0][:, 16:16 + NVH], aof_s[0:1, :], alog_s[:], start=True, stop=True)
            t.matmul(pb[1][:, :], ao_s[0:1, :], nwr_s[:, 0:512], start=True, stop=True)
            ctx.inc(t.matmul(pb[2][:, :], ao_s[0:1, :], nwr_s[:, 512:1024], start=True, stop=True), "i_t", 1)

        @blk.vector
        def _(v):
            ctx.wait(v, "i_t")
            v.tensor_copy(dtb_b[:], pb[0][:, 0:NVH])
            v.tensor_copy(nega_b[:], pb[0][:, 16:16 + NVH])
            v.tensor_copy(nw_b[:, 0:512], pb[1][:, :])
            ctx.inc(v.tensor_copy(nw_b[:, 512:1024], pb[2][:, :]), "i_v2", 1)

        for seg in range(NSEG):
            t0 = seg * SEG

            @blk.gpsimd
            def _(g, t0=t0, seg=seg):
                ctx.wait(g, "i_v2")
                if seg > 0:
                    ctx.wait(g, "og_st")
                ctx.inc(g.dma_start(out=xb[:], in_=x_in[seg]), "x_ld", 16)

            for grp in range(4):
                @blk.scalar
                def _(sc, grp=grp):
                    if grp == 0:
                        ctx.wait(sc, "x_ld", "i_v2")
                    else:
                        ctx.wait(sc, "ln_t")
                    for i in range(4):
                        last = sc.activation(sqb[:, i, :], xb[:, grp * 4 + i, :], AF.Square)
                    ctx.inc(last, "ln_s", 1)

                @blk.tensor
                def _(t, grp=grp):
                    ctx.wait(t, "ln_s")
                    for i in range(4):
                        k = grp * 4 + i
                        t.matmul(pb[0][:, :], ao_s[:], xb[:, k, :], start=(k == 0), stop=(k == 15))
                        mm = t.matmul(pb[1][:, :], ao_s[:], sqb[:, i, :], start=(k == 0), stop=(k == 15))
                    ctx.inc(mm, "ln_t", 1)

            @blk.vector
            def _(v):
                ctx.wait(v, "ln_t")
                v.tensor_scalar_mul(mu_t[:], pb[0][:, :], 1.0 / D)
                v.tensor_scalar(var_t[:], pb[1][:, :], 1.0 / D, 1e-5, ALU.mult, ALU.add)
                inst = v.tensor_tensor(pb[0][:, :], mu_t[:], mu_t[:], ALU.mult)
                ctx.vsync(v, inst)
                ctx.inc(v.tensor_tensor(var_t[:], var_t[:], pb[0][:, :], ALU.subtract), "ph_v", 1)

            @blk.scalar
            def _(sc):
                ctx.wait(sc, "ph_v")
                sc.activation(var_t[:], var_t[:], AF.Ln)
                ctx.inc(sc.activation(var_t[:], var_t[:], AF.Exp, scale=-0.5), "ph_s", 1)

            @blk.vector
            def _(v, seg=seg):
                ctx.wait(v, "ph_s")
                if seg > 0:
                    v.tensor_copy(mixr[:, :, 0:3], mixr[:, :, SEG:SEG + 3])
                for k in range(16):
                    v.tensor_tensor(xb[:, k, :], xb[:, k, :], mu_t[:], ALU.subtract)
                for k in range(16):
                    last = v.tensor_tensor(xb[:, k, :], xb[:, k, :], var_t[:], ALU.mult)
                ctx.inc(last, "ph_v2", 1)

            if dbg and seg == 0:
                dump(blk, "lnx", xb[:], "ph_v2")

            WSEL = [(wq_s, m, m) for m in range(4)] + \
                   [(wk_s, m, 4 + m) for m in range(4)] + \
                   [(wv_s, m, 8 + m) for m in range(8)]
            for grp in range(4):
                @blk.tensor
                def _(t, grp=grp):
                    ctx.wait(t, "ph_v2")
                    if grp > 0:
                        ctx.wait(t, "pj_v")
                    for i in range(4):
                        w, mb, dst = WSEL[grp * 4 + i]
                        for k in range(16):
                            mm = t.matmul(pb[i][:, :], w[:, k, mb * 128:(mb + 1) * 128],
                                          xb[:, k, :], start=(k == 0), stop=(k == 15))
                    ctx.inc(mm, "pj_t", 1)

                @blk.vector
                def _(v, grp=grp):
                    ctx.wait(v, "pj_t")
                    for i in range(4):
                        _, _, dst = WSEL[grp * 4 + i]
                        last = v.tensor_copy(mixr[:, dst, 3:3 + SEG], pb[i][:, :])
                    ctx.inc(last, "pj_v", 1)

            for grp in range(4):
                @blk.vector
                def _(v, grp=grp):
                    ctx.wait(v, "pj_v")
                    if grp > 0:
                        ctx.wait(v, "cv_t")
                    for i in range(4):
                        cc = grp * 4 + i
                        for j in range(KTAP):
                            last = v.tensor_scalar_mul(cvg[:, j, i, :], id_s[:], cw_s[:, j, cc:cc + 1])
                    ctx.inc(last, "cv_v", 1)

                @blk.tensor
                def _(t, grp=grp):
                    ctx.wait(t, "cv_v")
                    if grp > 0:
                        ctx.wait(t, "cv_s")
                    for i in range(4):
                        cc = grp * 4 + i
                        for j in range(KTAP):
                            mm = t.matmul(pb[i][:, :], cvg[:, j, i, :],
                                          mixr[:, cc, j:j + SEG], start=(j == 0), stop=(j == 3))
                    ctx.inc(mm, "cv_t", 1)

                @blk.scalar
                def _(sc, grp=grp):
                    ctx.wait(sc, "cv_t")
                    for i in range(4):
                        cc = grp * 4 + i
                        last = sc.activation(qkv[:, cc, :], pb[i][:, :], AF.Silu)
                    ctx.inc(last, "cv_s", 1)

            if dbg and seg == 0:
                dump(blk, "qkv", qkv[:], "cv_s")

            for half in range(2):
                @blk.gpsimd
                def _(g, half=half):
                    ctx.wait(g, "cv_s" if half == 0 else "z_t")
                    ctx.inc(g.dma_start(out=wzh[:], in_=wzT[half]), "wz_ld", 16)

                for tc in range(CPS):
                    @blk.tensor
                    def _(t, half=half, tc=tc):
                        ctx.wait(t, "wz_ld")
                        if not (half == 0 and tc == 0):
                            ctx.wait(t, "z_sv")
                        for k in range(16):
                            mm = t.matmul(pb[tc % 2][:, :], xb[:, k, tc * 128:(tc + 1) * 128],
                                          wzh[:, k, :], start=(k == 0), stop=(k == 15))
                        if half == 0:
                            for k in range(16):
                                mm = t.matmul(pb[2][:, 0:16], xb[:, k, tc * 128:(tc + 1) * 128],
                                              wba_s[:, k, :], start=(k == 0), stop=(k == 15))
                        ctx.inc(mm, "z_t", 1)

                    @blk.scalar
                    def _(sc, half=half, tc=tc):
                        ctx.wait(sc, "z_t")
                        ctx.inc(sc.activation(zt[:, tc, half * 512:(half + 1) * 512],
                                              pb[tc % 2][:, :], AF.Silu), "z_s", 1)

                    @blk.vector
                    def _(v, half=half, tc=tc):
                        ctx.wait(v, "z_s")
                        if half == 0:
                            v.tensor_copy(gba4[:, tc, :], pb[2][:, 0:16])
                        ctx.inc(v.tensor_tensor(zt[:, tc, half * 512:(half + 1) * 512],
                                                zt[:, tc, half * 512:(half + 1) * 512],
                                                nw_b[:, half * 512:(half + 1) * 512], ALU.mult), "z_sv", 1)

            if dbg and seg == 0:
                dump(blk, "zt", zt[:], "z_sv")
                dump(blk, "gba", gba4[:], "z_sv")

            for grp in range(2):
                @blk.scalar
                def _(sc, grp=grp):
                    ctx.wait(sc, "z_sv" if grp == 0 else "lr_t")
                    for i in range(4):
                        last = sc.activation(sqb[:, i, :], qkv[:, grp * 4 + i, :], AF.Square)
                    ctx.inc(last, "lr_s", 1)

                @blk.tensor
                def _(t, grp=grp):
                    ctx.wait(t, "lr_s")
                    for i in range(4):
                        j = grp * 4 + i
                        mm = t.matmul(pb[0][0:8, :], oh8_s[:, j, :], sqb[:, i, :],
                                      start=(j == 0), stop=(j == 7))
                    ctx.inc(mm, "lr_t", 1)

            @blk.vector
            def _(v):
                ctx.wait(v, "lr_t")
                ctx.inc(v.tensor_scalar_add(lrqk[:], pb[0][0:8, :], 1e-6), "lr_v", 1)

            @blk.scalar
            def _(sc):
                ctx.wait(sc, "lr_v")
                ctx.inc(sc.activation(lrqk[:], lrqk[:], AF.Ln), "lr_s2", 1)

            if dbg and seg == 0:
                dump(blk, "lrqk", lrqk[:], "lr_s2")

            for tc in range(CPS):
                @blk.scalar
                def _(sc, tc=tc, seg=seg):
                    ctx.wait(sc, "lr_s2")
                    if not (seg == 0 and tc == 0):
                        ctx.wait(sc, "og_v")
                    sc.activation(t8a[:], gba4[:, tc, 0:8], AF.Exp, scale=-1.0)
                    ctx.inc(sc.activation(negb[:], t8a[:], AF.Ln, bias=1.0), "c0_s", 1)

                @blk.vector
                def _(v, tc=tc):
                    ctx.wait(v, "c0_s")
                    ctx.inc(v.tensor_tensor(t8b[:], gba4[:, tc, 8:16], dtb_b[:], ALU.add), "c0_v", 1)

                @blk.scalar
                def _(sc):
                    ctx.wait(sc, "c0_v")
                    sc.activation(t8b[:], t8b[:], AF.Exp)
                    ctx.inc(sc.activation(t8b[:], t8b[:], AF.Ln, bias=1.0), "c0_s2", 1)

                @blk.vector
                def _(v):
                    ctx.wait(v, "c0_s2")
                    ctx.inc(v.tensor_tensor(glog[:], t8b[:], nega_b[:], ALU.mult), "c0_v2", 1)

                @blk.tensor
                def _(t, tc=tc):
                    ctx.wait(t, "c0_v2")
                    t.matmul(pb[0][:, 0:8], trif_s[:], glog[:], start=True, stop=True)
                    t.matmul(pb[2][:, 0:8], aof_s[:], glog[:], start=True, stop=True)
                    ctx.inc(t.transpose(pb[1][0:128, 0:8], lrqk[:, tc * 128:(tc + 1) * 128], idf_s[0:8, 0:8]), "c0_t", 1)

                @blk.vector
                def _(v, tc=tc):
                    ctx.wait(v, "c0_t")
                    v.tensor_copy(lr_tm[:], pb[1][0:128, 0:8])
                    v.tensor_copy(lgam[:], pb[0][:, 0:8])
                    v.tensor_copy(lgCb[:], pb[2][:, 0:8])
                    v.tensor_copy(lk_e[:, 0::2], lr_tm[:, 4:8])
                    v.tensor_copy(lk_e[:, 1::2], lr_tm[:, 4:8])
                    v.tensor_copy(lq_e[:, 0::2], lr_tm[:, 0:4])
                    v.tensor_copy(lq_e[:, 1::2], lr_tm[:, 0:4])
                    v.scalar_tensor_tensor(t8a[:], lk_e[:], -0.5, negb[:], ALU.mult, ALU.subtract)
                    v.scalar_tensor_tensor(combo[:, 16:24], lq_e[:], -0.5, lgam[:], ALU.mult, ALU.add)
                    v.scalar_tensor_tensor(combo[:, 8:16], lk_e[:], -0.5, lgam[:], ALU.mult, ALU.add)
                    v.tensor_tensor(combo[:, 0:8], t8a[:], lgam[:], ALU.subtract)
                    ctx.inc(v.tensor_scalar_add(combo[:, 16:24], combo[:, 16:24], LN2I8), "c1_v", 1)

                if dbg and seg == 0 and tc == 0:
                    dump(blk, "lrtm", lr_tm[:], "c1_v")
                    dump(blk, "lke", lk_e[:], "c1_v")
                    dump(blk, "lqe", lq_e[:], "c1_v")
                    dump(blk, "combo", combo[:], "c1_v")
                    dump(blk, "lgam", lgam[:], "c1_v")
                    dump(blk, "glog", glog[:], "c1_v")
                    dump(blk, "negb", negb[:], "c1_v")

                @blk.vector
                def _(v):
                    ctx.wait(v, "c1_v")
                    ctx.inc(v.tensor_tensor(e_u3[:], combo[:, 0:8], lgCb[:], ALU.add), "c1_v2", 1)

                @blk.scalar
                def _(sc):
                    ctx.wait(sc, "c1_v2")
                    sc.activation(e_u3[:], e_u3[:], AF.Exp)
                    sc.activation(gamC[:], lgCb[:], AF.Exp)
                    sc.activation(erowM[:], combo[:, 16:24], AF.Exp)
                    ctx.inc(sc.activation(nrhs[:], combo[:, 8:16], AF.Exp), "c1_s", 1)

                @blk.tensor
                def _(t):
                    ctx.wait(t, "c1_s")
                    for h in range(16):
                        bank = pb[2 + h // 4]
                        mm = t.transpose(bank[0:1, (h % 4) * 128:(h % 4 + 1) * 128],
                                         combo[:, 8 + h:9 + h], idf_s[:])
                    ctx.inc(mm, "c1_t", 1)

                @blk.vector
                def _(v):
                    ctx.wait(v, "c1_t")
                    for h in range(16):
                        bank = pb[2 + h // 4]
                        v.tensor_copy(rows8[0:1, h * 128:(h + 1) * 128],
                                      bank[0:1, (h % 4) * 128:(h % 4 + 1) * 128])
                    ctx.inc(v.tensor_scalar_mul(nrhs[:], nrhs[:], -1.0), "c2_s", 1)

                if dbg and seg == 0 and tc == 0:
                    dump(blk, "nrhs", nrhs[:], "c2_s")
                    dump(blk, "erowm", erowM[:], "c2_s")
                    dump(blk, "eu3", e_u3[:], "c2_s")
                    dump(blk, "rows8", rows8[:], "c2_s")

                @blk.tensor
                def _(t, tc=tc):
                    ctx.wait(t, "c2_s")
                    sl = slice(tc * 128, (tc + 1) * 128)
                    for kh in range(4):
                        t.matmul(pb[0][:, kh * 128:(kh + 1) * 128], qkv[:, 4 + kh, sl], qkv[:, 4 + kh, sl], start=True, stop=True)
                        t.matmul(pb[1][:, kh * 128:(kh + 1) * 128], qkv[:, 4 + kh, sl], qkv[:, kh, sl], start=True, stop=True)
                        t.transpose(pvt[:, kh, :], qkv[:, 4 + kh, sl], id_s[:])
                    for h in range(8):
                        t.transpose(pvt[:, 4 + h, :], qkv[:, 8 + h, sl], id_s[:])
                    for h in range(16):
                        bank = pb[2 + h // 4]
                        mm = t.matmul(bank[:, (h % 4) * 128:(h % 4 + 1) * 128], aof_s[0:1, :],
                                      rows8[0:1, h * 128:(h + 1) * 128], start=True, stop=True)
                    ctx.inc(mm, "d1_t", 1)

                @blk.vector
                def _(v):
                    ctx.wait(v, "d1_t")
                    for h in range(8):
                        bank = pb[2 + h // 4]
                        dst = mu_t if h < 4 else var_t
                        v.tensor_scalar(dst[:, (h % 4) * 128:(h % 4 + 1) * 128],
                                        bank[:, (h % 4) * 128:(h % 4 + 1) * 128],
                                        combo[:, h:h + 1], 30.0, ALU.add, ALU.min)
                    ctx.inc(v.tensor_copy(kk8[:, 0::2, :], pb[0][:, :].rearrange("p (k c) -> p k c", c=128)), "d1_v", 1)

                @blk.scalar
                def _(sc):
                    ctx.wait(sc, "d1_v")
                    sc.activation(e8a[:, 0:4, :].rearrange("p k c -> p (k c)"), mu_t[:], AF.Exp)
                    ctx.inc(sc.activation(e8a[:, 4:8, :].rearrange("p k c -> p (k c)"), var_t[:], AF.Exp), "d1_s", 1)

                if dbg and seg == 0 and tc == 0:
                    dump(blk, "e8a", e8a[:], "d1_s")

                @blk.vector
                def _(v):
                    ctx.wait(v, "d1_s")
                    for h in range(8):
                        bank = pb[4 + h // 4]
                        dst = mu_t if h < 4 else var_t
                        v.tensor_scalar(dst[:, (h % 4) * 128:(h % 4 + 1) * 128],
                                        bank[:, (h % 4) * 128:(h % 4 + 1) * 128],
                                        combo[:, h:h + 1], 30.0, ALU.add, ALU.min)
                    v.tensor_copy(kk8[:, 1::2, :], pb[0][:, :].rearrange("p (k c) -> p k c", c=128))
                    v.tensor_copy(m8[:, 0::2, :], pb[1][:, :].rearrange("p (k c) -> p k c", c=128))
                    v.tensor_copy(m8[:, 1::2, :], pb[1][:, :].rearrange("p (k c) -> p k c", c=128))
                    v.tensor_tensor(ap_t[:], e8a[:], mus_s[:], ALU.mult)
                    v.tensor_copy(ktm[:], pvt[:, 0:4, :])
                    v.tensor_copy(vtm[:], pvt[:, 4:12, :])
                    ctx.inc(v.tensor_tensor(ap_t[:], ap_t[:], kk8[:], ALU.mult), "d1_v2", 1)

                @blk.scalar
                def _(sc):
                    ctx.wait(sc, "d1_v2")
                    sc.activation(e8a[:, 0:4, :].rearrange("p k c -> p (k c)"), mu_t[:], AF.Exp)
                    ctx.inc(sc.activation(e8a[:, 4:8, :].rearrange("p k c -> p (k c)"), var_t[:], AF.Exp), "d1_s2", 1)

                @blk.vector
                def _(v):
                    ctx.wait(v, "d1_s2")
                    inst = v.tensor_tensor(m8[:], m8[:], e8a[:], ALU.mult)
                    ctx.vsync(v, inst)
                    ctx.inc(v.tensor_tensor(m8[:], m8[:], mui_s[:], ALU.mult), "d2_v", 1)

                if dbg and seg == 0 and tc == 0:
                    dump(blk, "ap", ap_t[:], "d2_v")
                    dump(blk, "m8", m8[:], "d2_v")
                    dump(blk, "kk8", kk8[:], "d2_v")
                    dump(blk, "vtm", vtm[:], "d2_v")
                    dump(blk, "ktm", ktm[:], "d2_v")

                @blk.tensor
                def _(t):
                    ctx.wait(t, "d2_v")
                    for h in range(8):
                        mm = t.transpose(pvt[:, h, :], ap_t[:, h, :], id_s[:])
                    ctx.inc(mm, "d2_t", 1)

                @blk.vector
                def _(v):
                    ctx.wait(v, "d2_t")
                    ctx.inc(v.tensor_copy(au_t[:], pvt[:, 0:8, :]), "d2_v2", 1)

                if dbg and seg == 0 and tc == 0:
                    dump(blk, "au", au_t[:], "d2_v2")

                @blk.tensor
                def _(t, tc=tc):
                    ctx.wait(t, "d2_v2")
                    sl = slice(tc * 128, (tc + 1) * 128)
                    for h in range(8):
                        bank = pb[h // 4]
                        off = (h % 4) * 128
                        t.matmul(bank[:, off:off + 128], qkv[:, 4 + h // 2, sl], st_hi[:, h, :], start=True, stop=False)
                        mm = t.matmul(bank[:, off:off + 128], qkv[:, 4 + h // 2, sl], st_lo[:, h, :], start=False, stop=True)
                    ctx.inc(mm, "d3_t", 1)

                @blk.vector
                def _(v):
                    ctx.wait(v, "d3_t")
                    for h in range(8):
                        bank = pb[h // 4]
                        off = (h % 4) * 128
                        last = v.scalar_tensor_tensor(u_t[:, h, :], bank[:, off:off + 128],
                                                      nrhs[:, h:h + 1], vtm[:, h, :], ALU.mult, ALU.add)
                    ctx.inc(last, "d3_v", 1)

                if dbg and seg == 0 and tc == 0:
                    dump(blk, "rhs", u_t[:], "d3_v")

                for stage in range(7):
                    @blk.tensor
                    def _(t, stage=stage):
                        ctx.wait(t, "d3_v" if stage == 0 else "d4_v")
                        src_p = ap_t if stage == 0 else pp_t
                        src_u = au_t if stage == 0 else pu_t
                        for h in range(8):
                            bank = pb[h // 4]
                            off = (h % 4) * 128
                            mm = t.matmul(bank[:, off:off + 128], src_p[:, h, :], u_t[:, h, :], start=True, stop=True)
                        if stage < 6:
                            for h in range(8):
                                bank = pb[2 + h // 4]
                                off = (h % 4) * 128
                                t.matmul(bank[:, off:off + 128], src_p[:, h, :], src_u[:, h, :], start=True, stop=True)
                            for h in range(8):
                                bank = pb[4 + h // 4]
                                off = (h % 4) * 128
                                mm = t.matmul(bank[:, off:off + 128], src_u[:, h, :], src_p[:, h, :], start=True, stop=True)
                        ctx.inc(mm, "d4_t", 1)

                    @blk.vector
                    def _(v, stage=stage):
                        ctx.wait(v, "d4_t")
                        op = ALU.subtract if stage == 0 else ALU.add
                        v.tensor_tensor(u_t[:, 0:4, :], u_t[:, 0:4, :], pb[0][:, :].rearrange("p (k c) -> p k c", c=128), op)
                        last = v.tensor_tensor(u_t[:, 4:8, :], u_t[:, 4:8, :], pb[1][:, :].rearrange("p (k c) -> p k c", c=128), op)
                        if stage < 6:
                            v.tensor_copy(pu_t[:, 0:4, :], pb[2][:, :].rearrange("p (k c) -> p k c", c=128))
                            v.tensor_copy(pu_t[:, 4:8, :], pb[3][:, :].rearrange("p (k c) -> p k c", c=128))
                            v.tensor_copy(pp_t[:, 0:4, :], pb[4][:, :].rearrange("p (k c) -> p k c", c=128))
                            last = v.tensor_copy(pp_t[:, 4:8, :], pb[5][:, :].rearrange("p (k c) -> p k c", c=128))
                        ctx.inc(last, "d4_v", 1)

                if dbg and seg == 0 and tc == 0:
                    dump(blk, "uu", u_t[:], "d4_v")

                @blk.vector
                def _(v):
                    ctx.wait(v, "d4_v")
                    inst = v.memset(t8a[:], 0.0)
                    ctx.vsync(v, inst)
                    for h in range(8):
                        last = v.tensor_scalar_mul(u3_t[:, h, :], u_t[:, h, :], e_u3[:, h:h + 1])
                    ctx.inc(last, "d5_v", 1)

                if dbg and seg == 0 and tc == 0:
                    dump(blk, "u3", u3_t[:], "d5_v")

                @blk.tensor
                def _(t, tc=tc):
                    ctx.wait(t, "d5_v")
                    sl = slice(tc * 128, (tc + 1) * 128)
                    for h in range(8):
                        off = (h % 4) * 128
                        t.matmul(pb[h // 4][:, off:off + 128], m8[:, h, :], u_t[:, h, :], start=True, stop=True)
                        t.matmul(pb[2 + h // 4][:, off:off + 128], qkv[:, h // 2, sl], st_hi[:, h, :], start=True, stop=False)
                        t.matmul(pb[2 + h // 4][:, off:off + 128], qkv[:, h // 2, sl], st_lo[:, h, :], start=False, stop=True)
                        mm = t.matmul(pb[4 + h // 4][:, off:off + 128], ktm[:, h // 2, :], u3_t[:, h, :], start=True, stop=True)
                    ctx.inc(mm, "d6_t", 1)

                @blk.vector
                def _(v):
                    ctx.wait(v, "d6_t")
                    for h in range(8):
                        off = (h % 4) * 128
                        v.tensor_copy(ogt[:, h, :], pb[h // 4][:, off:off + 128])
                    for h in range(8):
                        off = (h % 4) * 128
                        last = v.scalar_tensor_tensor(ogt[:, h, :], pb[2 + h // 4][:, off:off + 128],
                                                      erowM[:, h:h + 1], ogt[:, h, :], ALU.mult, ALU.add)
                    ctx.inc(last, "d6_v", 1)

                @blk.scalar
                def _(sc):
                    ctx.wait(sc, "d6_v")
                    for h in range(8):
                        last = sc.activation(u3_t[:, h, :], ogt[:, h, :], AF.Square,
                                             accum_out=osum[:, h:h + 1])
                    ctx.inc(last, "d7_s", 1)

                if dbg and seg == 0 and tc == 0:
                    dump(blk, "osum", osum[:], "d7_s")

                @blk.vector
                def _(v):
                    ctx.wait(v, "d7_s")
                    ctx.inc(v.tensor_scalar(t8b[:], osum[:], 1.0 / 128.0, 1e-6, ALU.mult, ALU.add), "d7_v", 1)

                @blk.scalar
                def _(sc):
                    ctx.wait(sc, "d7_v")
                    sc.activation(t8b[:], t8b[:], AF.Ln)
                    ctx.inc(sc.activation(osc[:], t8b[:], AF.Exp, scale=-0.5), "d7_s3", 1)

                @blk.vector
                def _(v, tc=tc):
                    ctx.wait(v, "d7_s3")
                    for h in range(8):
                        v.tensor_scalar_mul(ogt[:, h, :], ogt[:, h, :], osc[:, h:h + 1])
                    for h in range(8):
                        bank = pb[4 + h // 4]
                        off = (h % 4) * 128
                        v.scalar_tensor_tensor(st_f[:, h, :], st_f[:, h, :], gamC[:, h:h + 1],
                                               bank[:, off:off + 128], ALU.mult, ALU.add)
                    ctx.inc(v.tensor_tensor(og1[:], ogt[:].rearrange("p k c -> p (k c)"), zt[:, tc, :], ALU.mult), "d8_v", 1)

                if dbg and seg == 0 and tc == 0:
                    dump(blk, "osc", osc[:], "d7_s3")
                    dump(blk, "ogt", ogt[:], "d8_v")
                    dump(blk, "og1", og1[:], "d8_v")
                    dump(blk, "stf", st_f[:], "d8_v")

                @blk.scalar
                def _(sc):
                    ctx.wait(sc, "d8_v")
                    ctx.inc(sc.copy(st_hi[:], st_f[:]), "d8_s", 1)

                @blk.vector
                def _(v):
                    ctx.wait(v, "d8_s")
                    if dbg:
                        ctx.wait(v, "dbg_st")
                    ctx.inc(v.tensor_tensor(st_lo[:], st_f[:], st_hi[:], ALU.subtract), "og_v", 1)

                @blk.sync
                def _(s, t0=t0, tc=tc):
                    ctx.wait(s, "og_v")
                    ctx.inc(s.dma_start(out=og_out[t0 + tc * 128:t0 + (tc + 1) * 128, :],
                                        in_=og1[:]), "og_st", 16)

        @blk.gpsimd
        def _(g):
            ctx.wait(g, "og_st")

    return nc



def build_wsum():
    RW1 = INTER // 8
    RW3 = D // 8
    NCOL = 6144
    nc = bass.Bass("TRN2", target_bir_lowering=False)
    w1 = nc.dram_tensor("w1", [RW1, D], F32, kind="ExternalInput").ap()
    w2 = nc.dram_tensor("w2", [RW1, D], F32, kind="ExternalInput").ap()
    w3 = nc.dram_tensor("w3", [RW3, INTER], F32, kind="ExternalInput").ap()
    allones = nc.dram_tensor("allones", [128, 128], F32, kind="ExternalInput").ap()
    out = nc.dram_tensor("wsums", [1, 3], F32, kind="ExternalOutput").ap()
    es = contextlib.ExitStack()
    with es:
        ctx = Ctx(nc, es)
        buf = [ctx.sb(f"buf{i}", [128, NCOL], F32) for i in range(2)]
        ao = ctx.sb("ao", [128, 128], F32)
        acc = ctx.sb("acc", [128, 3], F32)
        red = ctx.sb("red", [128, 2], F32)
        res = ctx.sb("res", [1, 3], F32)
        ps = ctx.ps("ps", [128, 4])
        blk = es.enter_context(nc.Block())

        @blk.gpsimd
        def _(g):
            ctx.inc(g.dma_start(out=ao[:], in_=allones), "ld0", 16)

        @blk.vector
        def _(v):
            ctx.wait(v, "ld0")
            ctx.inc(v.memset(acc[:], 0.0), "acc_v", 1)

        plans = [(w1, RW1 * D // 128, 0), (w2, RW1 * D // 128, 1), (w3, RW3 * INTER // 128, 2)]
        ib = 0
        for wt, total, idx2 in plans:
            nblk = (total + NCOL - 1) // NCOL
            for b in range(nblk):
                cols = min(NCOL, total - b * NCOL)
                bb = ib % 2

                @blk.sync
                def _(s, wt=wt, b=b, cols=cols, bb=bb, ib=ib):
                    if ib >= 2:
                        ctx.wait(s, "acc_v")
                    ctx.inc(s.dma_start(
                        out=buf[bb][:, 0:cols],
                        in_=wt.rearrange("a b -> (a b)").rearrange("(p x) -> p x", p=128)[:, b * NCOL:b * NCOL + cols]),
                        "w_ld", 16)

                @blk.vector
                def _(v, cols=cols, bb=bb, idx2=idx2):
                    ctx.wait(v, "w_ld")
                    inst = v.tensor_reduce(red[:, bb:bb + 1], buf[bb][:, 0:cols], AX.X, ALU.add,
                                           apply_absolute_value=True)
                    ctx.vsync(v, inst)
                    ctx.inc(v.tensor_tensor(acc[:, idx2:idx2 + 1], acc[:, idx2:idx2 + 1],
                                            red[:, bb:bb + 1], ALU.add), "acc_v", 1)
                ib += 1

        @blk.tensor
        def _(t):
            ctx.wait(t, "acc_v")
            ctx.inc(t.matmul(ps[0:1, 0:3], ao[:, 0:1], acc[:], start=True, stop=True), "fin_t", 1)

        @blk.vector
        def _(v):
            ctx.wait(v, "fin_t")
            ctx.inc(v.tensor_copy(res[:], ps[0:1, 0:3]), "fin_v", 1)

        @blk.sync
        def _(s):
            ctx.wait(s, "fin_v")
            ctx.inc(s.dma_start(out=out, in_=res[:]), "fin_st", 16)

        @blk.gpsimd
        def _(g):
            ctx.wait(g, "fin_st")
    return nc


def build_wtern():
    RW1 = INTER // 8
    RW3 = D // 8
    NCOL = 6144
    NELEM = float(INTER * D)
    nc = bass.Bass("TRN2", target_bir_lowering=False)
    w1 = nc.dram_tensor("w1", [RW1, D], F32, kind="ExternalInput").ap()
    w2 = nc.dram_tensor("w2", [RW1, D], F32, kind="ExternalInput").ap()
    w3 = nc.dram_tensor("w3", [RW3, INTER], F32, kind="ExternalInput").ap()
    wsums = nc.dram_tensor("wsums", [8, 3], F32, kind="ExternalInput").ap()
    allones = nc.dram_tensor("allones", [128, 128], F32, kind="ExternalInput").ap()
    t1o = nc.dram_tensor("t1", [RW1, D], BF16, kind="ExternalOutput").ap()
    t2o = nc.dram_tensor("t2", [RW1, D], BF16, kind="ExternalOutput").ap()
    t3o = nc.dram_tensor("t3", [RW3, INTER], BF16, kind="ExternalOutput").ap()
    es = contextlib.ExitStack()
    with es:
        ctx = Ctx(nc, es)
        buf = [ctx.sb(f"buf{i}", [128, NCOL], F32) for i in range(2)]
        tb = [ctx.sb(f"tb{i}", [128, NCOL], BF16) for i in range(2)]
        g1 = ctx.sb("g1", [128, NCOL], F32)
        g2 = ctx.sb("g2", [128, NCOL], F32)
        ao = ctx.sb("ao", [128, 128], F32)
        ws = ctx.sb("ws", [8, 3], F32)
        thr = ctx.sb("thr", [1, 8], F32)
        thrb = ctx.sb("thrb", [128, 8], F32)
        ps = ctx.ps("ps", [128, 16])
        blk = es.enter_context(nc.Block())

        @blk.gpsimd
        def _(g):
            ctx.inc(g.dma_start(out=ao[:], in_=allones), "ld0", 16)
            ctx.inc(g.dma_start(out=ws[:], in_=wsums), "ld0", 16)

        @blk.tensor
        def _(t):
            ctx.wait(t, "ld0")
            ctx.inc(t.matmul(ps[0:1, 0:3], ao[0:8, 0:1], ws[:], start=True, stop=True), "t0", 1)

        @blk.vector
        def _(v):
            ctx.wait(v, "t0")
            v.tensor_scalar(thr[0:1, 0:3], ps[0:1, 0:3], 1.0 / NELEM, 1e-5, ALU.mult, ALU.max)
            inst = v.memset(thr[0:1, 3:8], 0.0)
            ctx.vsync(v, inst)
            v.tensor_scalar_mul(thr[0:1, 0:3], thr[0:1, 0:3], 0.5)
            inst = v.memset(g1[:, 0:1], 0.0)
            ctx.vsync(v, inst)
            ctx.inc(v.tensor_scalar_mul(thr[0:1, 3:6], thr[0:1, 0:3], -1.0), "t1v", 1)

        @blk.tensor
        def _(t):
            ctx.wait(t, "t1v")
            ctx.inc(t.matmul(ps[:, 8:16], ao[0:1, :], thr[0:1, 0:8], start=True, stop=True), "t2t", 1)

        @blk.vector
        def _(v):
            ctx.wait(v, "t2t")
            ctx.inc(v.tensor_copy(thrb[:], ps[:, 8:16]), "t3v", 1)

        plans = [(w1, t1o, RW1 * D // 128, 0), (w2, t2o, RW1 * D // 128, 1),
                 (w3, t3o, RW3 * INTER // 128, 2)]
        ib = 0
        for wt, wo, total, idx2 in plans:
            nblk = (total + NCOL - 1) // NCOL
            for b in range(nblk):
                cols = min(NCOL, total - b * NCOL)
                bb = ib % 2

                @blk.sync
                def _(s, wt=wt, b=b, cols=cols, bb=bb, ib=ib):
                    ctx.wait(s, "t3v")
                    if ib >= 2:
                        ctx.wait(s, "st_d")
                    ctx.inc(s.dma_start(
                        out=buf[bb][:, 0:cols],
                        in_=wt.rearrange("a b -> (a b)").rearrange("(p x) -> p x", p=128)[:, b * NCOL:b * NCOL + cols]),
                        "w_ld", 16)

                @blk.vector
                def _(v, cols=cols, bb=bb, idx2=idx2):
                    ctx.wait(v, "w_ld")
                    v.tensor_single_scalar(g1[:, 0:cols], buf[bb][:, 0:cols], thrb[:, idx2:idx2 + 1], ALU.is_gt)
                    inst = v.tensor_single_scalar(g2[:, 0:cols], buf[bb][:, 0:cols], thrb[:, 3 + idx2:4 + idx2], ALU.is_lt)
                    ctx.vsync(v, inst)
                    ctx.inc(v.tensor_tensor(tb[bb][:, 0:cols], g2[:, 0:cols], g1[:, 0:cols], ALU.subtract), "tern_v", 1)

                @blk.sync
                def _(s, wo=wo, b=b, cols=cols, bb=bb):
                    ctx.wait(s, "tern_v")
                    ctx.inc(s.dma_start(
                        out=wo.rearrange("a b -> (a b)").rearrange("(p x) -> p x", p=128)[:, b * NCOL:b * NCOL + cols],
                        in_=tb[bb][:, 0:cols]), "st_d", 16)
                ib += 1

        @blk.gpsimd
        def _(g):
            ctx.wait(g, "st_d")
    return nc



def build_l2():
    TOK = 1024
    ST = 512
    NST = TOK // ST
    nc = bass.Bass("TRN2", target_bir_lowering=False)
    og = nc.dram_tensor("og", [NST, 128, 32, ST], BF16, kind="ExternalInput").ap()
    x0 = nc.dram_tensor("x0", [NST, 128, 16, ST], F32, kind="ExternalInput").ap()
    woT = nc.dram_tensor("woT", [16, 128, 32, 128], F32, kind="ExternalInput").ap()
    gwT = nc.dram_tensor("gwT", [44, 128, 16, 128], F32, kind="ExternalInput").ap()
    uwT = nc.dram_tensor("uwT", [44, 128, 16, 128], F32, kind="ExternalInput").ap()
    dwT = nc.dram_tensor("dwT", [16, 128, 44, 128], F32, kind="ExternalInput").ap()
    gsc = nc.dram_tensor("gsc", [128, 44], F32, kind="ExternalInput").ap()
    usc = nc.dram_tensor("usc", [128, 44], F32, kind="ExternalInput").ap()
    dsc = nc.dram_tensor("dsc", [128, 16], F32, kind="ExternalInput").ap()
    allones = nc.dram_tensor("allones", [128, 128], F32, kind="ExternalInput").ap()
    x2_o = nc.dram_tensor("x2", [NST, 128, 16, ST], F32, kind="ExternalOutput").ap()
    ln2_o = nc.dram_tensor("ln2", [NST, 128, 16, ST], BF16, kind="ExternalOutput").ap()

    es = contextlib.ExitStack()
    with es:
        ctx = Ctx(nc, es)
        og_s = ctx.sb("og_s", [128, 32, ST], BF16)
        x_s = ctx.sb("x_s", [128, 16, ST], F32)
        ln_s = ctx.sb("ln_s", [128, 16, ST], BF16)
        h_s = ctx.sb("h_s", [128, 44, ST], BF16)
        gt_s = ctx.sb("gt_s", [128, ST], BF16)
        ut_s = ctx.sb("ut_s", [128, ST], BF16)
        wo_b = [ctx.sb(f"wo_b{i}", [128, 32, 128], BF16) for i in range(2)]
        wg_b = [ctx.sb(f"wg_b{i}", [128, 16, 128], BF16) for i in range(2)]
        wu_b = [ctx.sb(f"wu_b{i}", [128, 16, 128], BF16) for i in range(2)]
        wd_b = [ctx.sb(f"wd_b{i}", [128, 44, 128], BF16) for i in range(2)]
        gsc_s = ctx.sb("gsc_s", [128, 44], F32)
        usc_s = ctx.sb("usc_s", [128, 44], F32)
        dsc_s = ctx.sb("dsc_s", [128, 16], F32)
        aof = ctx.sb("aof", [128, 128], F32)
        sqf = ctx.sb("sqf", [128, 4, ST], F32)
        mu_t = ctx.sb("mu_t", [128, ST], F32)
        var_t = ctx.sb("var_t", [128, ST], F32)
        pb = [ctx.ps(f"pb{i}", [128, 512]) for i in range(6)]
        blk = es.enter_context(nc.Block())

        @blk.gpsimd
        def _(g):
            for dst, src_ in [(gsc_s, gsc), (usc_s, usc), (dsc_s, dsc), (aof, allones)]:
                ctx.inc(g.dma_start(out=dst[:], in_=src_), "c_ld", 16)

        def emit_ln(tag, dst_bf):
            for grp in range(4):
                @blk.scalar
                def _(sc, grp=grp, tag=tag):
                    ctx.wait(sc, f"{tag}_pre")
                    if grp > 0:
                        ctx.wait(sc, f"{tag}_t")
                    for i in range(4):
                        last = sc.activation(sqf[:, i, :], x_s[:, grp * 4 + i, :], AF.Square)
                    ctx.inc(last, f"{tag}_s", 1)

                @blk.tensor
                def _(t, grp=grp, tag=tag):
                    ctx.wait(t, f"{tag}_s")
                    for i in range(4):
                        k = grp * 4 + i
                        t.matmul(pb[4][:, :], aof[:], x_s[:, k, :], start=(k == 0), stop=(k == 15))
                        mm = t.matmul(pb[5][:, :], aof[:], sqf[:, i, :], start=(k == 0), stop=(k == 15))
                    ctx.inc(mm, f"{tag}_t", 1)

            @blk.vector
            def _(v, tag=tag):
                ctx.wait(v, f"{tag}_t")
                v.tensor_scalar_mul(mu_t[:], pb[4][:, :], 1.0 / D)
                v.tensor_scalar(var_t[:], pb[5][:, :], 1.0 / D, 1e-5, ALU.mult, ALU.add)
                inst = v.tensor_tensor(pb[4][:, :], mu_t[:], mu_t[:], ALU.mult)
                ctx.vsync(v, inst)
                ctx.inc(v.tensor_tensor(var_t[:], var_t[:], pb[4][:, :], ALU.subtract), f"{tag}_v", 1)

            @blk.scalar
            def _(sc, tag=tag):
                ctx.wait(sc, f"{tag}_v")
                sc.activation(var_t[:], var_t[:], AF.Ln)
                ctx.inc(sc.activation(var_t[:], var_t[:], AF.Exp, scale=-0.5), f"{tag}_s2", 1)

            @blk.vector
            def _(v, tag=tag, dst_bf=dst_bf):
                ctx.wait(v, f"{tag}_s2")
                for k in range(16):
                    v.tensor_tensor(ln_s[:, k, :] if dst_bf is None else dst_bf[:, k, :],
                                    x_s[:, k, :], mu_t[:], ALU.subtract)
                dd = ln_s if dst_bf is None else dst_bf
                for k in range(16):
                    last = v.tensor_tensor(dd[:, k, :], dd[:, k, :], var_t[:], ALU.mult)
                ctx.inc(last, f"{tag}_v2", 1)

        for st in range(NST):
            t0 = st * ST

            @blk.gpsimd
            def _(g, t0=t0, st=st):
                ctx.wait(g, "c_ld")
                if st > 0:
                    ctx.wait(g, "out_st")
                ctx.inc(g.dma_start(out=og_s[:], in_=og[st]), "i_ld", 16)
                ctx.inc(g.dma_start(out=x_s[:], in_=x0[st]), "i_ld", 16)

            for m in range(16):
                @blk.gpsimd
                def _(g, m=m):
                    ctx.wait(g, "i_ld")
                    ctx.wait_val(g, "wo_t", ctx.semv.get("wo_t", 0) - 1)
                    ctx.inc(g.dma_start(out=wo_b[m % 2][:], in_=woT[m]), "wo_ld", 16)

                @blk.tensor
                def _(t, m=m):
                    ctx.wait(t, "wo_ld")
                    ctx.wait_val(t, "wo_v", ctx.semv.get("wo_v", 0) - 1)
                    for k in range(32):
                        mm = t.matmul(pb[m % 2][:, :], wo_b[m % 2][:, k, :], og_s[:, k, :],
                                      start=(k == 0), stop=(k == 31))
                    ctx.inc(mm, "wo_t", 1)

                @blk.vector
                def _(v, m=m):
                    ctx.wait(v, "wo_t")
                    ctx.inc(v.tensor_tensor(x_s[:, m, :], x_s[:, m, :], pb[m % 2][:, :], ALU.add), "wo_v", 1)

            @blk.vector
            def _(v):
                ctx.wait(v, "wo_v")
                ctx.inc(v.memset(sqf[:, 0, 0:1], 0.0), "ln1_pre", 1)
            emit_ln("ln1", None)

            for m in range(44):
                @blk.gpsimd
                def _(g, m=m):
                    ctx.wait(g, "ln1_v2")
                    ctx.wait_val(g, "gu_t", ctx.semv.get("gu_t", 0) - 1)
                    ctx.inc(g.dma_start(out=wg_b[m % 2][:], in_=gwT[m]), "gu_ld", 16)
                    ctx.inc(g.dma_start(out=wu_b[m % 2][:], in_=uwT[m]), "gu_ld", 16)

                @blk.tensor
                def _(t, m=m):
                    ctx.wait(t, "gu_ld")
                    ctx.wait_val(t, "h_v", ctx.semv.get("h_v", 0) - 1)
                    pg, pu = pb[2 + 2 * (m % 2)], pb[3 + 2 * (m % 2)]
                    for k in range(16):
                        t.matmul(pg[:, :], wg_b[m % 2][:, k, :], ln_s[:, k, :], start=(k == 0), stop=(k == 15))
                    for k in range(16):
                        mm = t.matmul(pu[:, :], wu_b[m % 2][:, k, :], ln_s[:, k, :], start=(k == 0), stop=(k == 15))
                    ctx.inc(mm, "gu_t", 1)

                @blk.scalar
                def _(sc, m=m):
                    ctx.wait(sc, "gu_t")
                    ctx.wait(sc, "h_v")
                    ctx.inc(sc.activation(gt_s[:], pb[2 + 2 * (m % 2)][:, :], AF.Silu, scale=gsc_s[:, m:m + 1]), "gu_s", 1)

                @blk.vector
                def _(v, m=m):
                    ctx.wait(v, "gu_s")
                    v.tensor_scalar_mul(ut_s[:], pb[3 + 2 * (m % 2)][:, :], usc_s[:, m:m + 1])
                    ctx.inc(v.tensor_tensor(h_s[:, m, :], gt_s[:], ut_s[:], ALU.mult), "h_v", 1)

            for m in range(16):
                @blk.gpsimd
                def _(g, m=m):
                    ctx.wait(g, "h_v")
                    ctx.wait_val(g, "dn_t", ctx.semv.get("dn_t", 0) - 1)
                    ctx.inc(g.dma_start(out=wd_b[m % 2][:], in_=dwT[m]), "dn_ld", 16)

                @blk.tensor
                def _(t, m=m):
                    ctx.wait(t, "dn_ld")
                    ctx.wait_val(t, "dn_v", ctx.semv.get("dn_v", 0) - 1)
                    for k in range(44):
                        mm = t.matmul(pb[m % 2][:, :], wd_b[m % 2][:, k, :], h_s[:, k, :],
                                      start=(k == 0), stop=(k == 43))
                    ctx.inc(mm, "dn_t", 1)

                @blk.vector
                def _(v, m=m):
                    ctx.wait(v, "dn_t")
                    ctx.inc(v.scalar_tensor_tensor(x_s[:, m, :], pb[m % 2][:, :], dsc_s[:, m:m + 1],
                                                   x_s[:, m, :], ALU.mult, ALU.add), "dn_v", 1)

            @blk.vector
            def _(v):
                ctx.wait(v, "dn_v")
                ctx.inc(v.memset(sqf[:, 0, 0:1], 0.0), "ln2_pre", 1)
            emit_ln("ln2", ln_s)

            @blk.sync
            def _(s, st=st):
                ctx.wait(s, "ln2_v2")
                ctx.inc(s.dma_start(out=x2_o[st], in_=x_s[:]), "out_st", 16)
                ctx.inc(s.dma_start(out=ln2_o[st], in_=ln_s[:]), "out_st", 16)

        @blk.gpsimd
        def _(g):
            ctx.wait(g, "out_st")
    return nc



def build_l3():
    HC = 512
    NKT = S // 128
    NQB = S // 512
    LN8 = math.log(8.0)
    nc = bass.Bass("TRN2", target_bir_lowering=False)
    ln2 = nc.dram_tensor("ln2", [D, S], BF16, kind="ExternalInput").ap()
    wq_c = nc.dram_tensor("wq_c", [D, HC], F32, kind="ExternalInput").ap()
    wk_c = nc.dram_tensor("wk_c", [D, HC], F32, kind="ExternalInput").ap()
    wv_c = nc.dram_tensor("wv_c", [D, HC], F32, kind="ExternalInput").ap()
    lq1 = nc.dram_tensor("lq1", [1, DH1], F32, kind="ExternalInput").ap()
    lk1 = nc.dram_tensor("lk1", [1, DH1], F32, kind="ExternalInput").ap()
    lq2 = nc.dram_tensor("lq2", [1, DH1], F32, kind="ExternalInput").ap()
    lk2 = nc.dram_tensor("lk2", [1, DH1], F32, kind="ExternalInput").ap()
    gnw = nc.dram_tensor("gnw", [128, 4, 128], F32, kind="ExternalInput").ap()
    gnb = nc.dram_tensor("gnb", [128, 4, 128], F32, kind="ExternalInput").ap()
    allones = nc.dram_tensor("allones", [128, 128], F32, kind="ExternalInput").ap()
    ident = nc.dram_tensor("ident", [128, 128], F32, kind="ExternalInput").ap()
    att_o = nc.dram_tensor("att", [S, HC], BF16, kind="ExternalOutput").ap()

    es = contextlib.ExitStack()
    with es:
        ctx = Ctx(nc, es)
        wq_s = ctx.sb("wq_s", [128, 16, HC], BF16)
        wk_s = ctx.sb("wk_s", [128, 16, HC], BF16)
        wv_s = ctx.sb("wv_s", [128, 16, HC], BF16)
        lt = ctx.sb("lt", [1, 4 * DH1], F32)
        aof = ctx.sb("aof", [128, 128], F32)
        idf_s = ctx.sb("idf_s", [128, 128], F32)
        gnw_s = ctx.sb("gnw_s", [128, 4, 128], F32)
        gnb_s = ctx.sb("gnb_s", [128, 4, 128], F32)
        xt = ctx.sb("xt", [128, 16, 512], BF16)
        qf = ctx.sb("qf", [128, 4, S], BF16)
        kf = ctx.sb("kf", [128, 4, S], BF16)
        vt = ctx.sb("vt", [128, 4, 32, 132], BF16)
        nrm = ctx.sb("nrm", [1, 64], F32)
        negc = ctx.sb("negc", [128, 4], F32)
        lamv = ctx.sb("lamv", [128, 1], F32)
        p_sb = ctx.sb("p_sb", [128, 2, 512], BF16)
        att1 = ctx.sb("att1", [128, 4, 128], F32)
        att2 = ctx.sb("att2", [128, 4, 128], F32)
        attc = ctx.sb("attc", [128, 4, 128], BF16)
        stat = ctx.sb("stat", [128, 16], F32)
        st2 = ctx.sb("st2", [128, 16], F32)
        sdump = ctx.sb("sdump", [128, 512], BF16)
        ps_s = [ctx.ps(f"ps_s{i}", [128, 512]) for i in range(2)]
        ps_o = [ctx.ps(f"ps_o{i}", [128, 2, 132]) for i in range(4)]
        ps_m = ctx.ps("ps_m", [128, 512])
        blk = es.enter_context(nc.Block())

        @blk.gpsimd
        def _(g):
            for dst, src_ in [(aof, allones), (idf_s, ident), (gnw_s, gnw), (gnb_s, gnb)]:
                ctx.inc(g.dma_start(out=dst[:], in_=src_), "c_ld", 16)
            for i, src_ in enumerate([lq1, lk1, lq2, lk2]):
                ctx.inc(g.dma_start(out=lt[0:1, i * DH1:(i + 1) * DH1], in_=src_), "c_ld", 16)
            ctx.inc(g.dma_start(out=wq_s[:], in_=wq_c.rearrange("(c p) q -> p c q", p=128)), "c_ld", 16)
            ctx.inc(g.dma_start(out=wk_s[:], in_=wk_c.rearrange("(c p) q -> p c q", p=128)), "c_ld", 16)
            ctx.inc(g.dma_start(out=wv_s[:], in_=wv_c.rearrange("(c p) q -> p c q", p=128)), "c_ld", 16)

        @blk.vector
        def _(v):
            ctx.wait(v, "c_ld")
            v.tensor_tensor(lt[0:1, 0:DH1], lt[0:1, 0:DH1], lt[0:1, DH1:2 * DH1], ALU.mult)
            inst = v.tensor_tensor(lt[0:1, 2 * DH1:3 * DH1], lt[0:1, 2 * DH1:3 * DH1], lt[0:1, 3 * DH1:4 * DH1], ALU.mult)
            ctx.vsync(v, inst)
            v.tensor_reduce(nrm[0:1, 0:1], lt[0:1, 0:DH1], AX.X, ALU.add)
            inst = v.tensor_reduce(nrm[0:1, 1:2], lt[0:1, 2 * DH1:3 * DH1], AX.X, ALU.add)
            ctx.vsync(v, inst)
            v.tensor_scalar_mul(gnw_s[:], gnw_s[:], 1.0 - LAMBDA_INIT)
            ctx.inc(v.tensor_scalar_mul(gnb_s[:], gnb_s[:], 1.0 - LAMBDA_INIT), "l_v", 1)

        @blk.scalar
        def _(sc):
            ctx.wait(sc, "l_v")
            ctx.inc(sc.activation(nrm[0:1, 0:2], nrm[0:1, 0:2], AF.Exp), "l_s", 1)

        @blk.vector
        def _(v):
            ctx.wait(v, "l_s")
            inst = v.tensor_tensor(nrm[0:1, 2:3], nrm[0:1, 0:1], nrm[0:1, 1:2], ALU.subtract)
            ctx.vsync(v, inst)
            inst = v.tensor_scalar(nrm[0:1, 3:4], nrm[0:1, 2:3], LAMBDA_INIT, -1.0, ALU.add, ALU.mult)
            ctx.vsync(v, inst)
            ctx.inc(v.memset(vt[:, :, :, 128:132], 0.0), "l_v2", 1)

        @blk.tensor
        def _(t):
            ctx.wait(t, "l_v2")
            ctx.inc(t.matmul(ps_m[:, 0:1], aof[0:1, :], nrm[0:1, 3:4], start=True, stop=True), "l_t", 1)

        @blk.vector
        def _(v):
            ctx.wait(v, "l_t")
            v.tensor_copy(lamv[:], ps_m[:, 0:1])
            ctx.inc(v.memset(vt[:, :, :, 128:129], 1.0), "l_v3", 1)

        for n in range(NQB):
            @blk.gpsimd
            def _(g, n=n):
                ctx.wait(g, "l_v3")
                if n > 0:
                    ctx.wait(g, "pv_t")
                ctx.inc(g.dma_start(out=xt[:], in_=ln2.rearrange("(c p) s -> p c s", p=128)[:, :, n * 512:(n + 1) * 512]), "x_ld", 16)

            outs = [("q", hh) for hh in range(4)] + [("k", hh) for hh in range(4)]
            for j, (kind, hh) in enumerate(outs):
                @blk.tensor
                def _(t, j=j, kind=kind, hh=hh):
                    ctx.wait(t, "x_ld")
                    if ctx.semv.get("pj_v", 0) >= 2:
                        ctx.wait(t, "pj_v")
                    w = wq_s if kind == "q" else wk_s
                    for k in range(16):
                        mm = t.matmul(ps_s[j % 2][:, :], w[:, k, hh * 128:(hh + 1) * 128], xt[:, k, :],
                                      start=(k == 0), stop=(k == 15))
                    ctx.inc(mm, "pj_t", 1)

                @blk.vector
                def _(v, j=j, kind=kind, hh=hh, n=n):
                    ctx.wait(v, "pj_t")
                    dst = qf if kind == "q" else kf
                    ctx.inc(v.tensor_copy(dst[:, hh, n * 512:(n + 1) * 512], ps_s[j % 2][:, :]), "pj_v", 1)

            for ts in range(4):
                @blk.tensor
                def _(t, ts=ts):
                    ctx.wait(t, "pj_v")
                    if ts >= 1:
                        ctx.wait(t, "pv_v")
                    for k in range(16):
                        mm = t.matmul(ps_m[:, :], xt[:, k, ts * 128:(ts + 1) * 128], wv_s[:, k, :],
                                      start=(k == 0), stop=(k == 15))
                    ctx.inc(mm, "pv_t", 1)

                @blk.vector
                def _(v, ts=ts, n=n):
                    ctx.wait(v, "pv_t")
                    for hh in range(4):
                        last = v.tensor_copy(vt[:, hh, n * 4 + ts, 0:128], ps_m[:, hh * 128:(hh + 1) * 128])
                    ctx.inc(last, "pv_v", 1)

        @blk.vector
        def _(v):
            ctx.wait(v, "pv_v")
            for hh in range(4):
                v.tensor_reduce(stat[:, hh:hh + 1], qf[:, hh, :], AX.X, ALU.max, apply_absolute_value=True)
            for hh in range(4):
                last = v.tensor_reduce(stat[:, 4 + hh:5 + hh], kf[:, hh, :], AX.X, ALU.max, apply_absolute_value=True)
            ctx.inc(last, "m_v", 1)

        @blk.tensor
        def _(t):
            ctx.wait(t, "m_v")
            for c in range(8):
                bank = ps_s[c // 4]
                mm = t.transpose(bank[0:1, (c % 4) * 128:(c % 4 + 1) * 128], stat[:, c:c + 1], idf_s[:])
            ctx.inc(mm, "m_t", 1)

        @blk.vector
        def _(v):
            ctx.wait(v, "m_t")
            for c in range(8):
                bank = ps_s[c // 4]
                last = v.tensor_reduce(nrm[0:1, 8 + c:9 + c], bank[0:1, (c % 4) * 128:(c % 4 + 1) * 128], AX.X, ALU.max)
            ctx.inc(last, "m_v2", 1)

        @blk.scalar
        def _(sc):
            ctx.wait(sc, "m_v2")
            ctx.inc(sc.activation(nrm[0:1, 8:16], nrm[0:1, 8:16], AF.Ln), "m_s", 1)

        @blk.vector
        def _(v):
            ctx.wait(v, "m_s")
            inst = v.tensor_tensor(nrm[0:1, 16:20], nrm[0:1, 8:12], nrm[0:1, 12:16], ALU.add)
            ctx.vsync(v, inst)
            ctx.inc(v.tensor_scalar(nrm[0:1, 20:24], nrm[0:1, 16:20], LN8 + math.log(64.0), -1.0, ALU.add, ALU.mult), "m_v3", 1)

        @blk.tensor
        def _(t):
            ctx.wait(t, "m_v3")
            ctx.inc(t.matmul(ps_m[:, 0:4], aof[0:1, :], nrm[0:1, 20:24], start=True, stop=True), "m_t2", 1)

        @blk.vector
        def _(v):
            ctx.wait(v, "m_v4" if False else "m_t2")
            ctx.inc(v.tensor_copy(negc[:], ps_m[:, 0:4]), "m_v4", 1)

        for hh in range(4):
            for qb in range(NQB):
                for kt in range(NKT):
                    @blk.tensor
                    def _(t, hh=hh, qb=qb, kt=kt):
                        if hh == 0 and qb == 0 and kt == 0:
                            ctx.wait(t, "m_v4")
                        else:
                            ctx.wait(t, "a_s")
                            if kt == 0:
                                ctx.wait(t, "o_v")
                        ksl = slice(kt * 128, (kt + 1) * 128)
                        qsl = slice(qb * 512, (qb + 1) * 512)
                        t.matmul(ps_s[0][:, :], kf[0:64, hh, ksl], qf[0:64, hh, qsl],
                                 start=True, stop=True, tile_position=(0, 0))
                        ctx.inc(t.matmul(ps_s[1][:, :], kf[64:128, hh, ksl], qf[64:128, hh, qsl],
                                         start=True, stop=True, tile_position=(64, 0)), "a_t", 1)

                    @blk.scalar
                    def _(sc, hh=hh):
                        ctx.wait(sc, "a_t")
                        sc.activation(p_sb[:, 0, :], ps_s[0][:, :], AF.Exp, scale=0.125, bias=negc[:, hh:hh + 1])
                        ctx.inc(sc.activation(p_sb[:, 1, :], ps_s[1][:, :], AF.Exp, scale=0.125, bias=negc[:, hh:hh + 1]), "a_s", 1)

                    @blk.tensor
                    def _(t, hh=hh, kt=kt):
                        ctx.wait(t, "a_s")
                        for sl in range(4):
                            for a in range(2):
                                mm = t.matmul(ps_o[sl][:, a, 0:129], p_sb[:, a, sl * 128:(sl + 1) * 128],
                                              vt[:, hh, kt, 0:129], start=(kt == 0), stop=(kt == NKT - 1))
                        ctx.inc(mm, "a_t2", 1)

                @blk.vector
                def _(v):
                    ctx.wait(v, "a_t2")
                    for sl in range(4):
                        v.reciprocal(stat[:, sl:sl + 1], ps_o[sl][:, 0, 128:129])
                        v.reciprocal(stat[:, 4 + sl:5 + sl], ps_o[sl][:, 1, 128:129])
                    for sl in range(4):
                        v.tensor_scalar_mul(att1[:, sl, :], ps_o[sl][:, 0, 0:128], stat[:, sl:sl + 1])
                        v.tensor_scalar_mul(att2[:, sl, :], ps_o[sl][:, 1, 0:128], stat[:, 4 + sl:5 + sl])
                    for sl in range(4):
                        last = v.scalar_tensor_tensor(attc[:, sl, :], att2[:, sl, :], lamv[:, 0:1],
                                                      att1[:, sl, :], ALU.mult, ALU.add)
                    ctx.inc(last, "g_v", 1)

                @blk.vector
                def _(v):
                    ctx.wait(v, "g_v")
                    for sl in range(4):
                        last = v.tensor_reduce(stat[:, 8 + sl:9 + sl], attc[:, sl, :], AX.X, ALU.add)
                    ctx.inc(last, "g_v2", 1)

                @blk.scalar
                def _(sc):
                    ctx.wait(sc, "g_v2")
                    for sl in range(4):
                        last = sc.activation(sdump[:, sl * 128:(sl + 1) * 128], attc[:, sl, :], AF.Square,
                                             accum_out=stat[:, 12 + sl:13 + sl])
                    ctx.inc(last, "g_s", 1)

                @blk.vector
                def _(v):
                    ctx.wait(v, "g_s")
                    v.tensor_scalar_mul(st2[:, 0:4], stat[:, 8:12], 1.0 / 128.0)
                    inst = v.tensor_scalar(st2[:, 4:8], stat[:, 12:16], 1.0 / 128.0, 1e-5, ALU.mult, ALU.add)
                    ctx.vsync(v, inst)
                    inst = v.tensor_tensor(st2[:, 8:12], st2[:, 0:4], st2[:, 0:4], ALU.mult)
                    ctx.vsync(v, inst)
                    ctx.inc(v.tensor_tensor(st2[:, 4:8], st2[:, 4:8], st2[:, 8:12], ALU.subtract), "g_v3", 1)

                @blk.scalar
                def _(sc):
                    ctx.wait(sc, "g_v3")
                    sc.activation(st2[:, 4:8], st2[:, 4:8], AF.Ln)
                    ctx.inc(sc.activation(st2[:, 4:8], st2[:, 4:8], AF.Exp, scale=-0.5), "g_s2", 1)

                @blk.vector
                def _(v, hh=hh):
                    ctx.wait(v, "g_s2")
                    for sl in range(4):
                        v.tensor_single_scalar(attc[:, sl, :], attc[:, sl, :], st2[:, sl:sl + 1], ALU.subtract)
                    for sl in range(4):
                        v.tensor_scalar_mul(attc[:, sl, :], attc[:, sl, :], st2[:, 4 + sl:5 + sl])
                    for sl in range(4):
                        v.tensor_tensor(attc[:, sl, :], attc[:, sl, :], gnw_s[:, hh, :], ALU.mult)
                    for sl in range(4):
                        last = v.tensor_tensor(attc[:, sl, :], attc[:, sl, :], gnb_s[:, hh, :], ALU.add)
                    ctx.inc(last, "g_v4", 1)

                @blk.sync
                def _(s, hh=hh, qb=qb):
                    ctx.wait(s, "g_v4")
                    ctx.inc(s.dma_start(
                        out=att_o[qb * 512:(qb + 1) * 512, hh * 128:(hh + 1) * 128].rearrange("(c p) v -> p c v", p=128),
                        in_=attc[:]), "o_st", 16)

                @blk.vector
                def _(v):
                    ctx.wait(v, "o_st")
                    ctx.inc(v.memset(stat[:, 0:1], 0.0), "o_v", 1)

        @blk.gpsimd
        def _(g):
            ctx.wait(g, "o_st")
    return nc



def build_l4():
    TOK = 1024
    ST = 512
    NST = TOK // ST
    NELEM = float(INTER * D)
    nc = bass.Bass("TRN2", target_bir_lowering=False)
    att = nc.dram_tensor("att", [D, TOK], BF16, kind="ExternalInput").ap()
    x2 = nc.dram_tensor("x2", [D, TOK], F32, kind="ExternalInput").ap()
    wo = nc.dram_tensor("wo", [D, D], F32, kind="ExternalInput").ap()
    t1T = nc.dram_tensor("t1T", [D, INTER], BF16, kind="ExternalInput").ap()
    t2T = nc.dram_tensor("t2T", [D, INTER], BF16, kind="ExternalInput").ap()
    t3T = nc.dram_tensor("t3T", [INTER, D], BF16, kind="ExternalInput").ap()
    wsums = nc.dram_tensor("wsums", [8, 3], F32, kind="ExternalInput").ap()
    allones = nc.dram_tensor("allones", [128, 128], F32, kind="ExternalInput").ap()
    ident = nc.dram_tensor("ident", [128, 128], F32, kind="ExternalInput").ap()
    x4_o = nc.dram_tensor("x4", [D, TOK], F32, kind="ExternalOutput").ap()

    es = contextlib.ExitStack()
    with es:
        ctx = Ctx(nc, es)
        att_s = ctx.sb("att_s", [128, 16, ST], BF16)
        x_s = ctx.sb("x_s", [128, 16, ST], F32)
        kq_s = ctx.sb("kq_s", [128, 16, ST], BF16)
        h_s = ctx.sb("h_s", [128, 44, ST], BF16)
        xa_s = ctx.sb("xa_s", [128, ST], BF16)
        gt_s = ctx.sb("gt_s", [128, ST], BF16)
        ut_s = ctx.sb("ut_s", [128, ST], BF16)
        wo_b = [ctx.sb(f"wo_b{i}", [128, 16, 128], BF16) for i in range(2)]
        wt_b = [ctx.sb(f"wt_b{i}", [128, 16, 128], BF16) for i in range(2)]
        wu_b = [ctx.sb(f"wu_b{i}", [128, 16, 128], BF16) for i in range(2)]
        wd_b = [ctx.sb(f"wd_b{i}", [128, 44, 128], BF16) for i in range(2)]
        id_b = ctx.sb("id_b", [128, 128], BF16)
        idf_s = ctx.sb("idf_s", [128, 128], F32)
        aof = ctx.sb("aof", [128, 128], F32)
        ao_b = ctx.sb("ao_b", [128, 128], BF16)
        ws_s = ctx.sb("ws_s", [8, 3], F32)
        mrow = ctx.sb("mrow", [1, 16], F32)
        mcol = ctx.sb("mcol", [128, 3], F32)
        sqf = ctx.sb("sqf", [128, 4, ST], F32)
        rstd_t = ctx.sb("rstd_t", [128, ST], F32)
        s127_t = ctx.sb("s127_t", [128, ST], F32)
        dq_t = ctx.sb("dq_t", [128, ST], F32)
        amax = ctx.sb("amax", [128, 4], F32)
        tmax = ctx.sb("tmax", [128, 4], F32)
        rowa = ctx.sb("rowa", [1, ST], F32)
        t1f = ctx.sb("t1f", [128, 2, ST], F32)
        kib = ctx.sb("kib", [128, 2, ST], I32)
        kfb = ctx.sb("kfb", [128, 2, ST], F32)
        cbf = ctx.sb("cbf", [128, 2, ST], F32)
        pb = [ctx.ps(f"pb{i}", [128, 512]) for i in range(6)]
        pvt = ctx.ps("pvt", [128, 8, 128], BF16)
        blk = es.enter_context(nc.Block())

        @blk.gpsimd
        def _(g):
            for dst, src_ in [(aof, allones), (ao_b, allones), (id_b, ident),
                              (idf_s, ident), (ws_s, wsums)]:
                ctx.inc(g.dma_start(out=dst[:], in_=src_), "c_ld", 16)

        @blk.tensor
        def _(t):
            ctx.wait(t, "c_ld")
            ctx.inc(t.matmul(pb[0][0:1, 0:3], aof[0:8, 0:1], ws_s[:], start=True, stop=True), "i_t", 1)

        @blk.vector
        def _(v):
            ctx.wait(v, "i_t")
            ctx.inc(v.tensor_scalar_mul(mrow[0:1, 0:3], pb[0][0:1, 0:3], 1.0 / NELEM), "i_v", 1)

        @blk.tensor
        def _(t):
            ctx.wait(t, "i_v")
            ctx.inc(t.matmul(pb[1][:, 0:3], aof[0:1, :], mrow[0:1, 0:3], start=True, stop=True), "i_t2", 1)

        @blk.vector
        def _(v):
            ctx.wait(v, "i_t2")
            ctx.inc(v.tensor_copy(mcol[:], pb[1][:, 0:3]), "i_v2", 1)

        def emit_quant(tag, nch, src_s, dst_s):
            for pr in range((nch + 1) // 2):
                @blk.vector
                def _(v, pr=pr, tag=tag, nch=nch):
                    ctx.wait(v, f"{tag}_pre")
                    ids = [2 * pr + i for i in range(2) if 2 * pr + i < nch]
                    for i, kk in enumerate(ids):
                        v.tensor_tensor(t1f[:, i, :], src_s[:, kk, :], s127_t[:], ALU.mult)
                    for i in range(len(ids)):
                        v.tensor_copy(kib[:, i, :], t1f[:, i, :])
                    for i in range(len(ids)):
                        v.tensor_copy(kfb[:, i, :], kib[:, i, :])
                    for i in range(len(ids)):
                        v.tensor_tensor(t1f[:, i, :], t1f[:, i, :], kfb[:, i, :], ALU.subtract)
                    for i in range(len(ids)):
                        v.scalar_tensor_tensor(cbf[:, i, :], t1f[:, i, :], 0.5, kfb[:, i, :], ALU.is_gt, ALU.add)
                    for i in range(len(ids)):
                        v.scalar_tensor_tensor(kfb[:, i, :], t1f[:, i, :], -0.5, cbf[:, i, :], ALU.is_lt, ALU.subtract)
                    for i, kk in enumerate(ids):
                        last = v.tensor_tensor(dst_s[:, kk, :], kfb[:, i, :], dq_t[:], ALU.mult)
                    ctx.inc(last, f"{tag}_q", 1)

        def emit_norm_amax(tag, nch, src_s, isrms):
            ao_use = ao_b if src_s.dtype == BF16 else aof
            for grp in range((nch + 3) // 4):
                @blk.scalar
                def _(sc, grp=grp, tag=tag, nch=nch):
                    ctx.wait(sc, f"{tag}_pre0")
                    if grp > 0:
                        ctx.wait(sc, f"{tag}_t")
                    ids = [grp * 4 + i for i in range(4) if grp * 4 + i < nch]
                    for i, kk in enumerate(ids):
                        last = sc.activation(sqf[:, i, :], src_s[:, kk, :], AF.Square)
                    ctx.inc(last, f"{tag}_s", 1)

                @blk.tensor
                def _(t, grp=grp, tag=tag, nch=nch):
                    ctx.wait(t, f"{tag}_s")
                    ids = [grp * 4 + i for i in range(4) if grp * 4 + i < nch]
                    for i, kk in enumerate(ids):
                        mm = t.matmul(pb[5][:, :], aof[:], sqf[:, i, :], start=(kk == 0), stop=(kk == nch - 1))
                    ctx.inc(mm, f"{tag}_t", 1)

            @blk.vector
            def _(v, tag=tag):
                ctx.wait(v, f"{tag}_t")
                ctx.inc(v.tensor_scalar(rstd_t[:], pb[5][:, :], 1.0 / (nch * 128.0), 1e-6, ALU.mult, ALU.add), f"{tag}_v", 1)

            @blk.scalar
            def _(sc, tag=tag):
                ctx.wait(sc, f"{tag}_v")
                sc.activation(rstd_t[:], rstd_t[:], AF.Ln)
                ctx.inc(sc.activation(rstd_t[:], rstd_t[:], AF.Exp, scale=-0.5), f"{tag}_s2", 1)

            @blk.vector
            def _(v, tag=tag):
                ctx.wait(v, f"{tag}_s2")
                ctx.inc(v.memset(amax[:], 0.0), f"{tag}_v2", 1)

            for kk in range(nch):
                @blk.scalar
                def _(sc, kk=kk, tag=tag):
                    ctx.wait(sc, f"{tag}_v2")
                    if kk > 0:
                        ctx.wait(sc, f"{tag}_tt")
                    ctx.inc(sc.activation(xa_s[:], src_s[:, kk, :], AF.Abs), f"{tag}_as", 1)

                @blk.tensor
                def _(t, kk=kk, tag=tag):
                    ctx.wait(t, f"{tag}_as")
                    if kk > 0:
                        ctx.wait(t, f"{tag}_av")
                    for ts in range(4):
                        mm = t.transpose(pvt[:, ts, :], xa_s[:, ts * 128:(ts + 1) * 128], id_b[:])
                    ctx.inc(mm, f"{tag}_tt", 1)

                @blk.vector
                def _(v, kk=kk, tag=tag):
                    ctx.wait(v, f"{tag}_tt")
                    for ts in range(4):
                        v.tensor_reduce(tmax[:, ts:ts + 1], pvt[:, ts, :], AX.X, ALU.max)
                    for ts in range(4):
                        last = v.tensor_tensor(amax[:, ts:ts + 1], amax[:, ts:ts + 1], tmax[:, ts:ts + 1], ALU.max)
                    ctx.inc(last, f"{tag}_av", 1)

            @blk.tensor
            def _(t, tag=tag):
                ctx.wait(t, f"{tag}_av")
                for ts in range(4):
                    mm = t.transpose(pb[4][0:1, ts * 128:(ts + 1) * 128], amax[:, ts:ts + 1], idf_s[:])
                ctx.inc(mm, f"{tag}_t2", 1)

            @blk.vector
            def _(v, tag=tag):
                ctx.wait(v, f"{tag}_t2")
                inst = v.tensor_scalar_max(rowa[0:1, :], pb[4][0:1, 0:512], 1e-5)
                ctx.vsync(v, inst)
                ctx.inc(v.reciprocal(rowa[0:1, :], rowa[0:1, :]), f"{tag}_v3", 1)

            @blk.tensor
            def _(t, tag=tag):
                ctx.wait(t, f"{tag}_v3")
                ctx.inc(t.matmul(pb[4][:, :], aof[0:1, :], rowa[0:1, :], start=True, stop=True), f"{tag}_t3", 1)

            @blk.vector
            def _(v, tag=tag):
                ctx.wait(v, f"{tag}_t3")
                inst = v.tensor_scalar_mul(s127_t[:], pb[4][:, :], 127.0)
                ctx.vsync(v, inst)
                inst = v.reciprocal(t1f[:, 0, :], s127_t[:])
                ctx.vsync(v, inst)
                ctx.inc(v.tensor_tensor(dq_t[:], rstd_t[:], t1f[:, 0, :], ALU.mult), f"{tag}_pre", 1)

        for st in range(NST):
            t0 = st * ST

            @blk.gpsimd
            def _(g, t0=t0, st=st):
                ctx.wait(g, "i_v2")
                if st > 0:
                    ctx.wait(g, "out_st")
                ctx.inc(g.dma_start(out=att_s[:], in_=att.rearrange("(c p) s -> p c s", p=128)[:, :, t0:t0 + ST]), "in_ld", 16)
                ctx.inc(g.dma_start(out=x_s[:], in_=x2.rearrange("(c p) s -> p c s", p=128)[:, :, t0:t0 + ST]), "in_ld", 16)

            for m in range(16):
                @blk.gpsimd
                def _(g, m=m):
                    ctx.wait(g, "in_ld")
                    if m >= 2:
                        ctx.wait(g, "woq_t")
                    ctx.inc(g.dma_start(out=wo_b[m % 2][:], in_=wo[m]), "woq_ld", 16)

                @blk.tensor
                def _(t, m=m):
                    ctx.wait(t, "woq_ld")
                    if m >= 2:
                        ctx.wait(t, "woq_v")
                    for k in range(16):
                        mm = t.matmul(pb[m % 2][:, :], wo_b[m % 2][:, k, :], att_s[:, k, :],
                                      start=(k == 0), stop=(k == 15))
                    ctx.inc(mm, "woq_t", 1)

                @blk.vector
                def _(v, m=m):
                    ctx.wait(v, "woq_t")
                    ctx.inc(v.tensor_tensor(x_s[:, m, :], x_s[:, m, :], pb[m % 2][:, :], ALU.add), "woq_v", 1)

            @blk.vector
            def _(v):
                ctx.wait(v, "woq_v")
                ctx.inc(v.memset(sqf[:, 0, 0:1], 0.0), "q1_pre0", 1)
            emit_norm_amax("q1", 16, x_s, True)
            emit_quant("q1", 16, x_s, kq_s)

            for m in range(44):
                @blk.sync
                def _(s, m=m):
                    ctx.wait(s, "q1_q")
                    if m >= 2:
                        ctx.wait(s, "gu_t")
                    ctx.inc(s.dma_start(out=wt_b[m % 2][:], in_=t1T[m]), "gu_ld", 16)
                    ctx.inc(s.dma_start(out=wu_b[m % 2][:], in_=t2T[m]), "gu_ld", 16)

                @blk.tensor
                def _(t, m=m):
                    ctx.wait(t, "gu_ld")
                    if m >= 1:
                        ctx.wait(t, "h_v")
                    for k in range(16):
                        t.matmul(pb[2][:, :], wt_b[m % 2][:, k, :], kq_s[:, k, :], start=(k == 0), stop=(k == 15))
                    for k in range(16):
                        mm = t.matmul(pb[3][:, :], wu_b[m % 2][:, k, :], kq_s[:, k, :], start=(k == 0), stop=(k == 15))
                    ctx.inc(mm, "gu_t", 1)

                @blk.scalar
                def _(sc, m=m):
                    ctx.wait(sc, "gu_t")
                    sc.activation(gt_s[:], pb[2][:, :], AF.Silu, scale=mcol[:, 0:1])
                    ctx.inc(sc.activation(ut_s[:], pb[3][:, :], AF.Copy, scale=mcol[:, 1:2]), "gu_s", 1)

                @blk.vector
                def _(v, m=m):
                    ctx.wait(v, "gu_s")
                    ctx.inc(v.tensor_tensor(h_s[:, m, :], gt_s[:], ut_s[:], ALU.mult), "h_v", 1)

            @blk.vector
            def _(v):
                ctx.wait(v, "h_v")
                ctx.inc(v.memset(sqf[:, 0, 0:1], 0.0), "q2_pre0", 1)
            emit_norm_amax("q2", 44, h_s, True)
            emit_quant("q2", 44, h_s, h_s)

            for m in range(16):
                @blk.sync
                def _(s, m=m):
                    ctx.wait(s, "q2_q")
                    if m >= 2:
                        ctx.wait(s, "dn_t")
                    ctx.inc(s.dma_start(out=wd_b[m % 2][:], in_=t3T[m]), "dn_ld", 16)

                @blk.tensor
                def _(t, m=m):
                    ctx.wait(t, "dn_ld")
                    if m >= 2:
                        ctx.wait(t, "dn_v")
                    for k in range(44):
                        mm = t.matmul(pb[m % 2][:, :], wd_b[m % 2][:, k, :], h_s[:, k, :],
                                      start=(k == 0), stop=(k == 43))
                    ctx.inc(mm, "dn_t", 1)

                @blk.vector
                def _(v, m=m):
                    ctx.wait(v, "dn_t")
                    ctx.inc(v.scalar_tensor_tensor(x_s[:, m, :], pb[m % 2][:, :], mcol[:, 2:3],
                                                   x_s[:, m, :], ALU.mult, ALU.add), "dn_v", 1)

            @blk.sync
            def _(s, st=st):
                ctx.wait(s, "dn_v")
                ctx.inc(s.dma_start(out=x4_o[st], in_=x_s[:]), "out_st", 16)

        @blk.gpsimd
        def _(g):
            ctx.wait(g, "out_st")
    return nc


def build_l3l4():
    TOK = 1024
    ST = 512
    NST = 2
    SFULL = 4096
    NKT = SFULL // 128
    LN8 = math.log(8.0)
    NELEM = float(INTER * D)
    nc = bass.Bass("TRN2", target_bir_lowering=False)
    ln2 = nc.dram_tensor("ln2", [8, 128, 16, 512], BF16, kind="ExternalInput").ap()
    x2 = nc.dram_tensor("x2", [NST, 128, 16, ST], F32, kind="ExternalInput").ap()
    wkT = nc.dram_tensor("wkT", [2, 128, 16, 1024], F32, kind="ExternalInput").ap()
    wvT = nc.dram_tensor("wvT", [2, 128, 16, 1024], F32, kind="ExternalInput").ap()
    wqT = nc.dram_tensor("wqT", [2, 128, 16, 1024], F32, kind="ExternalInput").ap()
    lq1 = nc.dram_tensor("lq1", [1, DH1], F32, kind="ExternalInput").ap()
    lk1 = nc.dram_tensor("lk1", [1, DH1], F32, kind="ExternalInput").ap()
    lq2 = nc.dram_tensor("lq2", [1, DH1], F32, kind="ExternalInput").ap()
    lk2 = nc.dram_tensor("lk2", [1, DH1], F32, kind="ExternalInput").ap()
    gnw = nc.dram_tensor("gnw", [128, 16, 128], F32, kind="ExternalInput").ap()
    gnb = nc.dram_tensor("gnb", [128, 16, 128], F32, kind="ExternalInput").ap()
    wo = nc.dram_tensor("wo", [16, 128, 16, 128], F32, kind="ExternalInput").ap()
    t1T = nc.dram_tensor("t1T", [44, 128, 16, 128], BF16, kind="ExternalInput").ap()
    t2T = nc.dram_tensor("t2T", [44, 128, 16, 128], BF16, kind="ExternalInput").ap()
    t3T = nc.dram_tensor("t3T", [16, 128, 44, 128], BF16, kind="ExternalInput").ap()
    wsums = nc.dram_tensor("wsums", [8, 3], F32, kind="ExternalInput").ap()
    allones = nc.dram_tensor("allones", [128, 128], F32, kind="ExternalInput").ap()
    ident = nc.dram_tensor("ident", [128, 128], F32, kind="ExternalInput").ap()
    x4_o = nc.dram_tensor("x4", [NST, 128, 16, ST], F32, kind="ExternalOutput").ap()
    kf_d = nc.dram_tensor("kf_d", [D, SFULL], BF16, kind="Internal").ap()
    v_qd = nc.dram_tensor("v_qd", [4, 128, 32, 512], BF16, kind="Internal").ap()
    qf_d = nc.dram_tensor("qf_d", [D, TOK], BF16, kind="Internal").ap()
    att_t = nc.dram_tensor("att_t", [NST, 128, 16, ST], BF16, kind="Internal").ap()

    es = contextlib.ExitStack()
    with es:
        ctx = Ctx(nc, es)
        ab = ctx.sb("ab", [128, 32768], BF16)
        cw = ctx.sb("cw", [128, 19456], BF16)
        att_s = ctx.sb("att_s", [128, 16, 512], BF16)
        x_s = ctx.sb("x_s", [128, 16, ST], F32)
        wph = [ab[:, 0:16384].rearrange("p (c q) -> p c q", c=16),
               cw[:, 0:16384].rearrange("p (c q) -> p c q", c=16)]
        xt_b = [ab[:, 16384:24576].rearrange("p (c q) -> p c q", c=16),
                att_s[:, :, :]]
        kf_h = ab[:, 0:8192].rearrange("p (b s) -> p b s", b=2)
        qf_h = ab[:, 8192:10240].rearrange("p (b s) -> p b s", b=2)
        vt_quad = ab[:, 10240:26624].rearrange("p (k f) -> p k f", k=NKT)
        h_s = ab[:, 0:22528].rearrange("p (c q) -> p c q", c=44)
        kq_s = ab[:, 22528:30720].rearrange("p (c q) -> p c q", c=16)
        wt_b = [cw[:, 0:2048].rearrange("p (c q) -> p c q", c=16),
                cw[:, 2048:4096].rearrange("p (c q) -> p c q", c=16)]
        wu_b = [cw[:, 4096:6144].rearrange("p (c q) -> p c q", c=16),
                cw[:, 6144:8192].rearrange("p (c q) -> p c q", c=16)]
        wd_b = [cw[:, 8192:13824].rearrange("p (c q) -> p c q", c=44),
                cw[:, 13824:19456].rearrange("p (c q) -> p c q", c=44)]
        wo_b = [ctx.sb(f"wo_b{i}", [128, 16, 128], BF16) for i in range(2)]
        kout = ctx.sb("kout", [128, 2, 512], BF16)
        vout = ctx.sb("vout", [128, 2, 1024], BF16)
        gnw_s = ctx.sb("gnw_s", [128, 16, 128], BF16)
        gnb_s = ctx.sb("gnb_s", [128, 16, 128], BF16)
        lt = ctx.sb("lt", [1, 4 * DH1], F32)
        nrm = ctx.sb("nrm", [1, 64], F32)
        lamv = ctx.sb("lamv", [128, 1], F32)
        negc = ctx.sb("negc", [128, 16], F32)
        maxk = ctx.sb("maxk", [128, 16], F32)
        maxq = ctx.sb("maxq", [128, 16], F32)
        redt = ctx.sb("redt", [128, 2], F32)
        p_sb = ctx.sb("p_sb", [128, 2, 1024], BF16)
        attc = ctx.sb("attc", [128, 4, 128], BF16)
        att1 = ctx.sb("att1", [128, 4, 128], F32)
        stat = ctx.sb("stat", [128, 16], F32)
        st2 = ctx.sb("st2", [128, 16], F32)
        rsum = ctx.sb("rsum", [1, 1024], F32)
        xa_s = ctx.sb("xa_s", [128, ST], BF16)
        gt_s = ctx.sb("gt_s", [128, ST], BF16)
        ut_s = ctx.sb("ut_s", [128, ST], BF16)
        sdump = xa_s[:, :]
        atstg = gt_s[:].rearrange("p (a q) -> p a q", a=4)
        id_b = ctx.sb("id_b", [128, 128], BF16)
        idf_s = ctx.sb("idf_s", [128, 128], F32)
        aof = ctx.sb("aof", [128, 128], F32)
        ao_b = ctx.sb("ao_b", [128, 128], BF16)
        ws_s = ctx.sb("ws_s", [8, 3], F32)
        mrow = ctx.sb("mrow", [1, 16], F32)
        mcol = ctx.sb("mcol", [128, 3], F32)
        sqf = ctx.sb("sqf", [128, 4, ST], F32)
        rstd_t = ctx.sb("rstd_t", [128, ST], F32)
        att2 = rstd_t[:].rearrange("p (a q) -> p a q", a=4)
        s127_t = sqf[:, 0, :]
        dq_t = sqf[:, 1, :]
        cbf = sqf[:, 2, :]
        amax = ctx.sb("amax", [128, 4], F32)
        tmax = ctx.sb("tmax", [128, 4], F32)
        rowa = ctx.sb("rowa", [1, ST], F32)
        t1f = ctx.sb("t1f", [128, 1, ST], F32)
        kib = ctx.sb("kib", [128, 1, ST], I32)
        kfb = ctx.sb("kfb", [128, 1, ST], F32)
        psc_ev = ctx.ps("psc_ev", [128, 1024])
        psc_od = ctx.ps("psc_od", [128, 1024])
        pv4 = ctx.ps("pv4", [128, 2048])
        pb0 = psc_ev[:, 0:512]
        pb1 = psc_ev[:, 512:1024]
        l4pb = [pb0, pb1,
                pv4[:, 0:512], pv4[:, 512:1024],
                pv4[:, 1024:1536], pv4[:, 1536:2048]]

        def pso(sl, a):
            off = (sl * 2 + a) * 128
            return pv4[:, off:off + 128]
        prow = pv4[0:1, 1024:2048]

        blk = es.enter_context(nc.Block())

        @blk.gpsimd
        def _(g):
            for dst, src_ in [(aof, allones), (ao_b, allones), (id_b, ident),
                              (idf_s, ident), (ws_s, wsums),
                              (gnw_s, gnw), (gnb_s, gnb)]:
                ctx.inc(g.dma_start(out=dst[:], in_=src_), "c_ld", 16)
            for i, src_ in enumerate([lq1, lk1, lq2, lk2]):
                ctx.inc(g.dma_start(out=lt[0:1, i * DH1:(i + 1) * DH1], in_=src_), "c_ld", 16)

        @blk.vector
        def _(v):
            ctx.wait(v, "c_ld")
            v.tensor_tensor(lt[0:1, 0:DH1], lt[0:1, 0:DH1], lt[0:1, DH1:2 * DH1], ALU.mult)
            inst = v.tensor_tensor(lt[0:1, 2 * DH1:3 * DH1], lt[0:1, 2 * DH1:3 * DH1], lt[0:1, 3 * DH1:4 * DH1], ALU.mult)
            ctx.vsync(v, inst)
            v.tensor_reduce(nrm[0:1, 48:49], lt[0:1, 0:DH1], AX.X, ALU.add)
            inst = v.tensor_reduce(nrm[0:1, 49:50], lt[0:1, 2 * DH1:3 * DH1], AX.X, ALU.add)
            ctx.vsync(v, inst)
            v.tensor_scalar_mul(gnw_s[:], gnw_s[:], 1.0 - LAMBDA_INIT)
            v.tensor_scalar_mul(gnb_s[:], gnb_s[:], 1.0 - LAMBDA_INIT)
            v.memset(maxk[:], 0.0)
            ctx.inc(v.memset(maxq[:], 0.0), "l_v", 1)

        @blk.scalar
        def _(sc):
            ctx.wait(sc, "l_v")
            ctx.inc(sc.activation(nrm[0:1, 48:50], nrm[0:1, 48:50], AF.Exp), "l_s", 1)

        @blk.vector
        def _(v):
            ctx.wait(v, "l_s")
            inst = v.tensor_tensor(nrm[0:1, 50:51], nrm[0:1, 48:49], nrm[0:1, 49:50], ALU.subtract)
            ctx.vsync(v, inst)
            ctx.inc(v.tensor_scalar(nrm[0:1, 51:52], nrm[0:1, 50:51], LAMBDA_INIT, -1.0, ALU.add, ALU.mult), "l_v2", 1)

        @blk.tensor
        def _(t):
            ctx.wait(t, "l_v2")
            ctx.inc(t.matmul(pb0[:, 0:1], aof[0:1, :], nrm[0:1, 51:52], start=True, stop=True), "l_t", 1)

        @blk.vector
        def _(v):
            ctx.wait(v, "l_t")
            ctx.inc(v.tensor_copy(lamv[:], pb0[:, 0:1]), "l_v3", 1)

        @blk.tensor
        def _(t):
            ctx.wait(t, "l_v3")
            ctx.inc(t.matmul(pb1[0:1, 0:3], aof[0:8, 0:1], ws_s[:], start=True, stop=True), "i_t", 1)

        @blk.vector
        def _(v):
            ctx.wait(v, "i_t")
            ctx.inc(v.tensor_scalar_mul(mrow[0:1, 0:3], pb1[0:1, 0:3], 1.0 / NELEM), "i_v", 1)

        @blk.tensor
        def _(t):
            ctx.wait(t, "i_v")
            ctx.inc(t.matmul(pb1[:, 0:3], aof[0:1, :], mrow[0:1, 0:3], start=True, stop=True), "i_t2", 1)

        @blk.vector
        def _(v):
            ctx.wait(v, "i_v2" if False else "i_t2")
            ctx.inc(v.tensor_copy(mcol[:], pb1[:, 0:3]), "i_v2", 1)

        agd = {"wph": [0, 0], "xt": [0, 0], "ps": [0, 0, 0, 0],
               "psx": [0, 0], "kst": [0, 0], "vst": [0, 0]}

        def emit_proj(kind, W, nblocks):
            for half in range(2):
                @blk.gpsimd
                def _(g, half=half, W=W):
                    ctx.wait(g, "i_v2")
                    ctx.wait_val(g, "a_t", agd["wph"][half % 2])
                    ctx.inc(g.dma_start(
                        out=wph[half % 2][:, :, 0:1024], in_=W[half]),
                        "aw_ld", 16)

                for n in range(nblocks):
                    @blk.sync
                    def _(s, n=n):
                        ctx.wait(s, "i_v2")
                        ctx.wait_val(s, "a_t", agd["xt"][n % 2])
                        ctx.inc(s.dma_start(
                            out=xt_b[n % 2][:, :, :], in_=ln2[n]),
                            "xt_ld", 16)

                    if kind in ("k", "q"):
                        for ich in range(8):
                            @blk.tensor
                            def _(t, n=n, half=half, ich=ich):
                                ctx.wait(t, "aw_ld", "xt_ld")
                                if ctx.semv.get("a_t", 0) == 0:
                                    ctx.wait(t, "i_v2", "l_v3")
                                ctx.wait_val(t, "a_sc", agd["ps"][ich % 2])
                                ctx.wait_val(t, "a_mx", agd["psx"][ich % 2])
                                dstp = (pb0 if ich % 2 == 0 else pb1)
                                for kc in range(16):
                                    mm = t.matmul(dstp[:, :], wph[half % 2][:, kc, ich * 128:(ich + 1) * 128],
                                                  xt_b[n % 2][:, kc, :], start=(kc == 0), stop=(kc == 15))
                                ctx.inc(mm, "a_t", 1)

                            @blk.scalar
                            def _(sc, ich=ich):
                                ctx.wait(sc, "a_t")
                                ctx.wait_val(sc, "a_ev", agd["kst"][ich % 2])
                                ctx.inc(sc.activation(kout[:, ich % 2, :],
                                                      (pb0 if ich % 2 == 0 else pb1)[:, :], AF.Copy), "a_sc", 1)

                            @blk.vector
                            def _(v, half=half, ich=ich, kind=kind):
                                ctx.wait(v, "a_t")
                                col = half * 8 + ich
                                dst = maxk if kind == "k" else maxq
                                inst = v.tensor_reduce(redt[:, ich % 2:ich % 2 + 1],
                                                       (pb0 if ich % 2 == 0 else pb1)[:, :], AX.X, ALU.max,
                                                       apply_absolute_value=True)
                                ctx.vsync(v, inst)
                                ctx.inc(v.tensor_tensor(dst[:, col:col + 1], dst[:, col:col + 1],
                                                        redt[:, ich % 2:ich % 2 + 1], ALU.max), "a_mx", 1)

                            agd["ps"][ich % 2] = ctx.semv["a_sc"]
                            agd["psx"][ich % 2] = ctx.semv["a_mx"]

                            @blk.sync
                            def _(s, n=n, half=half, ich=ich, kind=kind):
                                ctx.wait(s, "a_sc")
                                dd = kf_d if kind == "k" else qf_d
                                ctx.inc(s.dma_start(
                                    out=dd[(half * 8 + ich) * 128:(half * 8 + ich + 1) * 128,
                                           n * 512:(n + 1) * 512],
                                    in_=kout[:, ich % 2, :]), "a_ev", 16)
                            agd["kst"][ich % 2] = ctx.semv["a_ev"]
                    else:
                        for ts in range(4):
                            @blk.tensor
                            def _(t, n=n, half=half, ts=ts):
                                ctx.wait(t, "aw_ld", "xt_ld")
                                ctx.wait_val(t, "a_sc", agd["ps"][2 + ts % 2])
                                for vh in range(2):
                                    dstp = pv4[:, (ts % 2) * 1024 + vh * 512:(ts % 2) * 1024 + (vh + 1) * 512]
                                    for kc in range(16):
                                        mm = t.matmul(dstp, xt_b[n % 2][:, kc, ts * 128:(ts + 1) * 128],
                                                      wph[half % 2][:, kc, vh * 512:(vh + 1) * 512],
                                                      start=(kc == 0), stop=(kc == 15))
                                ctx.inc(mm, "a_t", 1)

                            @blk.scalar
                            def _(sc, ts=ts):
                                ctx.wait(sc, "a_t")
                                ctx.wait_val(sc, "a_ev", agd["vst"][ts % 2])
                                ctx.inc(sc.activation(vout[:, ts % 2, :],
                                                      pv4[:, (ts % 2) * 1024:(ts % 2 + 1) * 1024], AF.Copy), "a_sc", 1)
                            agd["ps"][2 + ts % 2] = ctx.semv["a_sc"]

                            @blk.sync
                            def _(s, n=n, half=half, ts=ts):
                                ctx.wait(s, "a_sc")
                                for j in range(2):
                                    ctx.inc(s.dma_start(
                                        out=v_qd[half * 2 + j, :, n * 4 + ts, :],
                                        in_=vout[:, ts % 2, j * 512:(j + 1) * 512]), "a_ev", 16)
                            agd["vst"][ts % 2] = ctx.semv["a_ev"]
                    agd["xt"][n % 2] = ctx.semv["a_t"]
                agd["wph"][half % 2] = ctx.semv["a_t"]

        emit_proj("k", wkT, 8)
        emit_proj("v", wvT, 8)
        emit_proj("q", wqT, 2)

        @blk.tensor
        def _(t):
            ctx.wait(t, "a_mx", "a_t", "a_sc")
            for j in range(16):
                mm = t.transpose(pv4[0:1, j * 128:(j + 1) * 128], maxk[:, j:j + 1], idf_s[:])
            ctx.inc(mm, "n_t", 1)

        @blk.vector
        def _(v):
            ctx.wait(v, "n_t")
            for j in range(16):
                last = v.tensor_reduce(nrm[0:1, j:j + 1], pv4[0:1, j * 128:(j + 1) * 128], AX.X, ALU.max)
            ctx.inc(last, "n_v", 1)

        @blk.tensor
        def _(t):
            ctx.wait(t, "n_v")
            for j in range(16):
                mm = t.transpose(pv4[0:1, j * 128:(j + 1) * 128], maxq[:, j:j + 1], idf_s[:])
            ctx.inc(mm, "n_t2", 1)

        @blk.vector
        def _(v):
            ctx.wait(v, "n_t2")
            for j in range(16):
                last = v.tensor_reduce(nrm[0:1, 16 + j:17 + j], pv4[0:1, j * 128:(j + 1) * 128], AX.X, ALU.max)
            ctx.inc(last, "n_v2", 1)

        @blk.scalar
        def _(sc):
            ctx.wait(sc, "n_v2")
            ctx.inc(sc.activation(nrm[0:1, 0:32], nrm[0:1, 0:32], AF.Ln), "n_s", 1)

        @blk.vector
        def _(v):
            ctx.wait(v, "n_s")
            inst = v.tensor_tensor(nrm[0:1, 32:48], nrm[0:1, 0:16], nrm[0:1, 16:32], ALU.add)
            ctx.vsync(v, inst)
            ctx.inc(v.tensor_scalar(nrm[0:1, 32:48], nrm[0:1, 32:48], LN8 + math.log(64.0), -1.0, ALU.add, ALU.mult), "n_v3", 1)

        @blk.tensor
        def _(t):
            ctx.wait(t, "n_v3")
            ctx.inc(t.matmul(pb0[:, 0:16], aof[0:1, :], nrm[0:1, 32:48], start=True, stop=True), "n_t3", 1)

        @blk.vector
        def _(v):
            ctx.wait(v, "n_t3")
            ctx.inc(v.tensor_copy(negc[:], pb0[:, 0:16]), "n_v4", 1)

        for hh in range(16):
            buf = hh % 2

            @blk.gpsimd
            def _(g, hh=hh, buf=buf):
                ctx.wait(g, "n_v4", "a_ev")
                if hh >= 2:
                    ctx.wait_val(g, "g_t", 2 * (hh - 1))
                ctx.inc(g.dma_start(out=kf_h[:, buf, :], in_=kf_d[hh * 128:(hh + 1) * 128, :]), "kv_ld", 16)
                ctx.inc(g.dma_start(out=qf_h[:, buf, :], in_=qf_d[hh * 128:(hh + 1) * 128, :]), "kv_ld", 16)
                if hh % 4 == 0:
                    ctx.wait_val(g, "g_t", 2 * hh)
                    ctx.inc(g.dma_start(out=vt_quad[:, :, :], in_=v_qd[hh // 4]), "kv_ld", 16)

            for qb in range(2):
                sb0 = ctx.semv.get("b_t", 0)

                def emit_score(kt, hh=hh, buf=buf, qb=qb):
                    @blk.tensor
                    def _(t, kt=kt):
                        ctx.wait(t, "kv_ld")
                        if hh == 0 and qb == 0 and kt == 0:
                            ctx.wait(t, "n_v4")
                        if kt < 2:
                            if ctx.semv.get("g_v", 0) > 0:
                                ctx.wait(t, "g_v")
                        else:
                            ctx.wait(t, "b_s")
                        psc = psc_ev if kt % 2 == 0 else psc_od
                        ksl = slice(kt * 128, (kt + 1) * 128)
                        qsl = slice(qb * 512, (qb + 1) * 512)
                        t.matmul(psc[:, 0:512], kf_h[0:64, buf, ksl], qf_h[0:64, buf, qsl],
                                 start=True, stop=True, tile_position=(0, 0))
                        ctx.inc(t.matmul(psc[:, 512:1024], kf_h[64:128, buf, ksl], qf_h[64:128, buf, qsl],
                                         start=True, stop=True, tile_position=(64, 0)), "b_t", 1)

                emit_score(0)
                emit_score(1)
                for kt in range(NKT):
                    @blk.scalar
                    def _(sc, hh=hh, kt=kt, sb0=sb0):
                        ctx.wait_val(sc, "b_t", sb0 + kt + 1)
                        psc = psc_ev if kt % 2 == 0 else psc_od
                        ctx.inc(sc.activation(p_sb[:, kt % 2, :], psc[:, :], AF.Exp,
                                              scale=0.125, bias=negc[:, hh:hh + 1]), "b_s", 1)

                    @blk.tensor
                    def _(t, hh=hh, kt=kt):
                        ctx.wait(t, "b_s")
                        hq = hh % 4
                        for sl in range(4):
                            for a in range(2):
                                t.matmul(pso(sl, a), p_sb[:, kt % 2, a * 512 + sl * 128:a * 512 + (sl + 1) * 128],
                                         vt_quad[:, kt, hq * 128:(hq + 1) * 128],
                                         start=(kt == 0), stop=(kt == NKT - 1))
                        for a in range(2):
                            mm = t.matmul(pv4[0:1, 1024 + a * 512:1024 + (a + 1) * 512],
                                          ao_b[:, 0:1], p_sb[:, kt % 2, a * 512:(a + 1) * 512],
                                          start=(kt == 0), stop=(kt == NKT - 1))
                        if kt == NKT - 1:
                            ctx.inc(mm, "b_t2", 1)
                        else:
                            ctx.inc(mm, "b_av", 1)

                    if kt + 2 < NKT:
                        emit_score(kt + 2)

                @blk.vector
                def _(v):
                    ctx.wait(v, "b_t2")
                    inst = v.tensor_copy(rsum[0:1, :], prow)
                    ctx.vsync(v, inst)
                    ctx.inc(v.reciprocal(rsum[0:1, :], rsum[0:1, :]), "g_r", 1)

                @blk.tensor
                def _(t):
                    ctx.wait(t, "g_r")
                    for j in range(8):
                        mm = t.transpose(pb0[:, j:j + 1], rsum[0:1, j * 128:(j + 1) * 128],
                                         idf_s[0:1, 0:1])
                    ctx.inc(mm, "g_tt", 1)

                @blk.vector
                def _(v, hh=hh, qb=qb):
                    ctx.wait(v, "g_tt")
                    inst = v.tensor_copy(stat[:, 0:8], pb0[:, 0:8])
                    ctx.vsync(v, inst)
                    for sl in range(4):
                        v.tensor_scalar_mul(att1[:, sl, :], pso(sl, 0), stat[:, sl:sl + 1])
                        v.tensor_scalar_mul(att2[:, sl, :], pso(sl, 1), stat[:, 4 + sl:5 + sl])
                    for sl in range(4):
                        last = v.scalar_tensor_tensor(attc[:, sl, :], att2[:, sl, :], lamv[:, 0:1],
                                                      att1[:, sl, :], ALU.mult, ALU.add)
                    ctx.inc(last, "g_v", 1)

                @blk.vector
                def _(v):
                    ctx.wait(v, "g_v")
                    for sl in range(4):
                        last = v.tensor_reduce(stat[:, 8 + sl:9 + sl], attc[:, sl, :], AX.X, ALU.add)
                    ctx.inc(last, "g_v2", 1)

                @blk.scalar
                def _(sc):
                    ctx.wait(sc, "g_v2")
                    for sl in range(4):
                        last = sc.activation(sdump[:, sl * 128:(sl + 1) * 128], attc[:, sl, :], AF.Square,
                                             accum_out=stat[:, 12 + sl:13 + sl])
                    ctx.inc(last, "g_s", 1)

                @blk.vector
                def _(v):
                    ctx.wait(v, "g_s")
                    v.tensor_scalar_mul(st2[:, 0:4], stat[:, 8:12], 1.0 / 128.0)
                    inst = v.tensor_scalar(st2[:, 4:8], stat[:, 12:16], 1.0 / 128.0, 1e-5, ALU.mult, ALU.add)
                    ctx.vsync(v, inst)
                    inst = v.tensor_tensor(st2[:, 8:12], st2[:, 0:4], st2[:, 0:4], ALU.mult)
                    ctx.vsync(v, inst)
                    ctx.inc(v.tensor_tensor(st2[:, 4:8], st2[:, 4:8], st2[:, 8:12], ALU.subtract), "g_v3", 1)

                @blk.scalar
                def _(sc):
                    ctx.wait(sc, "g_v3")
                    sc.activation(st2[:, 4:8], st2[:, 4:8], AF.Ln)
                    ctx.inc(sc.activation(st2[:, 4:8], st2[:, 4:8], AF.Exp, scale=-0.5), "g_s2", 1)

                @blk.vector
                def _(v, hh=hh):
                    ctx.wait(v, "g_s2")
                    for sl in range(4):
                        v.tensor_single_scalar(attc[:, sl, :], attc[:, sl, :], st2[:, sl:sl + 1], ALU.subtract)
                    for sl in range(4):
                        v.tensor_scalar_mul(attc[:, sl, :], attc[:, sl, :], st2[:, 4 + sl:5 + sl])
                    for sl in range(4):
                        v.tensor_tensor(att1[:, sl, :], attc[:, sl, :], gnw_s[:, hh, :], ALU.mult)
                    for sl in range(4):
                        last = v.tensor_tensor(att1[:, sl, :], att1[:, sl, :], gnb_s[:, hh, :], ALU.add)
                    ctx.inc(last, "g_v4", 1)

                @blk.tensor
                def _(t, hh=hh, qb=qb):
                    ctx.wait(t, "g_v4")
                    if ctx.semv.get("g_v5", 0) > 0:
                        ctx.wait(t, "g_v5")
                    for sl in range(4):
                        mm = t.transpose(pv4[:, 1024 + sl * 128:1024 + (sl + 1) * 128],
                                         att1[:, sl, :], idf_s[:])
                    ctx.inc(mm, "g_t", 1)

                @blk.vector
                def _(v):
                    ctx.wait(v, "g_t")
                    if ctx.semv.get("at_st", 0) > 0:
                        ctx.wait(v, "at_st")
                    ctx.inc(v.tensor_copy(
                        atstg[:], pv4[:, 1024:1536].rearrange("p (a q) -> p a q", a=4)), "g_v5", 1)

                @blk.sync
                def _(s, hh=hh, qb=qb):
                    ctx.wait(s, "g_v5")
                    ctx.inc(s.dma_start(
                        out=att_t[qb, :, hh, :],
                        in_=atstg[:].rearrange("p a q -> p (a q)")), "at_st", 16)

        pb = l4pb

        def emit_quant(tag, nch, src_s, dst_s):
            for kk in range(nch):
                @blk.vector
                def _(v, kk=kk, tag=tag):
                    ctx.wait(v, f"{tag}_pre")
                    v.tensor_tensor(t1f[:, 0, :], src_s[:, kk, :], s127_t[:], ALU.mult)
                    v.tensor_copy(kib[:, 0, :], t1f[:, 0, :])
                    v.tensor_copy(kfb[:, 0, :], kib[:, 0, :])
                    v.tensor_tensor(t1f[:, 0, :], t1f[:, 0, :], kfb[:, 0, :], ALU.subtract)
                    v.scalar_tensor_tensor(cbf[:], t1f[:, 0, :], 0.5, kfb[:, 0, :], ALU.is_gt, ALU.add)
                    v.scalar_tensor_tensor(kfb[:, 0, :], t1f[:, 0, :], -0.5, cbf[:], ALU.is_lt, ALU.subtract)
                    last = v.tensor_tensor(dst_s[:, kk, :], kfb[:, 0, :], dq_t[:], ALU.mult)
                    ctx.inc(last, f"{tag}_q", 1)

        def emit_norm_amax(tag, nch, src_s):
            for grp in range((nch + 3) // 4):
                @blk.scalar
                def _(sc, grp=grp, tag=tag, nch=nch):
                    ctx.wait(sc, f"{tag}_pre0")
                    if grp > 0:
                        ctx.wait(sc, f"{tag}_t")
                    ids = [grp * 4 + i for i in range(4) if grp * 4 + i < nch]
                    for i, kk in enumerate(ids):
                        last = sc.activation(sqf[:, i, :], src_s[:, kk, :], AF.Square)
                    ctx.inc(last, f"{tag}_s", 1)

                @blk.tensor
                def _(t, grp=grp, tag=tag, nch=nch):
                    ctx.wait(t, f"{tag}_s")
                    ids = [grp * 4 + i for i in range(4) if grp * 4 + i < nch]
                    for i, kk in enumerate(ids):
                        mm = t.matmul(pb[5], aof[:], sqf[:, i, :], start=(kk == 0), stop=(kk == nch - 1))
                    ctx.inc(mm, f"{tag}_t", 1)

            @blk.vector
            def _(v, tag=tag, nch=nch):
                ctx.wait(v, f"{tag}_t")
                ctx.inc(v.tensor_scalar(rstd_t[:], pb[5], 1.0 / (nch * 128.0), 1e-6, ALU.mult, ALU.add), f"{tag}_v", 1)

            @blk.scalar
            def _(sc, tag=tag):
                ctx.wait(sc, f"{tag}_v")
                sc.activation(rstd_t[:], rstd_t[:], AF.Ln)
                ctx.inc(sc.activation(rstd_t[:], rstd_t[:], AF.Exp, scale=-0.5), f"{tag}_s2", 1)

            @blk.vector
            def _(v, tag=tag):
                ctx.wait(v, f"{tag}_s2")
                ctx.inc(v.memset(amax[:], 0.0), f"{tag}_v2", 1)

            for kk in range(nch):
                @blk.scalar
                def _(sc, kk=kk, tag=tag):
                    ctx.wait(sc, f"{tag}_v2")
                    if kk > 0:
                        ctx.wait(sc, f"{tag}_tt")
                    ctx.inc(sc.activation(sqf[:, 3, :], src_s[:, kk, :], AF.Abs), f"{tag}_as", 1)

                @blk.tensor
                def _(t, kk=kk, tag=tag):
                    ctx.wait(t, f"{tag}_as")
                    if kk > 0:
                        ctx.wait(t, f"{tag}_av")
                    for ts in range(4):
                        mm = t.transpose(psc_od[:, ts * 128:(ts + 1) * 128],
                                         sqf[:, 3, ts * 128:(ts + 1) * 128], idf_s[:])
                    ctx.inc(mm, f"{tag}_tt", 1)

                @blk.vector
                def _(v, kk=kk, tag=tag):
                    ctx.wait(v, f"{tag}_tt")
                    for ts in range(4):
                        v.tensor_reduce(tmax[:, ts:ts + 1], psc_od[:, ts * 128:(ts + 1) * 128], AX.X, ALU.max)
                    for ts in range(4):
                        last = v.tensor_tensor(amax[:, ts:ts + 1], amax[:, ts:ts + 1], tmax[:, ts:ts + 1], ALU.max)
                    ctx.inc(last, f"{tag}_av", 1)

            @blk.tensor
            def _(t, tag=tag):
                ctx.wait(t, f"{tag}_av")
                for ts in range(4):
                    mm = t.transpose(pb[4][0:1, ts * 128:(ts + 1) * 128], amax[:, ts:ts + 1], idf_s[:])
                ctx.inc(mm, f"{tag}_t2", 1)

            @blk.vector
            def _(v, tag=tag):
                ctx.wait(v, f"{tag}_t2")
                inst = v.tensor_scalar_max(rowa[0:1, :], pb[4][0:1, 0:512], 1e-5)
                ctx.vsync(v, inst)
                ctx.inc(v.reciprocal(rowa[0:1, :], rowa[0:1, :]), f"{tag}_v3", 1)

            @blk.tensor
            def _(t, tag=tag):
                ctx.wait(t, f"{tag}_v3")
                ctx.inc(t.matmul(pb[4], aof[0:1, :], rowa[0:1, :], start=True, stop=True), f"{tag}_t3", 1)

            @blk.vector
            def _(v, tag=tag):
                ctx.wait(v, f"{tag}_t3")
                inst = v.tensor_scalar_mul(s127_t[:], pb[4], 127.0)
                ctx.vsync(v, inst)
                inst = v.reciprocal(t1f[:, 0, :], s127_t[:])
                ctx.vsync(v, inst)
                ctx.inc(v.tensor_tensor(dq_t[:], rstd_t[:], t1f[:, 0, :], ALU.mult), f"{tag}_pre", 1)

        for st in range(NST):
            t0 = st * ST

            @blk.gpsimd
            def _(g, t0=t0, st=st):
                ctx.wait(g, "at_st", "i_v2")
                if st > 0:
                    ctx.wait(g, "out_st")
                ctx.inc(g.dma_start(out=att_s[:], in_=att_t[st]), "in_ld", 16)
                ctx.inc(g.dma_start(out=x_s[:], in_=x2[st]), "in_ld", 16)

            for m in range(16):
                @blk.gpsimd
                def _(g, m=m):
                    ctx.wait(g, "in_ld")
                    ctx.wait_val(g, "woq_t", ctx.semv.get("woq_t", 0) - 1)
                    ctx.inc(g.dma_start(out=wo_b[m % 2][:], in_=wo[m]), "woq_ld", 16)

                @blk.tensor
                def _(t, m=m):
                    ctx.wait(t, "woq_ld")
                    ctx.wait_val(t, "woq_v", ctx.semv.get("woq_v", 0) - 1)
                    for k in range(16):
                        mm = t.matmul(pb[m % 2], wo_b[m % 2][:, k, :], att_s[:, k, :],
                                      start=(k == 0), stop=(k == 15))
                    ctx.inc(mm, "woq_t", 1)

                @blk.vector
                def _(v, m=m):
                    ctx.wait(v, "woq_t")
                    ctx.inc(v.tensor_tensor(x_s[:, m, :], x_s[:, m, :], pb[m % 2], ALU.add), "woq_v", 1)

            @blk.vector
            def _(v):
                ctx.wait(v, "woq_v")
                ctx.inc(v.memset(sqf[:, 0, 0:1], 0.0), "q1_pre0", 1)
            emit_norm_amax("q1", 16, x_s)
            emit_quant("q1", 16, x_s, kq_s)

            for m in range(44):
                @blk.sync
                def _(s, m=m):
                    ctx.wait(s, "q1_q")
                    ctx.wait_val(s, "gu_t", ctx.semv.get("gu_t", 0) - 1)
                    ctx.inc(s.dma_start(out=wt_b[m % 2][:], in_=t1T[m]), "gu_ld", 16)
                    ctx.inc(s.dma_start(out=wu_b[m % 2][:], in_=t2T[m]), "gu_ld", 16)

                @blk.tensor
                def _(t, m=m):
                    ctx.wait(t, "gu_ld")
                    ctx.wait_val(t, "h_v", ctx.semv.get("h_v", 0) - 1)
                    pg, pu = pb[2 + 2 * (m % 2)], pb[3 + 2 * (m % 2)]
                    for k in range(16):
                        t.matmul(pg, wt_b[m % 2][:, k, :], kq_s[:, k, :], start=(k == 0), stop=(k == 15))
                    for k in range(16):
                        mm = t.matmul(pu, wu_b[m % 2][:, k, :], kq_s[:, k, :], start=(k == 0), stop=(k == 15))
                    ctx.inc(mm, "gu_t", 1)

                @blk.scalar
                def _(sc, m=m):
                    ctx.wait(sc, "gu_t")
                    ctx.wait(sc, "h_v")
                    ctx.inc(sc.activation(gt_s[:], pb[2 + 2 * (m % 2)], AF.Silu, scale=mcol[:, 0:1]), "gu_s", 1)

                @blk.vector
                def _(v, m=m):
                    ctx.wait(v, "gu_s")
                    v.tensor_scalar_mul(ut_s[:], pb[3 + 2 * (m % 2)], mcol[:, 1:2])
                    ctx.inc(v.tensor_tensor(h_s[:, m, :], gt_s[:], ut_s[:], ALU.mult), "h_v", 1)

            @blk.vector
            def _(v):
                ctx.wait(v, "h_v")
                ctx.inc(v.memset(sqf[:, 0, 0:1], 0.0), "q2_pre0", 1)
            emit_norm_amax("q2", 44, h_s)
            emit_quant("q2", 44, h_s, h_s)

            for m in range(16):
                @blk.sync
                def _(s, m=m):
                    ctx.wait(s, "q2_q")
                    ctx.wait_val(s, "dn_t", ctx.semv.get("dn_t", 0) - 1)
                    ctx.inc(s.dma_start(out=wd_b[m % 2][:], in_=t3T[m]), "dn_ld", 16)

                @blk.tensor
                def _(t, m=m):
                    ctx.wait(t, "dn_ld")
                    ctx.wait_val(t, "dn_v", ctx.semv.get("dn_v", 0) - 1)
                    for k in range(44):
                        mm = t.matmul(pb[m % 2], wd_b[m % 2][:, k, :], h_s[:, k, :],
                                      start=(k == 0), stop=(k == 43))
                    ctx.inc(mm, "dn_t", 1)

                @blk.vector
                def _(v, m=m):
                    ctx.wait(v, "dn_t")
                    ctx.inc(v.scalar_tensor_tensor(x_s[:, m, :], pb[m % 2], mcol[:, 2:3],
                                                   x_s[:, m, :], ALU.mult, ALU.add), "dn_v", 1)

            @blk.sync
            def _(s, st=st):
                ctx.wait(s, "dn_v")
                ctx.inc(s.dma_start(out=x4_o[st], in_=x_s[:]), "out_st", 16)

        @blk.gpsimd
        def _(g):
            ctx.wait(g, "out_st")
    return nc


def prep_l0attn(inputs):
    cst = _consts()
    x = np.asarray(inputs["L0_attn_hidden_states"])
    wqkv = np.asarray(inputs["L0_attn_W_qkv"])
    wz = np.asarray(inputs["L0_attn_W_z"])
    wb = np.asarray(inputs["L0_attn_W_b"])
    wa = np.asarray(inputs["L0_attn_W_a"])
    cw = np.asarray(inputs["L0_attn_conv_w"])
    nw = np.asarray(inputs["L0_attn_norm_w"])
    alog = np.asarray(inputs["L0_attn_A_log"])
    dtb = np.asarray(inputs["L0_attn_dt_bias"])
    in_maps = []
    for c in range(8):
        b, q = c // 4, c % 4
        qs = slice(4 * q * 128, (4 * q + 4) * 128)
        vs = slice(8 * q * 128, (8 * q + 8) * 128)
        hv = slice(8 * q, 8 * q + 8)
        chan = np.concatenate([
            np.arange(qs.start, qs.stop),
            2048 + np.arange(qs.start, qs.stop),
            4096 + np.arange(vs.start, vs.stop)])
        cwT = np.ascontiguousarray(
            cw[chan, :].reshape(16, 128, KTAP).transpose(1, 2, 0)).astype(np.float32)
        m = {
            "x": tile_fm(x[b].T.astype(np.float32), 16, 512),
            "wqT": tile_fm(wqkv[qs, :].T, 16, 512)[0],
            "wkT": tile_fm(wqkv[2048 + qs.start:2048 + qs.stop, :].T, 16, 512)[0],
            "wvT": tile_fm(wqkv[4096 + vs.start:4096 + vs.stop, :].T, 16, 1024)[0],
            "wzT": tile_fm(wz[vs, :].T, 16, 512),
            "wbaT": tile_fm(
                np.concatenate([wb[hv, :], wa[hv, :]], axis=0).T, 16, 16)[0],
            "cwT": cwT,
            "nwrep": np.tile(nw, 8)[None, :].astype(np.float32),
            "alog": np.ascontiguousarray(alog[hv])[None, :].astype(np.float32),
            "dtb": np.ascontiguousarray(dtb[hv])[None, :].astype(np.float32),
            "ident": cst["ident"], "allones": cst["allones"],
            "mus8": cst["mus8"], "mui8": cst["mui8"],
            "trilones": cst["trilones"], "onehot8": cst["onehot8"],
        }
        in_maps.append(m)
    return in_maps


def run_l0attn(inputs):
    nc = build_l0attn()
    in_maps = prep_l0attn(inputs)
    res = run_bass_kernel_spmd(nc, in_maps, core_ids=list(range(8)))
    og = np.zeros((B, S, VAL_DIM), BF)
    for c in range(8):
        b, q = c // 4, c % 4
        og[b, :, 8 * q * 128:(8 * q + 8) * 128] = res.results[c]["og"]
    return og


def prep_w(inputs):
    cst = _consts()
    w1 = np.asarray(inputs["L1_ffn_w1"])
    w2 = np.asarray(inputs["L1_ffn_w2"])
    w3 = np.asarray(inputs["L1_ffn_w3"])
    RW1, RW3 = INTER // 8, D // 8
    maps = []
    for c in range(8):
        maps.append({
            "w1": np.ascontiguousarray(w1[c * RW1:(c + 1) * RW1, :]),
            "w2": np.ascontiguousarray(w2[c * RW1:(c + 1) * RW1, :]),
            "w3": np.ascontiguousarray(w3[c * RW3:(c + 1) * RW3, :]),
            "allones": cst["allones"],
        })
    return maps


def run_pipeline(inputs):
    cst = _consts()
    ncw = build_wsum()
    mw = prep_w(inputs)
    rw = run_bass_kernel_spmd(ncw, mw, core_ids=list(range(8)))
    wsums = np.stack([rw.results[c]["wsums"][0] for c in range(8)])

    nct = build_wtern()
    for c in range(8):
        mw[c]["wsums"] = wsums
    rt = run_bass_kernel_spmd(nct, mw, core_ids=list(range(8)))
    t1 = np.concatenate([rt.results[c]["t1"] for c in range(8)], axis=0)
    t2 = np.concatenate([rt.results[c]["t2"] for c in range(8)], axis=0)
    t3 = np.concatenate([rt.results[c]["t3"] for c in range(8)], axis=0)

    nc1 = build_l0attn()
    m1 = prep_l0attn(inputs)
    r1 = run_bass_kernel_spmd(nc1, m1, core_ids=list(range(8)))
    og = [r1.results[c]["og"] for c in range(8)]

    x = np.asarray(inputs["L0_attn_hidden_states"])
    wout = np.asarray(inputs["L0_attn_W_out"])
    gw = np.asarray(inputs["L0_ffn_gate_w"]); uw = np.asarray(inputs["L0_ffn_up_w"])
    dw = np.asarray(inputs["L0_ffn_down_w"])
    gs = np.asarray(inputs["L0_ffn_gate_s"]); us = np.asarray(inputs["L0_ffn_up_s"])
    ds = np.asarray(inputs["L0_ffn_down_s"])
    nc2 = build_l2()
    m2 = []
    woT = tile_w(np.ascontiguousarray(wout.T), 32)
    gwT = tile_w(np.ascontiguousarray(gw.T), 16)
    uwT = tile_w(np.ascontiguousarray(uw.T), 16)
    dwT = tile_w(np.ascontiguousarray(dw.T), 44)
    gsc = np.ascontiguousarray(gs.reshape(44, 128).T).astype(np.float32)
    usc = np.ascontiguousarray(us.reshape(44, 128).T).astype(np.float32)
    dsc = np.ascontiguousarray(ds.reshape(16, 128).T).astype(np.float32)
    for c in range(8):
        b, q = c // 4, c % 4
        tsl = slice(q * 1024, (q + 1) * 1024)
        og_fm = np.concatenate([og[b * 4 + j][tsl, :].T for j in range(4)], axis=0)
        m2.append({
            "og": tile_fm(og_fm, 32, 512),
            "x0": tile_fm(x[b, tsl, :].T.astype(np.float32), 16, 512),
            "woT": woT, "gwT": gwT, "uwT": uwT, "dwT": dwT,
            "gsc": gsc, "usc": usc, "dsc": dsc, "allones": cst["allones"],
        })
    r2 = run_bass_kernel_spmd(nc2, m2, core_ids=list(range(8)))
    x2 = [r2.results[c]["x2"] for c in range(8)]
    ln2 = [r2.results[c]["ln2"] for c in range(8)]

    wq1 = np.asarray(inputs["L1_attn_Wq"]); wk1 = np.asarray(inputs["L1_attn_Wk"])
    wv1 = np.asarray(inputs["L1_attn_Wv"])
    gnw = np.asarray(inputs["L1_attn_gn_weight"]); gnb = np.asarray(inputs["L1_attn_gn_bias"])
    wo1 = np.asarray(inputs["L1_attn_Wo"])
    nc34 = build_l3l4()
    t1T = tile_w(np.ascontiguousarray(t1.T), 16)
    t2T = tile_w(np.ascontiguousarray(t2.T), 16)
    t3T = tile_w(np.ascontiguousarray(t3.T), 44)
    wqf = tile_fm(wq1.astype(np.float32), 16, 1024)
    wkf = tile_fm(wk1.astype(np.float32), 16, 1024)
    wvf = tile_fm(wv1.astype(np.float32), 16, 1024)
    wof = tile_w(wo1.astype(np.float32), 16)
    gnwb = np.ascontiguousarray(np.broadcast_to(
        gnw.reshape(16, 128)[None, :, :], (128, 16, 128))).astype(np.float32)
    gnbb = np.ascontiguousarray(np.broadcast_to(
        gnb.reshape(16, 128)[None, :, :], (128, 16, 128))).astype(np.float32)
    m34 = []
    for c in range(8):
        b, q = c // 4, c % 4
        order = [2 * q, 2 * q + 1] + [g for g in range(8) if g // 2 != q]
        ln2r = np.stack([ln2[b * 4 + g // 2][g % 2] for g in order])
        m34.append({
            "ln2": np.ascontiguousarray(ln2r),
            "x2": x2[c],
            "wkT": wkf, "wvT": wvf, "wqT": wqf,
            "lq1": inputs["L1_attn_lambda_q1"][None, :].astype(np.float32),
            "lk1": inputs["L1_attn_lambda_k1"][None, :].astype(np.float32),
            "lq2": inputs["L1_attn_lambda_q2"][None, :].astype(np.float32),
            "lk2": inputs["L1_attn_lambda_k2"][None, :].astype(np.float32),
            "gnw": gnwb, "gnb": gnbb,
            "wo": wof,
            "t1T": t1T, "t2T": t2T, "t3T": t3T,
            "wsums": wsums,
            "allones": cst["allones"], "ident": cst["ident"],
        })
    r34 = run_bass_kernel_spmd(nc34, m34, core_ids=list(range(8)))
    out = np.zeros((B, S, D), np.float32)
    for c in range(8):
        b, q = c // 4, c % 4
        x4 = r34.results[c]["x4"]
        for st in range(2):
            out[b, q * 1024 + st * 512:q * 1024 + (st + 1) * 512, :] = \
                x4[st].transpose(1, 0, 2).reshape(D, 512).T
    return out


def kernel(**inputs):
    return run_pipeline(inputs)


# version 07
# speedup vs baseline: 1.2541x; 1.0020x over previous
import contextlib
import math
import numpy as np
import ml_dtypes
import concourse.bass as bass
import concourse.mybir as mybir
from concourse.bass_utils import run_bass_kernel_spmd

F32 = mybir.dt.float32
BF16 = mybir.dt.bfloat16
I32 = mybir.dt.int32
AF = mybir.ActivationFunctionType
ALU = mybir.AluOpType
AX = mybir.AxisListType
BF = ml_dtypes.bfloat16

B, S, D = 2, 4096, 2048
NK, NV, DK, DV, KTAP = 16, 32, 128, 128, 4
KEY_DIM, VAL_DIM, CONV_DIM = NK * DK, NV * DV, 2 * NK * DK + NV * DV
INTER = 5632
H1, DH1 = 16, 64
LAMBDA_INIT = 0.8
C = 128


class Ctx:
    def __init__(self, nc, es):
        self.nc, self.es = nc, es
        self.sems, self.semv = {}, {}

    def sem(self, name):
        if name not in self.sems:
            self.sems[name] = self.es.enter_context(self.nc.semaphore(name))
            self.semv[name] = 0
        return self.sems[name]

    def inc(self, inst, name, n=1):
        s = self.sem(name)
        self.semv[name] += n
        inst.then_inc(s, n)

    def wait(self, eng, *names):
        for n in names:
            eng.wait_ge(self.sem(n), self.semv[n])

    def wait_val(self, eng, name, val):
        if val > 0:
            eng.wait_ge(self.sem(name), val)

    def vsync(self, v, inst):
        s = self.sem("vself")
        self.semv["vself"] += 1
        inst.then_inc(s, 1)
        v.wait_ge(s, self.semv["vself"])

    def sb(self, name, shape, dt):
        return self.es.enter_context(self.nc.sbuf_tensor(name, shape, dt))

    def ps(self, name, shape, dt=F32):
        return self.es.enter_context(self.nc.psum_tensor(name, shape, dt))


def tile_fm(a, C, TS):
    a = np.ascontiguousarray(a)
    T = a.shape[1]
    return np.ascontiguousarray(
        a.reshape(C, 128, T).transpose(1, 0, 2).reshape(128, C, T // TS, TS)
        .transpose(2, 0, 1, 3))


def tile_w(w, C):
    w = np.ascontiguousarray(w)
    M = w.shape[1]
    return np.ascontiguousarray(
        w.reshape(C, 128, M).transpose(1, 0, 2).reshape(128, C, M // 128, 128)
        .transpose(2, 0, 1, 3))


def _consts():
    i_ = np.arange(128)
    c = {}
    c["ident"] = np.eye(128, dtype=np.float32)
    c["allones"] = np.ones((128, 128), np.float32)
    up_s = (i_[:, None] < i_[None, :]).astype(np.float32)
    lo_s = (i_[None, :] < i_[:, None]).astype(np.float32)
    up_i = (i_[:, None] <= i_[None, :]).astype(np.float32)
    c["mus8"] = np.broadcast_to(up_s[:, None, :], (128, 8, 128)).copy()
    c["mls8"] = np.broadcast_to(lo_s[:, None, :], (128, 8, 128)).copy()
    c["mui8"] = np.broadcast_to(up_i[:, None, :], (128, 8, 128)).copy()
    c["trilones"] = up_i.copy()
    oh8 = np.zeros((128, 8, 8), np.float32)
    for j in range(8):
        oh8[:, j, j] = 1.0
    c["onehot8"] = oh8
    return c


def build_l0attn(dbg=False):
    NKH, NVH = 4, 8
    QC, VC = NKH * DK, NVH * DV
    SEG = 512
    NSEG = S // SEG
    CPS = SEG // C
    LN2I8 = -0.5 * math.log(128.0)

    nc = bass.Bass("TRN2", target_bir_lowering=False)
    x_in = nc.dram_tensor("x", [NSEG, 128, 16, SEG], F32, kind="ExternalInput").ap()
    wqT = nc.dram_tensor("wqT", [128, 16, QC], F32, kind="ExternalInput").ap()
    wkT = nc.dram_tensor("wkT", [128, 16, QC], F32, kind="ExternalInput").ap()
    wvT = nc.dram_tensor("wvT", [128, 16, VC], F32, kind="ExternalInput").ap()
    wzT = nc.dram_tensor("wzT", [2, 128, 16, 512], F32, kind="ExternalInput").ap()
    wbaT = nc.dram_tensor("wbaT", [128, 16, 16], F32, kind="ExternalInput").ap()
    cwT = nc.dram_tensor("cwT", [128, KTAP, 16], F32, kind="ExternalInput").ap()
    nwrep = nc.dram_tensor("nwrep", [1, VC], F32, kind="ExternalInput").ap()
    alog = nc.dram_tensor("alog", [1, NVH], F32, kind="ExternalInput").ap()
    dtb = nc.dram_tensor("dtb", [1, NVH], F32, kind="ExternalInput").ap()
    ident = nc.dram_tensor("ident", [128, 128], F32, kind="ExternalInput").ap()
    allones = nc.dram_tensor("allones", [128, 128], F32, kind="ExternalInput").ap()
    mus8 = nc.dram_tensor("mus8", [128, 8, 128], F32, kind="ExternalInput").ap()
    mui8 = nc.dram_tensor("mui8", [128, 8, 128], F32, kind="ExternalInput").ap()
    tril = nc.dram_tensor("trilones", [128, 128], F32, kind="ExternalInput").ap()
    oh8 = nc.dram_tensor("onehot8", [128, 8, 8], F32, kind="ExternalInput").ap()
    og_out = nc.dram_tensor("og", [S, VC], BF16, kind="ExternalOutput").ap()

    dbg_t = {}
    if dbg:
        for nm, shp, dt_ in [
            ("lnx", [128, 16, 512], BF16), ("qkv", [128, 16, 512], BF16),
            ("zt", [128, 4, 1024], BF16), ("gba", [128, 4, 16], F32),
            ("lrqk", [8, 512], F32), ("combo", [128, 32], F32),
            ("lgam", [128, 8], F32), ("glog", [128, 8], F32),
            ("negb", [128, 8], F32), ("e8a", [128, 8, 128], BF16),
            ("ap", [128, 8, 128], BF16), ("au", [128, 8, 128], BF16),
            ("m8", [128, 8, 128], BF16), ("kk8", [128, 8, 128], BF16),
            ("vtm", [128, 8, 128], BF16), ("ktm", [128, 4, 128], BF16),
            ("rhs", [128, 8, 128], BF16), ("uu", [128, 8, 128], BF16),
            ("u3", [128, 8, 128], BF16), ("osum", [128, 8], F32),
            ("osc", [128, 8], F32), ("ogt", [128, 8, 128], BF16),
            ("og1", [128, 1024], BF16), ("stf", [128, 8, 128], F32),
            ("erowm", [128, 8], F32), ("eu3", [128, 8], F32),
            ("nrhs", [128, 8], F32), ("rows8", [1, 2048], F32),
            ("lrtm", [128, 8], F32), ("lke", [128, 8], F32), ("lqe", [128, 8], F32),
        ]:
            dbg_t[nm] = nc.dram_tensor("dbg_" + nm, shp, dt_, kind="ExternalOutput").ap()

    es = contextlib.ExitStack()
    with es:
        ctx = Ctx(nc, es)

        def dump(blk_, name, tile, wait_name):
            if not dbg:
                return
            @blk_.sync
            def _(s):
                ctx.wait(s, wait_name)
                ctx.inc(s.dma_start(out=dbg_t[name], in_=tile), "dbg_st", 16)

        wq_s = ctx.sb("wq_s", [128, 16, QC], BF16)
        wk_s = ctx.sb("wk_s", [128, 16, QC], BF16)
        wv_s = ctx.sb("wv_s", [128, 16, VC], BF16)
        wzh = ctx.sb("wzh", [128, 16, 512], BF16)
        wba_s = ctx.sb("wba_s", [128, 16, 16], BF16)
        cw_s = ctx.sb("cw_s", [128, KTAP, 16], F32)
        cvg = ctx.sb("cvg", [128, 4, 4, 128], BF16)
        id_s = ctx.sb("id_s", [128, 128], BF16)
        idf_s = ctx.sb("idf_s", [128, 128], F32)
        ao_s = ctx.sb("ao_s", [128, 128], BF16)
        aof_s = ctx.sb("aof_s", [128, 128], F32)
        trif_s = ctx.sb("trif_s", [128, 128], F32)
        mus_s = ctx.sb("mus_s", [128, 8, 128], BF16)
        mui_s = ctx.sb("mui_s", [128, 8, 128], BF16)
        tri_s = ctx.sb("tri_s", [128, 128], BF16)
        oh8_s = ctx.sb("oh8_s", [128, 8, 8], BF16)
        alog_s = ctx.sb("alog_s", [1, NVH], F32)
        dtb_s = ctx.sb("dtb_s", [1, NVH], F32)
        nwr_s = ctx.sb("nwr_s", [1, VC], BF16)
        dtb_b = ctx.sb("dtb_b", [128, NVH], F32)
        nega_b = ctx.sb("nega_b", [128, NVH], F32)
        nw_b = ctx.sb("nw_b", [128, VC], BF16)
        xb = ctx.sb("xb", [128, 16, SEG], BF16)
        sqb = ctx.sb("sqb", [128, 4, SEG], BF16)
        mu_t = ctx.sb("mu_t", [128, SEG], F32)
        var_t = ctx.sb("var_t", [128, SEG], F32)
        mixr = ctx.sb("mixr", [128, 16, 3 + SEG], BF16)
        qkv = ctx.sb("qkv", [128, 16, SEG], BF16)
        zt = ctx.sb("zt", [128, CPS, VC], BF16)
        gba4 = ctx.sb("gba4", [128, CPS, 16], F32)
        lrqk = ctx.sb("lrqk", [8, SEG], F32)
        og1 = ctx.sb("og1", [128, VC], BF16)
        combo = ctx.sb("combo", [128, 32], F32)
        rows8 = ctx.sb("rows8", [1, 2048], F32)
        negb = ctx.sb("negb", [128, NVH], F32)
        glog = ctx.sb("glog", [128, NVH], F32)
        lgam = ctx.sb("lgam", [128, NVH], F32)
        e_u3 = ctx.sb("e_u3", [128, NVH], F32)
        erowM = ctx.sb("erowM", [128, NVH], F32)
        lgCb = ctx.sb("lgCb", [128, NVH], F32)
        nrhs = ctx.sb("nrhs", [128, NVH], F32)
        gamC = ctx.sb("gamC", [128, NVH], F32)
        osum = ctx.sb("osum", [128, NVH], F32)
        osc = ctx.sb("osc", [128, NVH], F32)
        t8a = ctx.sb("t8a", [128, NVH], F32)
        t8b = ctx.sb("t8b", [128, NVH], F32)
        lr_tm = ctx.sb("lr_tm", [128, 8], F32)
        lk_e = ctx.sb("lk_e", [128, NVH], F32)
        lq_e = ctx.sb("lq_e", [128, NVH], F32)
        kk8 = ctx.sb("kk8", [128, 8, 128], BF16)
        m8 = ctx.sb("m8", [128, 8, 128], BF16)
        ktm = ctx.sb("ktm", [128, NKH, 128], BF16)
        vtm = ctx.sb("vtm", [128, NVH, 128], BF16)
        e8a = ctx.sb("e8a", [128, 8, 128], BF16)
        ap_t = ctx.sb("ap_t", [128, 8, 128], BF16)
        au_t = ctx.sb("au_t", [128, 8, 128], BF16)
        pp_t = ctx.sb("pp_t", [128, 8, 128], BF16)
        pu_t = ctx.sb("pu_t", [128, 8, 128], BF16)
        u_t = ctx.sb("u_t", [128, 8, 128], BF16)
        u3_t = ctx.sb("u3_t", [128, 8, 128], BF16)
        st_hi = ctx.sb("st_hi", [128, 8, 128], BF16)
        st_lo = ctx.sb("st_lo", [128, 8, 128], BF16)
        st_f = ctx.sb("st_f", [128, 8, 128], F32)
        ogt = ctx.sb("ogt", [128, 8, 128], BF16)
        pb = [ctx.ps(f"pb{i}", [128, 512]) for i in range(6)]
        pvt = ctx.ps("pvt", [128, 16, 128], BF16)

        blk = es.enter_context(nc.Block())

        @blk.gpsimd
        def _(g):
            for dst, src_ in [
                (id_s, ident), (idf_s, ident), (ao_s, allones), (aof_s, allones),
                (mus_s, mus8),
                (mui_s, mui8), (tri_s, tril), (trif_s, tril), (oh8_s, oh8),
                (nwr_s, nwrep), (alog_s, alog), (dtb_s, dtb), (cw_s, cwT),
                (wq_s, wqT), (wk_s, wkT), (wv_s, wvT), (wba_s, wbaT),
            ]:
                ctx.inc(g.dma_start(out=dst[:], in_=src_), "c_ld", 16)

        @blk.scalar
        def _(sc):
            ctx.wait(sc, "c_ld")
            ctx.inc(sc.activation(alog_s[:], alog_s[:], AF.Exp), "i_s", 1)

        @blk.vector
        def _(v):
            ctx.wait(v, "i_s")
            v.tensor_scalar_mul(alog_s[:], alog_s[:], -1.0)
            v.memset(st_hi[:], 0.0)
            v.memset(st_lo[:], 0.0)
            v.memset(st_f[:], 0.0)
            ctx.inc(v.memset(mixr[:, :, 0:3], 0.0), "i_v", 1)

        @blk.tensor
        def _(t):
            ctx.wait(t, "i_v")
            t.matmul(pb[0][:, 0:NVH], aof_s[0:1, :], dtb_s[:], start=True, stop=True)
            t.matmul(pb[0][:, 16:16 + NVH], aof_s[0:1, :], alog_s[:], start=True, stop=True)
            t.matmul(pb[1][:, :], ao_s[0:1, :], nwr_s[:, 0:512], start=True, stop=True)
            ctx.inc(t.matmul(pb[2][:, :], ao_s[0:1, :], nwr_s[:, 512:1024], start=True, stop=True), "i_t", 1)

        @blk.vector
        def _(v):
            ctx.wait(v, "i_t")
            v.tensor_copy(dtb_b[:], pb[0][:, 0:NVH])
            v.tensor_copy(nega_b[:], pb[0][:, 16:16 + NVH])
            v.tensor_copy(nw_b[:, 0:512], pb[1][:, :])
            ctx.inc(v.tensor_copy(nw_b[:, 512:1024], pb[2][:, :]), "i_v2", 1)

        for seg in range(NSEG):
            t0 = seg * SEG

            @blk.gpsimd
            def _(g, t0=t0, seg=seg):
                ctx.wait(g, "i_v2")
                if seg > 0:
                    ctx.wait(g, "og_st")
                ctx.inc(g.dma_start(out=xb[:], in_=x_in[seg]), "x_ld", 16)

            for grp in range(4):
                @blk.scalar
                def _(sc, grp=grp):
                    if grp == 0:
                        ctx.wait(sc, "x_ld", "i_v2")
                    else:
                        ctx.wait(sc, "ln_t")
                    for i in range(4):
                        last = sc.activation(sqb[:, i, :], xb[:, grp * 4 + i, :], AF.Square)
                    ctx.inc(last, "ln_s", 1)

                @blk.tensor
                def _(t, grp=grp):
                    ctx.wait(t, "ln_s")
                    for i in range(4):
                        k = grp * 4 + i
                        t.matmul(pb[0][:, :], ao_s[:], xb[:, k, :], start=(k == 0), stop=(k == 15))
                        mm = t.matmul(pb[1][:, :], ao_s[:], sqb[:, i, :], start=(k == 0), stop=(k == 15))
                    ctx.inc(mm, "ln_t", 1)

            @blk.vector
            def _(v):
                ctx.wait(v, "ln_t")
                v.tensor_scalar_mul(mu_t[:], pb[0][:, :], 1.0 / D)
                v.tensor_scalar(var_t[:], pb[1][:, :], 1.0 / D, 1e-5, ALU.mult, ALU.add)
                inst = v.tensor_tensor(pb[0][:, :], mu_t[:], mu_t[:], ALU.mult)
                ctx.vsync(v, inst)
                ctx.inc(v.tensor_tensor(var_t[:], var_t[:], pb[0][:, :], ALU.subtract), "ph_v", 1)

            @blk.scalar
            def _(sc):
                ctx.wait(sc, "ph_v")
                ctx.inc(sc.activation(var_t[:], var_t[:], AF.Sqrt), "ph_s", 1)

            @blk.vector
            def _(v, seg=seg):
                ctx.wait(v, "ph_s")
                v.reciprocal(var_t[:], var_t[:])
                if seg > 0:
                    v.tensor_copy(mixr[:, :, 0:3], mixr[:, :, SEG:SEG + 3])
                for k in range(16):
                    v.tensor_tensor(xb[:, k, :], xb[:, k, :], mu_t[:], ALU.subtract)
                for k in range(16):
                    last = v.tensor_tensor(xb[:, k, :], xb[:, k, :], var_t[:], ALU.mult)
                ctx.inc(last, "ph_v2", 1)

            if dbg and seg == 0:
                dump(blk, "lnx", xb[:], "ph_v2")

            WSEL = [(wq_s, m, m) for m in range(4)] + \
                   [(wk_s, m, 4 + m) for m in range(4)] + \
                   [(wv_s, m, 8 + m) for m in range(8)]
            for grp in range(4):
                @blk.tensor
                def _(t, grp=grp):
                    ctx.wait(t, "ph_v2")
                    if grp > 0:
                        ctx.wait(t, "pj_v")
                    for i in range(4):
                        w, mb, dst = WSEL[grp * 4 + i]
                        for k in range(16):
                            mm = t.matmul(pb[i][:, :], w[:, k, mb * 128:(mb + 1) * 128],
                                          xb[:, k, :], start=(k == 0), stop=(k == 15))
                    ctx.inc(mm, "pj_t", 1)

                @blk.vector
                def _(v, grp=grp):
                    ctx.wait(v, "pj_t")
                    for i in range(4):
                        _, _, dst = WSEL[grp * 4 + i]
                        last = v.tensor_copy(mixr[:, dst, 3:3 + SEG], pb[i][:, :])
                    ctx.inc(last, "pj_v", 1)

            for grp in range(4):
                @blk.vector
                def _(v, grp=grp):
                    ctx.wait(v, "pj_v")
                    if grp > 0:
                        ctx.wait(v, "cv_t")
                    for i in range(4):
                        cc = grp * 4 + i
                        for j in range(KTAP):
                            last = v.tensor_scalar_mul(cvg[:, j, i, :], id_s[:], cw_s[:, j, cc:cc + 1])
                    ctx.inc(last, "cv_v", 1)

                @blk.tensor
                def _(t, grp=grp):
                    ctx.wait(t, "cv_v")
                    if grp > 0:
                        ctx.wait(t, "cv_s")
                    for i in range(4):
                        cc = grp * 4 + i
                        for j in range(KTAP):
                            mm = t.matmul(pb[i][:, :], cvg[:, j, i, :],
                                          mixr[:, cc, j:j + SEG], start=(j == 0), stop=(j == 3))
                    ctx.inc(mm, "cv_t", 1)

                @blk.scalar
                def _(sc, grp=grp):
                    ctx.wait(sc, "cv_t")
                    for i in range(4):
                        cc = grp * 4 + i
                        last = sc.activation(qkv[:, cc, :], pb[i][:, :], AF.Silu)
                    ctx.inc(last, "cv_s", 1)

            if dbg and seg == 0:
                dump(blk, "qkv", qkv[:], "cv_s")

            for half in range(2):
                @blk.gpsimd
                def _(g, half=half):
                    ctx.wait(g, "cv_s" if half == 0 else "z_t")
                    ctx.inc(g.dma_start(out=wzh[:], in_=wzT[half]), "wz_ld", 16)

                for tc in range(CPS):
                    @blk.tensor
                    def _(t, half=half, tc=tc):
                        ctx.wait(t, "wz_ld")
                        if not (half == 0 and tc == 0):
                            ctx.wait(t, "z_sv")
                        for k in range(16):
                            mm = t.matmul(pb[tc % 2][:, :], xb[:, k, tc * 128:(tc + 1) * 128],
                                          wzh[:, k, :], start=(k == 0), stop=(k == 15))
                        if half == 0:
                            for k in range(16):
                                mm = t.matmul(pb[2][:, 0:16], xb[:, k, tc * 128:(tc + 1) * 128],
                                              wba_s[:, k, :], start=(k == 0), stop=(k == 15))
                        ctx.inc(mm, "z_t", 1)

                    @blk.scalar
                    def _(sc, half=half, tc=tc):
                        ctx.wait(sc, "z_t")
                        ctx.inc(sc.activation(zt[:, tc, half * 512:(half + 1) * 512],
                                              pb[tc % 2][:, :], AF.Silu), "z_s", 1)

                    @blk.vector
                    def _(v, half=half, tc=tc):
                        ctx.wait(v, "z_s")
                        if half == 0:
                            v.tensor_copy(gba4[:, tc, :], pb[2][:, 0:16])
                        ctx.inc(v.tensor_tensor(zt[:, tc, half * 512:(half + 1) * 512],
                                                zt[:, tc, half * 512:(half + 1) * 512],
                                                nw_b[:, half * 512:(half + 1) * 512], ALU.mult), "z_sv", 1)

            if dbg and seg == 0:
                dump(blk, "zt", zt[:], "z_sv")
                dump(blk, "gba", gba4[:], "z_sv")

            for grp in range(2):
                @blk.scalar
                def _(sc, grp=grp):
                    ctx.wait(sc, "z_sv" if grp == 0 else "lr_t")
                    for i in range(4):
                        last = sc.activation(sqb[:, i, :], qkv[:, grp * 4 + i, :], AF.Square)
                    ctx.inc(last, "lr_s", 1)

                @blk.tensor
                def _(t, grp=grp):
                    ctx.wait(t, "lr_s")
                    for i in range(4):
                        j = grp * 4 + i
                        mm = t.matmul(pb[0][0:8, :], oh8_s[:, j, :], sqb[:, i, :],
                                      start=(j == 0), stop=(j == 7))
                    ctx.inc(mm, "lr_t", 1)

            @blk.vector
            def _(v):
                ctx.wait(v, "lr_t")
                ctx.inc(v.tensor_scalar_add(lrqk[:], pb[0][0:8, :], 1e-6), "lr_v", 1)

            @blk.scalar
            def _(sc):
                ctx.wait(sc, "lr_v")
                ctx.inc(sc.activation(lrqk[:], lrqk[:], AF.Ln), "lr_s2", 1)

            if dbg and seg == 0:
                dump(blk, "lrqk", lrqk[:], "lr_s2")

            for tc in range(CPS):
                @blk.scalar
                def _(sc, tc=tc, seg=seg):
                    ctx.wait(sc, "lr_s2")
                    if not (seg == 0 and tc == 0):
                        ctx.wait(sc, "og_v")
                    sc.activation(t8a[:], gba4[:, tc, 0:8], AF.Exp, scale=-1.0)
                    ctx.inc(sc.activation(negb[:], t8a[:], AF.Ln, bias=1.0), "c0_s", 1)

                @blk.vector
                def _(v, tc=tc):
                    ctx.wait(v, "c0_s")
                    ctx.inc(v.tensor_tensor(t8b[:], gba4[:, tc, 8:16], dtb_b[:], ALU.add), "c0_v", 1)

                @blk.scalar
                def _(sc):
                    ctx.wait(sc, "c0_v")
                    sc.activation(t8b[:], t8b[:], AF.Exp)
                    ctx.inc(sc.activation(t8b[:], t8b[:], AF.Ln, bias=1.0), "c0_s2", 1)

                @blk.vector
                def _(v):
                    ctx.wait(v, "c0_s2")
                    ctx.inc(v.tensor_tensor(glog[:], t8b[:], nega_b[:], ALU.mult), "c0_v2", 1)

                @blk.tensor
                def _(t, tc=tc):
                    ctx.wait(t, "c0_v2")
                    t.matmul(pb[0][:, 0:8], trif_s[:], glog[:], start=True, stop=True)
                    t.matmul(pb[2][:, 0:8], aof_s[:], glog[:], start=True, stop=True)
                    ctx.inc(t.transpose(pb[1][0:128, 0:8], lrqk[:, tc * 128:(tc + 1) * 128], idf_s[0:8, 0:8]), "c0_t", 1)

                @blk.vector
                def _(v, tc=tc):
                    ctx.wait(v, "c0_t")
                    v.tensor_copy(lr_tm[:], pb[1][0:128, 0:8])
                    v.tensor_copy(lgam[:], pb[0][:, 0:8])
                    v.tensor_copy(lgCb[:], pb[2][:, 0:8])
                    v.tensor_copy(lk_e[:, 0::2], lr_tm[:, 4:8])
                    v.tensor_copy(lk_e[:, 1::2], lr_tm[:, 4:8])
                    v.tensor_copy(lq_e[:, 0::2], lr_tm[:, 0:4])
                    v.tensor_copy(lq_e[:, 1::2], lr_tm[:, 0:4])
                    v.scalar_tensor_tensor(t8a[:], lk_e[:], -0.5, negb[:], ALU.mult, ALU.subtract)
                    v.scalar_tensor_tensor(combo[:, 16:24], lq_e[:], -0.5, lgam[:], ALU.mult, ALU.add)
                    v.scalar_tensor_tensor(combo[:, 8:16], lk_e[:], -0.5, lgam[:], ALU.mult, ALU.add)
                    v.tensor_tensor(combo[:, 0:8], t8a[:], lgam[:], ALU.subtract)
                    ctx.inc(v.tensor_scalar_add(combo[:, 16:24], combo[:, 16:24], LN2I8), "c1_v", 1)

                if dbg and seg == 0 and tc == 0:
                    dump(blk, "lrtm", lr_tm[:], "c1_v")
                    dump(blk, "lke", lk_e[:], "c1_v")
                    dump(blk, "lqe", lq_e[:], "c1_v")
                    dump(blk, "combo", combo[:], "c1_v")
                    dump(blk, "lgam", lgam[:], "c1_v")
                    dump(blk, "glog", glog[:], "c1_v")
                    dump(blk, "negb", negb[:], "c1_v")

                @blk.vector
                def _(v):
                    ctx.wait(v, "c1_v")
                    ctx.inc(v.tensor_tensor(e_u3[:], combo[:, 0:8], lgCb[:], ALU.add), "c1_v2", 1)

                @blk.scalar
                def _(sc):
                    ctx.wait(sc, "c1_v2")
                    sc.activation(e_u3[:], e_u3[:], AF.Exp)
                    sc.activation(gamC[:], lgCb[:], AF.Exp)
                    sc.activation(erowM[:], combo[:, 16:24], AF.Exp)
                    ctx.inc(sc.activation(nrhs[:], combo[:, 8:16], AF.Exp), "c1_s", 1)

                @blk.tensor
                def _(t):
                    ctx.wait(t, "c1_s")
                    for h in range(16):
                        bank = pb[2 + h // 4]
                        mm = t.transpose(bank[0:1, (h % 4) * 128:(h % 4 + 1) * 128],
                                         combo[:, 8 + h:9 + h], idf_s[:])
                    ctx.inc(mm, "c1_t", 1)

                @blk.vector
                def _(v):
                    ctx.wait(v, "c1_t")
                    for h in range(16):
                        bank = pb[2 + h // 4]
                        v.tensor_copy(rows8[0:1, h * 128:(h + 1) * 128],
                                      bank[0:1, (h % 4) * 128:(h % 4 + 1) * 128])
                    ctx.inc(v.tensor_scalar_mul(nrhs[:], nrhs[:], -1.0), "c2_s", 1)

                if dbg and seg == 0 and tc == 0:
                    dump(blk, "nrhs", nrhs[:], "c2_s")
                    dump(blk, "erowm", erowM[:], "c2_s")
                    dump(blk, "eu3", e_u3[:], "c2_s")
                    dump(blk, "rows8", rows8[:], "c2_s")

                @blk.tensor
                def _(t, tc=tc):
                    ctx.wait(t, "c2_s")
                    sl = slice(tc * 128, (tc + 1) * 128)
                    for kh in range(4):
                        t.matmul(pb[0][:, kh * 128:(kh + 1) * 128], qkv[:, 4 + kh, sl], qkv[:, 4 + kh, sl], start=True, stop=True)
                        t.matmul(pb[1][:, kh * 128:(kh + 1) * 128], qkv[:, 4 + kh, sl], qkv[:, kh, sl], start=True, stop=True)
                        t.transpose(pvt[:, kh, :], qkv[:, 4 + kh, sl], id_s[:])
                    for h in range(8):
                        t.transpose(pvt[:, 4 + h, :], qkv[:, 8 + h, sl], id_s[:])
                    for h in range(16):
                        bank = pb[2 + h // 4]
                        mm = t.matmul(bank[:, (h % 4) * 128:(h % 4 + 1) * 128], aof_s[0:1, :],
                                      rows8[0:1, h * 128:(h + 1) * 128], start=True, stop=True)
                    ctx.inc(mm, "d1_t", 1)

                @blk.vector
                def _(v):
                    ctx.wait(v, "d1_t")
                    for h in range(8):
                        bank = pb[2 + h // 4]
                        dst = mu_t if h < 4 else var_t
                        v.tensor_scalar(dst[:, (h % 4) * 128:(h % 4 + 1) * 128],
                                        bank[:, (h % 4) * 128:(h % 4 + 1) * 128],
                                        combo[:, h:h + 1], 30.0, ALU.add, ALU.min)
                    ctx.inc(v.tensor_copy(kk8[:, 0::2, :], pb[0][:, :].rearrange("p (k c) -> p k c", c=128)), "d1_v", 1)

                @blk.scalar
                def _(sc):
                    ctx.wait(sc, "d1_v")
                    sc.activation(e8a[:, 0:4, :].rearrange("p k c -> p (k c)"), mu_t[:], AF.Exp)
                    ctx.inc(sc.activation(e8a[:, 4:8, :].rearrange("p k c -> p (k c)"), var_t[:], AF.Exp), "d1_s", 1)

                if dbg and seg == 0 and tc == 0:
                    dump(blk, "e8a", e8a[:], "d1_s")

                @blk.vector
                def _(v):
                    ctx.wait(v, "d1_s")
                    for h in range(8):
                        bank = pb[4 + h // 4]
                        dst = mu_t if h < 4 else var_t
                        v.tensor_scalar(dst[:, (h % 4) * 128:(h % 4 + 1) * 128],
                                        bank[:, (h % 4) * 128:(h % 4 + 1) * 128],
                                        combo[:, h:h + 1], 30.0, ALU.add, ALU.min)
                    v.tensor_copy(kk8[:, 1::2, :], pb[0][:, :].rearrange("p (k c) -> p k c", c=128))
                    v.tensor_copy(m8[:, 0::2, :], pb[1][:, :].rearrange("p (k c) -> p k c", c=128))
                    v.tensor_copy(m8[:, 1::2, :], pb[1][:, :].rearrange("p (k c) -> p k c", c=128))
                    v.tensor_tensor(ap_t[:], e8a[:], mus_s[:], ALU.mult)
                    v.tensor_copy(ktm[:], pvt[:, 0:4, :])
                    v.tensor_copy(vtm[:], pvt[:, 4:12, :])
                    ctx.inc(v.tensor_tensor(ap_t[:], ap_t[:], kk8[:], ALU.mult), "d1_v2", 1)

                @blk.scalar
                def _(sc):
                    ctx.wait(sc, "d1_v2")
                    sc.activation(e8a[:, 0:4, :].rearrange("p k c -> p (k c)"), mu_t[:], AF.Exp)
                    ctx.inc(sc.activation(e8a[:, 4:8, :].rearrange("p k c -> p (k c)"), var_t[:], AF.Exp), "d1_s2", 1)

                @blk.vector
                def _(v):
                    ctx.wait(v, "d1_s2")
                    inst = v.tensor_tensor(m8[:], m8[:], e8a[:], ALU.mult)
                    ctx.vsync(v, inst)
                    ctx.inc(v.tensor_tensor(m8[:], m8[:], mui_s[:], ALU.mult), "d2_v", 1)

                if dbg and seg == 0 and tc == 0:
                    dump(blk, "ap", ap_t[:], "d2_v")
                    dump(blk, "m8", m8[:], "d2_v")
                    dump(blk, "kk8", kk8[:], "d2_v")
                    dump(blk, "vtm", vtm[:], "d2_v")
                    dump(blk, "ktm", ktm[:], "d2_v")

                @blk.tensor
                def _(t):
                    ctx.wait(t, "d2_v")
                    for h in range(8):
                        mm = t.transpose(pvt[:, h, :], ap_t[:, h, :], id_s[:])
                    ctx.inc(mm, "d2_t", 1)

                @blk.vector
                def _(v):
                    ctx.wait(v, "d2_t")
                    ctx.inc(v.tensor_copy(au_t[:], pvt[:, 0:8, :]), "d2_v2", 1)

                if dbg and seg == 0 and tc == 0:
                    dump(blk, "au", au_t[:], "d2_v2")

                @blk.tensor
                def _(t, tc=tc):
                    ctx.wait(t, "d2_v2")
                    sl = slice(tc * 128, (tc + 1) * 128)
                    for h in range(8):
                        bank = pb[h // 4]
                        off = (h % 4) * 128
                        t.matmul(bank[:, off:off + 128], qkv[:, 4 + h // 2, sl], st_hi[:, h, :], start=True, stop=False)
                        mm = t.matmul(bank[:, off:off + 128], qkv[:, 4 + h // 2, sl], st_lo[:, h, :], start=False, stop=True)
                    ctx.inc(mm, "d3_t", 1)

                @blk.vector
                def _(v):
                    ctx.wait(v, "d3_t")
                    for h in range(8):
                        bank = pb[h // 4]
                        off = (h % 4) * 128
                        last = v.scalar_tensor_tensor(u_t[:, h, :], bank[:, off:off + 128],
                                                      nrhs[:, h:h + 1], vtm[:, h, :], ALU.mult, ALU.add)
                    ctx.inc(last, "d3_v", 1)

                if dbg and seg == 0 and tc == 0:
                    dump(blk, "rhs", u_t[:], "d3_v")

                for stage in range(7):
                    @blk.tensor
                    def _(t, stage=stage):
                        ctx.wait(t, "d3_v" if stage == 0 else "d4_v")
                        src_p = ap_t if stage == 0 else pp_t
                        src_u = au_t if stage == 0 else pu_t
                        for h in range(8):
                            bank = pb[h // 4]
                            off = (h % 4) * 128
                            mm = t.matmul(bank[:, off:off + 128], src_p[:, h, :], u_t[:, h, :], start=True, stop=True)
                        if stage < 6:
                            for h in range(8):
                                bank = pb[2 + h // 4]
                                off = (h % 4) * 128
                                t.matmul(bank[:, off:off + 128], src_p[:, h, :], src_u[:, h, :], start=True, stop=True)
                            for h in range(8):
                                bank = pb[4 + h // 4]
                                off = (h % 4) * 128
                                mm = t.matmul(bank[:, off:off + 128], src_u[:, h, :], src_p[:, h, :], start=True, stop=True)
                        ctx.inc(mm, "d4_t", 1)

                    @blk.vector
                    def _(v, stage=stage):
                        ctx.wait(v, "d4_t")
                        op = ALU.subtract if stage == 0 else ALU.add
                        v.tensor_tensor(u_t[:, 0:4, :], u_t[:, 0:4, :], pb[0][:, :].rearrange("p (k c) -> p k c", c=128), op)
                        last = v.tensor_tensor(u_t[:, 4:8, :], u_t[:, 4:8, :], pb[1][:, :].rearrange("p (k c) -> p k c", c=128), op)
                        if stage < 6:
                            v.tensor_copy(pu_t[:, 0:4, :], pb[2][:, :].rearrange("p (k c) -> p k c", c=128))
                            v.tensor_copy(pu_t[:, 4:8, :], pb[3][:, :].rearrange("p (k c) -> p k c", c=128))
                            v.tensor_copy(pp_t[:, 0:4, :], pb[4][:, :].rearrange("p (k c) -> p k c", c=128))
                            last = v.tensor_copy(pp_t[:, 4:8, :], pb[5][:, :].rearrange("p (k c) -> p k c", c=128))
                        ctx.inc(last, "d4_v", 1)

                if dbg and seg == 0 and tc == 0:
                    dump(blk, "uu", u_t[:], "d4_v")

                @blk.vector
                def _(v):
                    ctx.wait(v, "d4_v")
                    inst = v.memset(t8a[:], 0.0)
                    ctx.vsync(v, inst)
                    for h in range(8):
                        last = v.tensor_scalar_mul(u3_t[:, h, :], u_t[:, h, :], e_u3[:, h:h + 1])
                    ctx.inc(last, "d5_v", 1)

                if dbg and seg == 0 and tc == 0:
                    dump(blk, "u3", u3_t[:], "d5_v")

                @blk.tensor
                def _(t, tc=tc):
                    ctx.wait(t, "d5_v")
                    sl = slice(tc * 128, (tc + 1) * 128)
                    for h in range(8):
                        off = (h % 4) * 128
                        t.matmul(pb[h // 4][:, off:off + 128], m8[:, h, :], u_t[:, h, :], start=True, stop=True)
                        t.matmul(pb[2 + h // 4][:, off:off + 128], qkv[:, h // 2, sl], st_hi[:, h, :], start=True, stop=False)
                        t.matmul(pb[2 + h // 4][:, off:off + 128], qkv[:, h // 2, sl], st_lo[:, h, :], start=False, stop=True)
                        mm = t.matmul(pb[4 + h // 4][:, off:off + 128], ktm[:, h // 2, :], u3_t[:, h, :], start=True, stop=True)
                    ctx.inc(mm, "d6_t", 1)

                @blk.vector
                def _(v):
                    ctx.wait(v, "d6_t")
                    for h in range(8):
                        off = (h % 4) * 128
                        v.tensor_copy(ogt[:, h, :], pb[h // 4][:, off:off + 128])
                    for h in range(8):
                        off = (h % 4) * 128
                        last = v.scalar_tensor_tensor(ogt[:, h, :], pb[2 + h // 4][:, off:off + 128],
                                                      erowM[:, h:h + 1], ogt[:, h, :], ALU.mult, ALU.add)
                    ctx.inc(last, "d6_v", 1)

                @blk.scalar
                def _(sc):
                    ctx.wait(sc, "d6_v")
                    for h in range(8):
                        last = sc.activation(u3_t[:, h, :], ogt[:, h, :], AF.Square,
                                             accum_out=osum[:, h:h + 1])
                    ctx.inc(last, "d7_s", 1)

                if dbg and seg == 0 and tc == 0:
                    dump(blk, "osum", osum[:], "d7_s")

                @blk.vector
                def _(v):
                    ctx.wait(v, "d7_s")
                    ctx.inc(v.tensor_scalar(t8b[:], osum[:], 1.0 / 128.0, 1e-6, ALU.mult, ALU.add), "d7_v", 1)

                @blk.scalar
                def _(sc):
                    ctx.wait(sc, "d7_v")
                    ctx.inc(sc.activation(osc[:], t8b[:], AF.Sqrt), "d7_s3", 1)

                @blk.vector
                def _(v, tc=tc):
                    ctx.wait(v, "d7_s3")
                    inst = v.reciprocal(osc[:], osc[:])
                    ctx.vsync(v, inst)
                    for h in range(8):
                        v.tensor_scalar_mul(ogt[:, h, :], ogt[:, h, :], osc[:, h:h + 1])
                    for h in range(8):
                        bank = pb[4 + h // 4]
                        off = (h % 4) * 128
                        v.scalar_tensor_tensor(st_f[:, h, :], st_f[:, h, :], gamC[:, h:h + 1],
                                               bank[:, off:off + 128], ALU.mult, ALU.add)
                    ctx.inc(v.tensor_tensor(og1[:], ogt[:].rearrange("p k c -> p (k c)"), zt[:, tc, :], ALU.mult), "d8_v", 1)

                if dbg and seg == 0 and tc == 0:
                    dump(blk, "osc", osc[:], "d7_s3")
                    dump(blk, "ogt", ogt[:], "d8_v")
                    dump(blk, "og1", og1[:], "d8_v")
                    dump(blk, "stf", st_f[:], "d8_v")

                @blk.scalar
                def _(sc):
                    ctx.wait(sc, "d8_v")
                    ctx.inc(sc.copy(st_hi[:], st_f[:]), "d8_s", 1)

                @blk.vector
                def _(v):
                    ctx.wait(v, "d8_s")
                    if dbg:
                        ctx.wait(v, "dbg_st")
                    ctx.inc(v.tensor_tensor(st_lo[:], st_f[:], st_hi[:], ALU.subtract), "og_v", 1)

                @blk.sync
                def _(s, t0=t0, tc=tc):
                    ctx.wait(s, "og_v")
                    ctx.inc(s.dma_start(out=og_out[t0 + tc * 128:t0 + (tc + 1) * 128, :],
                                        in_=og1[:]), "og_st", 16)

        @blk.gpsimd
        def _(g):
            ctx.wait(g, "og_st")

    return nc



def build_wsum():
    RW1 = INTER // 8
    RW3 = D // 8
    NCOL = 6144
    nc = bass.Bass("TRN2", target_bir_lowering=False)
    w1 = nc.dram_tensor("w1", [RW1, D], F32, kind="ExternalInput").ap()
    w2 = nc.dram_tensor("w2", [RW1, D], F32, kind="ExternalInput").ap()
    w3 = nc.dram_tensor("w3", [RW3, INTER], F32, kind="ExternalInput").ap()
    allones = nc.dram_tensor("allones", [128, 128], F32, kind="ExternalInput").ap()
    out = nc.dram_tensor("wsums", [1, 3], F32, kind="ExternalOutput").ap()
    es = contextlib.ExitStack()
    with es:
        ctx = Ctx(nc, es)
        buf = [ctx.sb(f"buf{i}", [128, NCOL], F32) for i in range(2)]
        ao = ctx.sb("ao", [128, 128], F32)
        acc = ctx.sb("acc", [128, 3], F32)
        red = ctx.sb("red", [128, 2], F32)
        res = ctx.sb("res", [1, 3], F32)
        ps = ctx.ps("ps", [128, 4])
        blk = es.enter_context(nc.Block())

        @blk.gpsimd
        def _(g):
            ctx.inc(g.dma_start(out=ao[:], in_=allones), "ld0", 16)

        @blk.vector
        def _(v):
            ctx.wait(v, "ld0")
            ctx.inc(v.memset(acc[:], 0.0), "acc_v", 1)

        plans = [(w1, RW1 * D // 128, 0), (w2, RW1 * D // 128, 1), (w3, RW3 * INTER // 128, 2)]
        ib = 0
        for wt, total, idx2 in plans:
            nblk = (total + NCOL - 1) // NCOL
            for b in range(nblk):
                cols = min(NCOL, total - b * NCOL)
                bb = ib % 2

                @blk.sync
                def _(s, wt=wt, b=b, cols=cols, bb=bb, ib=ib):
                    if ib >= 2:
                        ctx.wait(s, "acc_v")
                    ctx.inc(s.dma_start(
                        out=buf[bb][:, 0:cols],
                        in_=wt.rearrange("a b -> (a b)").rearrange("(p x) -> p x", p=128)[:, b * NCOL:b * NCOL + cols]),
                        "w_ld", 16)

                @blk.vector
                def _(v, cols=cols, bb=bb, idx2=idx2):
                    ctx.wait(v, "w_ld")
                    inst = v.tensor_reduce(red[:, bb:bb + 1], buf[bb][:, 0:cols], AX.X, ALU.add,
                                           apply_absolute_value=True)
                    ctx.vsync(v, inst)
                    ctx.inc(v.tensor_tensor(acc[:, idx2:idx2 + 1], acc[:, idx2:idx2 + 1],
                                            red[:, bb:bb + 1], ALU.add), "acc_v", 1)
                ib += 1

        @blk.tensor
        def _(t):
            ctx.wait(t, "acc_v")
            ctx.inc(t.matmul(ps[0:1, 0:3], ao[:, 0:1], acc[:], start=True, stop=True), "fin_t", 1)

        @blk.vector
        def _(v):
            ctx.wait(v, "fin_t")
            ctx.inc(v.tensor_copy(res[:], ps[0:1, 0:3]), "fin_v", 1)

        @blk.sync
        def _(s):
            ctx.wait(s, "fin_v")
            ctx.inc(s.dma_start(out=out, in_=res[:]), "fin_st", 16)

        @blk.gpsimd
        def _(g):
            ctx.wait(g, "fin_st")
    return nc


def build_wtern():
    RW1 = INTER // 8
    RW3 = D // 8
    NCOL = 6144
    NELEM = float(INTER * D)
    nc = bass.Bass("TRN2", target_bir_lowering=False)
    w1 = nc.dram_tensor("w1", [RW1, D], F32, kind="ExternalInput").ap()
    w2 = nc.dram_tensor("w2", [RW1, D], F32, kind="ExternalInput").ap()
    w3 = nc.dram_tensor("w3", [RW3, INTER], F32, kind="ExternalInput").ap()
    wsums = nc.dram_tensor("wsums", [8, 3], F32, kind="ExternalInput").ap()
    allones = nc.dram_tensor("allones", [128, 128], F32, kind="ExternalInput").ap()
    t1o = nc.dram_tensor("t1", [RW1, D], BF16, kind="ExternalOutput").ap()
    t2o = nc.dram_tensor("t2", [RW1, D], BF16, kind="ExternalOutput").ap()
    t3o = nc.dram_tensor("t3", [RW3, INTER], BF16, kind="ExternalOutput").ap()
    es = contextlib.ExitStack()
    with es:
        ctx = Ctx(nc, es)
        buf = [ctx.sb(f"buf{i}", [128, NCOL], F32) for i in range(2)]
        tb = [ctx.sb(f"tb{i}", [128, NCOL], BF16) for i in range(2)]
        g1 = ctx.sb("g1", [128, NCOL], F32)
        g2 = ctx.sb("g2", [128, NCOL], F32)
        ao = ctx.sb("ao", [128, 128], F32)
        ws = ctx.sb("ws", [8, 3], F32)
        thr = ctx.sb("thr", [1, 8], F32)
        thrb = ctx.sb("thrb", [128, 8], F32)
        ps = ctx.ps("ps", [128, 16])
        blk = es.enter_context(nc.Block())

        @blk.gpsimd
        def _(g):
            ctx.inc(g.dma_start(out=ao[:], in_=allones), "ld0", 16)
            ctx.inc(g.dma_start(out=ws[:], in_=wsums), "ld0", 16)

        @blk.tensor
        def _(t):
            ctx.wait(t, "ld0")
            ctx.inc(t.matmul(ps[0:1, 0:3], ao[0:8, 0:1], ws[:], start=True, stop=True), "t0", 1)

        @blk.vector
        def _(v):
            ctx.wait(v, "t0")
            v.tensor_scalar(thr[0:1, 0:3], ps[0:1, 0:3], 1.0 / NELEM, 1e-5, ALU.mult, ALU.max)
            inst = v.memset(thr[0:1, 3:8], 0.0)
            ctx.vsync(v, inst)
            v.tensor_scalar_mul(thr[0:1, 0:3], thr[0:1, 0:3], 0.5)
            inst = v.memset(g1[:, 0:1], 0.0)
            ctx.vsync(v, inst)
            ctx.inc(v.tensor_scalar_mul(thr[0:1, 3:6], thr[0:1, 0:3], -1.0), "t1v", 1)

        @blk.tensor
        def _(t):
            ctx.wait(t, "t1v")
            ctx.inc(t.matmul(ps[:, 8:16], ao[0:1, :], thr[0:1, 0:8], start=True, stop=True), "t2t", 1)

        @blk.vector
        def _(v):
            ctx.wait(v, "t2t")
            ctx.inc(v.tensor_copy(thrb[:], ps[:, 8:16]), "t3v", 1)

        plans = [(w1, t1o, RW1 * D // 128, 0), (w2, t2o, RW1 * D // 128, 1),
                 (w3, t3o, RW3 * INTER // 128, 2)]
        ib = 0
        for wt, wo, total, idx2 in plans:
            nblk = (total + NCOL - 1) // NCOL
            for b in range(nblk):
                cols = min(NCOL, total - b * NCOL)
                bb = ib % 2

                @blk.sync
                def _(s, wt=wt, b=b, cols=cols, bb=bb, ib=ib):
                    ctx.wait(s, "t3v")
                    if ib >= 2:
                        ctx.wait(s, "st_d")
                    ctx.inc(s.dma_start(
                        out=buf[bb][:, 0:cols],
                        in_=wt.rearrange("a b -> (a b)").rearrange("(p x) -> p x", p=128)[:, b * NCOL:b * NCOL + cols]),
                        "w_ld", 16)

                @blk.vector
                def _(v, cols=cols, bb=bb, idx2=idx2):
                    ctx.wait(v, "w_ld")
                    v.tensor_single_scalar(g1[:, 0:cols], buf[bb][:, 0:cols], thrb[:, idx2:idx2 + 1], ALU.is_gt)
                    inst = v.tensor_single_scalar(g2[:, 0:cols], buf[bb][:, 0:cols], thrb[:, 3 + idx2:4 + idx2], ALU.is_lt)
                    ctx.vsync(v, inst)
                    ctx.inc(v.tensor_tensor(tb[bb][:, 0:cols], g2[:, 0:cols], g1[:, 0:cols], ALU.subtract), "tern_v", 1)

                @blk.sync
                def _(s, wo=wo, b=b, cols=cols, bb=bb):
                    ctx.wait(s, "tern_v")
                    ctx.inc(s.dma_start(
                        out=wo.rearrange("a b -> (a b)").rearrange("(p x) -> p x", p=128)[:, b * NCOL:b * NCOL + cols],
                        in_=tb[bb][:, 0:cols]), "st_d", 16)
                ib += 1

        @blk.gpsimd
        def _(g):
            ctx.wait(g, "st_d")
    return nc



def build_l2():
    TOK = 1024
    ST = 512
    NST = TOK // ST
    nc = bass.Bass("TRN2", target_bir_lowering=False)
    og = nc.dram_tensor("og", [NST, 128, 32, ST], BF16, kind="ExternalInput").ap()
    x0 = nc.dram_tensor("x0", [NST, 128, 16, ST], F32, kind="ExternalInput").ap()
    woT = nc.dram_tensor("woT", [16, 128, 32, 128], F32, kind="ExternalInput").ap()
    gwT = nc.dram_tensor("gwT", [44, 128, 16, 128], F32, kind="ExternalInput").ap()
    uwT = nc.dram_tensor("uwT", [44, 128, 16, 128], F32, kind="ExternalInput").ap()
    dwT = nc.dram_tensor("dwT", [16, 128, 44, 128], F32, kind="ExternalInput").ap()
    gsc = nc.dram_tensor("gsc", [128, 44], F32, kind="ExternalInput").ap()
    usc = nc.dram_tensor("usc", [128, 44], F32, kind="ExternalInput").ap()
    dsc = nc.dram_tensor("dsc", [128, 16], F32, kind="ExternalInput").ap()
    allones = nc.dram_tensor("allones", [128, 128], F32, kind="ExternalInput").ap()
    x2_o = nc.dram_tensor("x2", [NST, 128, 16, ST], F32, kind="ExternalOutput").ap()
    ln2_o = nc.dram_tensor("ln2", [NST, 128, 16, ST], BF16, kind="ExternalOutput").ap()

    es = contextlib.ExitStack()
    with es:
        ctx = Ctx(nc, es)
        og_s = ctx.sb("og_s", [128, 32, ST], BF16)
        x_s = ctx.sb("x_s", [128, 16, ST], F32)
        ln_s = ctx.sb("ln_s", [128, 16, ST], BF16)
        h_s = ctx.sb("h_s", [128, 44, ST], BF16)
        gt_s = ctx.sb("gt_s", [128, ST], BF16)
        ut_s = ctx.sb("ut_s", [128, ST], BF16)
        wo_b = [ctx.sb(f"wo_b{i}", [128, 32, 128], BF16) for i in range(2)]
        wg_b = [ctx.sb(f"wg_b{i}", [128, 16, 128], BF16) for i in range(2)]
        wu_b = [ctx.sb(f"wu_b{i}", [128, 16, 128], BF16) for i in range(2)]
        wd_b = [ctx.sb(f"wd_b{i}", [128, 44, 128], BF16) for i in range(2)]
        gsc_s = ctx.sb("gsc_s", [128, 44], F32)
        usc_s = ctx.sb("usc_s", [128, 44], F32)
        dsc_s = ctx.sb("dsc_s", [128, 16], F32)
        aof = ctx.sb("aof", [128, 128], F32)
        sqf = ctx.sb("sqf", [128, 4, ST], F32)
        mu_t = ctx.sb("mu_t", [128, ST], F32)
        var_t = ctx.sb("var_t", [128, ST], F32)
        pb = [ctx.ps(f"pb{i}", [128, 512]) for i in range(6)]
        blk = es.enter_context(nc.Block())

        @blk.gpsimd
        def _(g):
            for dst, src_ in [(gsc_s, gsc), (usc_s, usc), (dsc_s, dsc), (aof, allones)]:
                ctx.inc(g.dma_start(out=dst[:], in_=src_), "c_ld", 16)

        def emit_ln(tag, dst_bf):
            for grp in range(4):
                @blk.scalar
                def _(sc, grp=grp, tag=tag):
                    ctx.wait(sc, f"{tag}_pre")
                    if grp > 0:
                        ctx.wait(sc, f"{tag}_t")
                    for i in range(4):
                        last = sc.activation(sqf[:, i, :], x_s[:, grp * 4 + i, :], AF.Square)
                    ctx.inc(last, f"{tag}_s", 1)

                @blk.tensor
                def _(t, grp=grp, tag=tag):
                    ctx.wait(t, f"{tag}_s")
                    for i in range(4):
                        k = grp * 4 + i
                        t.matmul(pb[4][:, :], aof[:], x_s[:, k, :], start=(k == 0), stop=(k == 15))
                        mm = t.matmul(pb[5][:, :], aof[:], sqf[:, i, :], start=(k == 0), stop=(k == 15))
                    ctx.inc(mm, f"{tag}_t", 1)

            @blk.vector
            def _(v, tag=tag):
                ctx.wait(v, f"{tag}_t")
                v.tensor_scalar_mul(mu_t[:], pb[4][:, :], 1.0 / D)
                v.tensor_scalar(var_t[:], pb[5][:, :], 1.0 / D, 1e-5, ALU.mult, ALU.add)
                inst = v.tensor_tensor(pb[4][:, :], mu_t[:], mu_t[:], ALU.mult)
                ctx.vsync(v, inst)
                ctx.inc(v.tensor_tensor(var_t[:], var_t[:], pb[4][:, :], ALU.subtract), f"{tag}_v", 1)

            @blk.scalar
            def _(sc, tag=tag):
                ctx.wait(sc, f"{tag}_v")
                ctx.inc(sc.activation(var_t[:], var_t[:], AF.Sqrt), f"{tag}_s2", 1)

            @blk.vector
            def _(v, tag=tag, dst_bf=dst_bf):
                ctx.wait(v, f"{tag}_s2")
                v.reciprocal(var_t[:], var_t[:])
                for k in range(16):
                    v.tensor_tensor(ln_s[:, k, :] if dst_bf is None else dst_bf[:, k, :],
                                    x_s[:, k, :], mu_t[:], ALU.subtract)
                dd = ln_s if dst_bf is None else dst_bf
                for k in range(16):
                    last = v.tensor_tensor(dd[:, k, :], dd[:, k, :], var_t[:], ALU.mult)
                ctx.inc(last, f"{tag}_v2", 1)

        for st in range(NST):
            t0 = st * ST

            @blk.gpsimd
            def _(g, t0=t0, st=st):
                ctx.wait(g, "c_ld")
                if st > 0:
                    ctx.wait(g, "out_st")
                ctx.inc(g.dma_start(out=og_s[:], in_=og[st]), "i_ld", 16)
                ctx.inc(g.dma_start(out=x_s[:], in_=x0[st]), "i_ld", 16)

            for m in range(16):
                @blk.gpsimd
                def _(g, m=m):
                    ctx.wait(g, "i_ld")
                    ctx.wait_val(g, "wo_t", ctx.semv.get("wo_t", 0) - 1)
                    ctx.inc(g.dma_start(out=wo_b[m % 2][:], in_=woT[m]), "wo_ld", 16)

                @blk.tensor
                def _(t, m=m):
                    ctx.wait(t, "wo_ld")
                    ctx.wait_val(t, "wo_v", ctx.semv.get("wo_v", 0) - 1)
                    for k in range(32):
                        mm = t.matmul(pb[m % 2][:, :], wo_b[m % 2][:, k, :], og_s[:, k, :],
                                      start=(k == 0), stop=(k == 31))
                    ctx.inc(mm, "wo_t", 1)

                @blk.vector
                def _(v, m=m):
                    ctx.wait(v, "wo_t")
                    ctx.inc(v.tensor_tensor(x_s[:, m, :], x_s[:, m, :], pb[m % 2][:, :], ALU.add), "wo_v", 1)

            @blk.vector
            def _(v):
                ctx.wait(v, "wo_v")
                ctx.inc(v.memset(sqf[:, 0, 0:1], 0.0), "ln1_pre", 1)
            emit_ln("ln1", None)

            for m in range(44):
                @blk.gpsimd
                def _(g, m=m):
                    ctx.wait(g, "ln1_v2")
                    ctx.wait_val(g, "gu_t", ctx.semv.get("gu_t", 0) - 1)
                    ctx.inc(g.dma_start(out=wg_b[m % 2][:], in_=gwT[m]), "gu_ld", 16)
                    ctx.inc(g.dma_start(out=wu_b[m % 2][:], in_=uwT[m]), "gu_ld", 16)

                @blk.tensor
                def _(t, m=m):
                    ctx.wait(t, "gu_ld")
                    ctx.wait_val(t, "h_v", ctx.semv.get("h_v", 0) - 1)
                    pg, pu = pb[2 + 2 * (m % 2)], pb[3 + 2 * (m % 2)]
                    for k in range(16):
                        t.matmul(pg[:, :], wg_b[m % 2][:, k, :], ln_s[:, k, :], start=(k == 0), stop=(k == 15))
                    for k in range(16):
                        mm = t.matmul(pu[:, :], wu_b[m % 2][:, k, :], ln_s[:, k, :], start=(k == 0), stop=(k == 15))
                    ctx.inc(mm, "gu_t", 1)

                @blk.scalar
                def _(sc, m=m):
                    ctx.wait(sc, "gu_t")
                    ctx.wait(sc, "h_v")
                    ctx.inc(sc.activation(gt_s[:], pb[2 + 2 * (m % 2)][:, :], AF.Silu, scale=gsc_s[:, m:m + 1]), "gu_s", 1)

                @blk.vector
                def _(v, m=m):
                    ctx.wait(v, "gu_s")
                    v.tensor_scalar_mul(ut_s[:], pb[3 + 2 * (m % 2)][:, :], usc_s[:, m:m + 1])
                    ctx.inc(v.tensor_tensor(h_s[:, m, :], gt_s[:], ut_s[:], ALU.mult), "h_v", 1)

            for m in range(16):
                @blk.gpsimd
                def _(g, m=m):
                    ctx.wait(g, "h_v")
                    ctx.wait_val(g, "dn_t", ctx.semv.get("dn_t", 0) - 1)
                    ctx.inc(g.dma_start(out=wd_b[m % 2][:], in_=dwT[m]), "dn_ld", 16)

                @blk.tensor
                def _(t, m=m):
                    ctx.wait(t, "dn_ld")
                    ctx.wait_val(t, "dn_v", ctx.semv.get("dn_v", 0) - 1)
                    for k in range(44):
                        mm = t.matmul(pb[m % 2][:, :], wd_b[m % 2][:, k, :], h_s[:, k, :],
                                      start=(k == 0), stop=(k == 43))
                    ctx.inc(mm, "dn_t", 1)

                @blk.vector
                def _(v, m=m):
                    ctx.wait(v, "dn_t")
                    ctx.inc(v.scalar_tensor_tensor(x_s[:, m, :], pb[m % 2][:, :], dsc_s[:, m:m + 1],
                                                   x_s[:, m, :], ALU.mult, ALU.add), "dn_v", 1)

            @blk.vector
            def _(v):
                ctx.wait(v, "dn_v")
                ctx.inc(v.memset(sqf[:, 0, 0:1], 0.0), "ln2_pre", 1)
            emit_ln("ln2", ln_s)

            @blk.sync
            def _(s, st=st):
                ctx.wait(s, "ln2_v2")
                ctx.inc(s.dma_start(out=x2_o[st], in_=x_s[:]), "out_st", 16)
                ctx.inc(s.dma_start(out=ln2_o[st], in_=ln_s[:]), "out_st", 16)

        @blk.gpsimd
        def _(g):
            ctx.wait(g, "out_st")
    return nc



def build_l3():
    HC = 512
    NKT = S // 128
    NQB = S // 512
    LN8 = math.log(8.0)
    nc = bass.Bass("TRN2", target_bir_lowering=False)
    ln2 = nc.dram_tensor("ln2", [D, S], BF16, kind="ExternalInput").ap()
    wq_c = nc.dram_tensor("wq_c", [D, HC], F32, kind="ExternalInput").ap()
    wk_c = nc.dram_tensor("wk_c", [D, HC], F32, kind="ExternalInput").ap()
    wv_c = nc.dram_tensor("wv_c", [D, HC], F32, kind="ExternalInput").ap()
    lq1 = nc.dram_tensor("lq1", [1, DH1], F32, kind="ExternalInput").ap()
    lk1 = nc.dram_tensor("lk1", [1, DH1], F32, kind="ExternalInput").ap()
    lq2 = nc.dram_tensor("lq2", [1, DH1], F32, kind="ExternalInput").ap()
    lk2 = nc.dram_tensor("lk2", [1, DH1], F32, kind="ExternalInput").ap()
    gnw = nc.dram_tensor("gnw", [128, 4, 128], F32, kind="ExternalInput").ap()
    gnb = nc.dram_tensor("gnb", [128, 4, 128], F32, kind="ExternalInput").ap()
    allones = nc.dram_tensor("allones", [128, 128], F32, kind="ExternalInput").ap()
    ident = nc.dram_tensor("ident", [128, 128], F32, kind="ExternalInput").ap()
    att_o = nc.dram_tensor("att", [S, HC], BF16, kind="ExternalOutput").ap()

    es = contextlib.ExitStack()
    with es:
        ctx = Ctx(nc, es)
        wq_s = ctx.sb("wq_s", [128, 16, HC], BF16)
        wk_s = ctx.sb("wk_s", [128, 16, HC], BF16)
        wv_s = ctx.sb("wv_s", [128, 16, HC], BF16)
        lt = ctx.sb("lt", [1, 4 * DH1], F32)
        aof = ctx.sb("aof", [128, 128], F32)
        idf_s = ctx.sb("idf_s", [128, 128], F32)
        gnw_s = ctx.sb("gnw_s", [128, 4, 128], F32)
        gnb_s = ctx.sb("gnb_s", [128, 4, 128], F32)
        xt = ctx.sb("xt", [128, 16, 512], BF16)
        qf = ctx.sb("qf", [128, 4, S], BF16)
        kf = ctx.sb("kf", [128, 4, S], BF16)
        vt = ctx.sb("vt", [128, 4, 32, 132], BF16)
        nrm = ctx.sb("nrm", [1, 64], F32)
        negc = ctx.sb("negc", [128, 4], F32)
        lamv = ctx.sb("lamv", [128, 1], F32)
        p_sb = ctx.sb("p_sb", [128, 2, 512], BF16)
        att1 = ctx.sb("att1", [128, 4, 128], F32)
        att2 = ctx.sb("att2", [128, 4, 128], F32)
        attc = ctx.sb("attc", [128, 4, 128], BF16)
        stat = ctx.sb("stat", [128, 16], F32)
        st2 = ctx.sb("st2", [128, 16], F32)
        sdump = ctx.sb("sdump", [128, 512], BF16)
        ps_s = [ctx.ps(f"ps_s{i}", [128, 512]) for i in range(2)]
        ps_o = [ctx.ps(f"ps_o{i}", [128, 2, 132]) for i in range(4)]
        ps_m = ctx.ps("ps_m", [128, 512])
        blk = es.enter_context(nc.Block())

        @blk.gpsimd
        def _(g):
            for dst, src_ in [(aof, allones), (idf_s, ident), (gnw_s, gnw), (gnb_s, gnb)]:
                ctx.inc(g.dma_start(out=dst[:], in_=src_), "c_ld", 16)
            for i, src_ in enumerate([lq1, lk1, lq2, lk2]):
                ctx.inc(g.dma_start(out=lt[0:1, i * DH1:(i + 1) * DH1], in_=src_), "c_ld", 16)
            ctx.inc(g.dma_start(out=wq_s[:], in_=wq_c.rearrange("(c p) q -> p c q", p=128)), "c_ld", 16)
            ctx.inc(g.dma_start(out=wk_s[:], in_=wk_c.rearrange("(c p) q -> p c q", p=128)), "c_ld", 16)
            ctx.inc(g.dma_start(out=wv_s[:], in_=wv_c.rearrange("(c p) q -> p c q", p=128)), "c_ld", 16)

        @blk.vector
        def _(v):
            ctx.wait(v, "c_ld")
            v.tensor_tensor(lt[0:1, 0:DH1], lt[0:1, 0:DH1], lt[0:1, DH1:2 * DH1], ALU.mult)
            inst = v.tensor_tensor(lt[0:1, 2 * DH1:3 * DH1], lt[0:1, 2 * DH1:3 * DH1], lt[0:1, 3 * DH1:4 * DH1], ALU.mult)
            ctx.vsync(v, inst)
            v.tensor_reduce(nrm[0:1, 0:1], lt[0:1, 0:DH1], AX.X, ALU.add)
            inst = v.tensor_reduce(nrm[0:1, 1:2], lt[0:1, 2 * DH1:3 * DH1], AX.X, ALU.add)
            ctx.vsync(v, inst)
            v.tensor_scalar_mul(gnw_s[:], gnw_s[:], 1.0 - LAMBDA_INIT)
            ctx.inc(v.tensor_scalar_mul(gnb_s[:], gnb_s[:], 1.0 - LAMBDA_INIT), "l_v", 1)

        @blk.scalar
        def _(sc):
            ctx.wait(sc, "l_v")
            ctx.inc(sc.activation(nrm[0:1, 0:2], nrm[0:1, 0:2], AF.Exp), "l_s", 1)

        @blk.vector
        def _(v):
            ctx.wait(v, "l_s")
            inst = v.tensor_tensor(nrm[0:1, 2:3], nrm[0:1, 0:1], nrm[0:1, 1:2], ALU.subtract)
            ctx.vsync(v, inst)
            inst = v.tensor_scalar(nrm[0:1, 3:4], nrm[0:1, 2:3], LAMBDA_INIT, -1.0, ALU.add, ALU.mult)
            ctx.vsync(v, inst)
            ctx.inc(v.memset(vt[:, :, :, 128:132], 0.0), "l_v2", 1)

        @blk.tensor
        def _(t):
            ctx.wait(t, "l_v2")
            ctx.inc(t.matmul(ps_m[:, 0:1], aof[0:1, :], nrm[0:1, 3:4], start=True, stop=True), "l_t", 1)

        @blk.vector
        def _(v):
            ctx.wait(v, "l_t")
            v.tensor_copy(lamv[:], ps_m[:, 0:1])
            ctx.inc(v.memset(vt[:, :, :, 128:129], 1.0), "l_v3", 1)

        for n in range(NQB):
            @blk.gpsimd
            def _(g, n=n):
                ctx.wait(g, "l_v3")
                if n > 0:
                    ctx.wait(g, "pv_t")
                ctx.inc(g.dma_start(out=xt[:], in_=ln2.rearrange("(c p) s -> p c s", p=128)[:, :, n * 512:(n + 1) * 512]), "x_ld", 16)

            outs = [("q", hh) for hh in range(4)] + [("k", hh) for hh in range(4)]
            for j, (kind, hh) in enumerate(outs):
                @blk.tensor
                def _(t, j=j, kind=kind, hh=hh):
                    ctx.wait(t, "x_ld")
                    if ctx.semv.get("pj_v", 0) >= 2:
                        ctx.wait(t, "pj_v")
                    w = wq_s if kind == "q" else wk_s
                    for k in range(16):
                        mm = t.matmul(ps_s[j % 2][:, :], w[:, k, hh * 128:(hh + 1) * 128], xt[:, k, :],
                                      start=(k == 0), stop=(k == 15))
                    ctx.inc(mm, "pj_t", 1)

                @blk.vector
                def _(v, j=j, kind=kind, hh=hh, n=n):
                    ctx.wait(v, "pj_t")
                    dst = qf if kind == "q" else kf
                    ctx.inc(v.tensor_copy(dst[:, hh, n * 512:(n + 1) * 512], ps_s[j % 2][:, :]), "pj_v", 1)

            for ts in range(4):
                @blk.tensor
                def _(t, ts=ts):
                    ctx.wait(t, "pj_v")
                    if ts >= 1:
                        ctx.wait(t, "pv_v")
                    for k in range(16):
                        mm = t.matmul(ps_m[:, :], xt[:, k, ts * 128:(ts + 1) * 128], wv_s[:, k, :],
                                      start=(k == 0), stop=(k == 15))
                    ctx.inc(mm, "pv_t", 1)

                @blk.vector
                def _(v, ts=ts, n=n):
                    ctx.wait(v, "pv_t")
                    for hh in range(4):
                        last = v.tensor_copy(vt[:, hh, n * 4 + ts, 0:128], ps_m[:, hh * 128:(hh + 1) * 128])
                    ctx.inc(last, "pv_v", 1)

        @blk.vector
        def _(v):
            ctx.wait(v, "pv_v")
            for hh in range(4):
                v.tensor_reduce(stat[:, hh:hh + 1], qf[:, hh, :], AX.X, ALU.max, apply_absolute_value=True)
            for hh in range(4):
                last = v.tensor_reduce(stat[:, 4 + hh:5 + hh], kf[:, hh, :], AX.X, ALU.max, apply_absolute_value=True)
            ctx.inc(last, "m_v", 1)

        @blk.tensor
        def _(t):
            ctx.wait(t, "m_v")
            for c in range(8):
                bank = ps_s[c // 4]
                mm = t.transpose(bank[0:1, (c % 4) * 128:(c % 4 + 1) * 128], stat[:, c:c + 1], idf_s[:])
            ctx.inc(mm, "m_t", 1)

        @blk.vector
        def _(v):
            ctx.wait(v, "m_t")
            for c in range(8):
                bank = ps_s[c // 4]
                last = v.tensor_reduce(nrm[0:1, 8 + c:9 + c], bank[0:1, (c % 4) * 128:(c % 4 + 1) * 128], AX.X, ALU.max)
            ctx.inc(last, "m_v2", 1)

        @blk.scalar
        def _(sc):
            ctx.wait(sc, "m_v2")
            ctx.inc(sc.activation(nrm[0:1, 8:16], nrm[0:1, 8:16], AF.Ln), "m_s", 1)

        @blk.vector
        def _(v):
            ctx.wait(v, "m_s")
            inst = v.tensor_tensor(nrm[0:1, 16:20], nrm[0:1, 8:12], nrm[0:1, 12:16], ALU.add)
            ctx.vsync(v, inst)
            ctx.inc(v.tensor_scalar(nrm[0:1, 20:24], nrm[0:1, 16:20], LN8 + math.log(64.0), -1.0, ALU.add, ALU.mult), "m_v3", 1)

        @blk.tensor
        def _(t):
            ctx.wait(t, "m_v3")
            ctx.inc(t.matmul(ps_m[:, 0:4], aof[0:1, :], nrm[0:1, 20:24], start=True, stop=True), "m_t2", 1)

        @blk.vector
        def _(v):
            ctx.wait(v, "m_v4" if False else "m_t2")
            ctx.inc(v.tensor_copy(negc[:], ps_m[:, 0:4]), "m_v4", 1)

        for hh in range(4):
            for qb in range(NQB):
                for kt in range(NKT):
                    @blk.tensor
                    def _(t, hh=hh, qb=qb, kt=kt):
                        if hh == 0 and qb == 0 and kt == 0:
                            ctx.wait(t, "m_v4")
                        else:
                            ctx.wait(t, "a_s")
                            if kt == 0:
                                ctx.wait(t, "o_v")
                        ksl = slice(kt * 128, (kt + 1) * 128)
                        qsl = slice(qb * 512, (qb + 1) * 512)
                        t.matmul(ps_s[0][:, :], kf[0:64, hh, ksl], qf[0:64, hh, qsl],
                                 start=True, stop=True, tile_position=(0, 0))
                        ctx.inc(t.matmul(ps_s[1][:, :], kf[64:128, hh, ksl], qf[64:128, hh, qsl],
                                         start=True, stop=True, tile_position=(64, 0)), "a_t", 1)

                    @blk.scalar
                    def _(sc, hh=hh):
                        ctx.wait(sc, "a_t")
                        sc.activation(p_sb[:, 0, :], ps_s[0][:, :], AF.Exp, scale=0.125, bias=negc[:, hh:hh + 1])
                        ctx.inc(sc.activation(p_sb[:, 1, :], ps_s[1][:, :], AF.Exp, scale=0.125, bias=negc[:, hh:hh + 1]), "a_s", 1)

                    @blk.tensor
                    def _(t, hh=hh, kt=kt):
                        ctx.wait(t, "a_s")
                        for sl in range(4):
                            for a in range(2):
                                mm = t.matmul(ps_o[sl][:, a, 0:129], p_sb[:, a, sl * 128:(sl + 1) * 128],
                                              vt[:, hh, kt, 0:129], start=(kt == 0), stop=(kt == NKT - 1))
                        ctx.inc(mm, "a_t2", 1)

                @blk.vector
                def _(v):
                    ctx.wait(v, "a_t2")
                    for sl in range(4):
                        v.reciprocal(stat[:, sl:sl + 1], ps_o[sl][:, 0, 128:129])
                        v.reciprocal(stat[:, 4 + sl:5 + sl], ps_o[sl][:, 1, 128:129])
                    for sl in range(4):
                        v.tensor_scalar_mul(att1[:, sl, :], ps_o[sl][:, 0, 0:128], stat[:, sl:sl + 1])
                        v.tensor_scalar_mul(att2[:, sl, :], ps_o[sl][:, 1, 0:128], stat[:, 4 + sl:5 + sl])
                    for sl in range(4):
                        last = v.scalar_tensor_tensor(attc[:, sl, :], att2[:, sl, :], lamv[:, 0:1],
                                                      att1[:, sl, :], ALU.mult, ALU.add)
                    ctx.inc(last, "g_v", 1)

                @blk.vector
                def _(v):
                    ctx.wait(v, "g_v")
                    for sl in range(4):
                        last = v.tensor_reduce(stat[:, 8 + sl:9 + sl], attc[:, sl, :], AX.X, ALU.add)
                    ctx.inc(last, "g_v2", 1)

                @blk.scalar
                def _(sc):
                    ctx.wait(sc, "g_v2")
                    for sl in range(4):
                        last = sc.activation(sdump[:, sl * 128:(sl + 1) * 128], attc[:, sl, :], AF.Square,
                                             accum_out=stat[:, 12 + sl:13 + sl])
                    ctx.inc(last, "g_s", 1)

                @blk.vector
                def _(v):
                    ctx.wait(v, "g_s")
                    v.tensor_scalar_mul(st2[:, 0:4], stat[:, 8:12], 1.0 / 128.0)
                    inst = v.tensor_scalar(st2[:, 4:8], stat[:, 12:16], 1.0 / 128.0, 1e-5, ALU.mult, ALU.add)
                    ctx.vsync(v, inst)
                    inst = v.tensor_tensor(st2[:, 8:12], st2[:, 0:4], st2[:, 0:4], ALU.mult)
                    ctx.vsync(v, inst)
                    ctx.inc(v.tensor_tensor(st2[:, 4:8], st2[:, 4:8], st2[:, 8:12], ALU.subtract), "g_v3", 1)

                @blk.scalar
                def _(sc):
                    ctx.wait(sc, "g_v3")
                    ctx.inc(sc.activation(st2[:, 4:8], st2[:, 4:8], AF.Sqrt), "g_s2", 1)

                @blk.vector
                def _(v, hh=hh):
                    ctx.wait(v, "g_s2")
                    for sl in range(4):
                        v.tensor_single_scalar(attc[:, sl, :], attc[:, sl, :], st2[:, sl:sl + 1], ALU.subtract)
                    for sl in range(4):
                        v.tensor_scalar_mul(attc[:, sl, :], attc[:, sl, :], st2[:, 4 + sl:5 + sl])
                    for sl in range(4):
                        v.tensor_tensor(attc[:, sl, :], attc[:, sl, :], gnw_s[:, hh, :], ALU.mult)
                    for sl in range(4):
                        last = v.tensor_tensor(attc[:, sl, :], attc[:, sl, :], gnb_s[:, hh, :], ALU.add)
                    ctx.inc(last, "g_v4", 1)

                @blk.sync
                def _(s, hh=hh, qb=qb):
                    ctx.wait(s, "g_v4")
                    ctx.inc(s.dma_start(
                        out=att_o[qb * 512:(qb + 1) * 512, hh * 128:(hh + 1) * 128].rearrange("(c p) v -> p c v", p=128),
                        in_=attc[:]), "o_st", 16)

                @blk.vector
                def _(v):
                    ctx.wait(v, "o_st")
                    ctx.inc(v.memset(stat[:, 0:1], 0.0), "o_v", 1)

        @blk.gpsimd
        def _(g):
            ctx.wait(g, "o_st")
    return nc



def build_l4():
    TOK = 1024
    ST = 512
    NST = TOK // ST
    NELEM = float(INTER * D)
    nc = bass.Bass("TRN2", target_bir_lowering=False)
    att = nc.dram_tensor("att", [D, TOK], BF16, kind="ExternalInput").ap()
    x2 = nc.dram_tensor("x2", [D, TOK], F32, kind="ExternalInput").ap()
    wo = nc.dram_tensor("wo", [D, D], F32, kind="ExternalInput").ap()
    t1T = nc.dram_tensor("t1T", [D, INTER], BF16, kind="ExternalInput").ap()
    t2T = nc.dram_tensor("t2T", [D, INTER], BF16, kind="ExternalInput").ap()
    t3T = nc.dram_tensor("t3T", [INTER, D], BF16, kind="ExternalInput").ap()
    wsums = nc.dram_tensor("wsums", [8, 3], F32, kind="ExternalInput").ap()
    allones = nc.dram_tensor("allones", [128, 128], F32, kind="ExternalInput").ap()
    ident = nc.dram_tensor("ident", [128, 128], F32, kind="ExternalInput").ap()
    x4_o = nc.dram_tensor("x4", [D, TOK], F32, kind="ExternalOutput").ap()

    es = contextlib.ExitStack()
    with es:
        ctx = Ctx(nc, es)
        att_s = ctx.sb("att_s", [128, 16, ST], BF16)
        x_s = ctx.sb("x_s", [128, 16, ST], F32)
        kq_s = ctx.sb("kq_s", [128, 16, ST], BF16)
        h_s = ctx.sb("h_s", [128, 44, ST], BF16)
        xa_s = ctx.sb("xa_s", [128, ST], BF16)
        gt_s = ctx.sb("gt_s", [128, ST], BF16)
        ut_s = ctx.sb("ut_s", [128, ST], BF16)
        wo_b = [ctx.sb(f"wo_b{i}", [128, 16, 128], BF16) for i in range(2)]
        wt_b = [ctx.sb(f"wt_b{i}", [128, 16, 128], BF16) for i in range(2)]
        wu_b = [ctx.sb(f"wu_b{i}", [128, 16, 128], BF16) for i in range(2)]
        wd_b = [ctx.sb(f"wd_b{i}", [128, 44, 128], BF16) for i in range(2)]
        id_b = ctx.sb("id_b", [128, 128], BF16)
        idf_s = ctx.sb("idf_s", [128, 128], F32)
        aof = ctx.sb("aof", [128, 128], F32)
        ao_b = ctx.sb("ao_b", [128, 128], BF16)
        ws_s = ctx.sb("ws_s", [8, 3], F32)
        mrow = ctx.sb("mrow", [1, 16], F32)
        mcol = ctx.sb("mcol", [128, 3], F32)
        sqf = ctx.sb("sqf", [128, 4, ST], F32)
        rstd_t = ctx.sb("rstd_t", [128, ST], F32)
        s127_t = ctx.sb("s127_t", [128, ST], F32)
        dq_t = ctx.sb("dq_t", [128, ST], F32)
        amax = ctx.sb("amax", [128, 4], F32)
        tmax = ctx.sb("tmax", [128, 4], F32)
        rowa = ctx.sb("rowa", [1, ST], F32)
        t1f = ctx.sb("t1f", [128, 2, ST], F32)
        kib = ctx.sb("kib", [128, 2, ST], I32)
        kfb = ctx.sb("kfb", [128, 2, ST], F32)
        cbf = ctx.sb("cbf", [128, 2, ST], F32)
        pb = [ctx.ps(f"pb{i}", [128, 512]) for i in range(6)]
        pvt = ctx.ps("pvt", [128, 8, 128], BF16)
        blk = es.enter_context(nc.Block())

        @blk.gpsimd
        def _(g):
            for dst, src_ in [(aof, allones), (ao_b, allones), (id_b, ident),
                              (idf_s, ident), (ws_s, wsums)]:
                ctx.inc(g.dma_start(out=dst[:], in_=src_), "c_ld", 16)

        @blk.tensor
        def _(t):
            ctx.wait(t, "c_ld")
            ctx.inc(t.matmul(pb[0][0:1, 0:3], aof[0:8, 0:1], ws_s[:], start=True, stop=True), "i_t", 1)

        @blk.vector
        def _(v):
            ctx.wait(v, "i_t")
            ctx.inc(v.tensor_scalar_mul(mrow[0:1, 0:3], pb[0][0:1, 0:3], 1.0 / NELEM), "i_v", 1)

        @blk.tensor
        def _(t):
            ctx.wait(t, "i_v")
            ctx.inc(t.matmul(pb[1][:, 0:3], aof[0:1, :], mrow[0:1, 0:3], start=True, stop=True), "i_t2", 1)

        @blk.vector
        def _(v):
            ctx.wait(v, "i_t2")
            ctx.inc(v.tensor_copy(mcol[:], pb[1][:, 0:3]), "i_v2", 1)

        def emit_quant(tag, nch, src_s, dst_s):
            for pr in range((nch + 1) // 2):
                @blk.vector
                def _(v, pr=pr, tag=tag, nch=nch):
                    ctx.wait(v, f"{tag}_pre")
                    ids = [2 * pr + i for i in range(2) if 2 * pr + i < nch]
                    for i, kk in enumerate(ids):
                        v.tensor_tensor(t1f[:, i, :], src_s[:, kk, :], s127_t[:], ALU.mult)
                    for i in range(len(ids)):
                        v.tensor_copy(kib[:, i, :], t1f[:, i, :])
                    for i in range(len(ids)):
                        v.tensor_copy(kfb[:, i, :], kib[:, i, :])
                    for i in range(len(ids)):
                        v.tensor_tensor(t1f[:, i, :], t1f[:, i, :], kfb[:, i, :], ALU.subtract)
                    for i in range(len(ids)):
                        v.scalar_tensor_tensor(cbf[:, i, :], t1f[:, i, :], 0.5, kfb[:, i, :], ALU.is_gt, ALU.add)
                    for i in range(len(ids)):
                        v.scalar_tensor_tensor(kfb[:, i, :], t1f[:, i, :], -0.5, cbf[:, i, :], ALU.is_lt, ALU.subtract)
                    for i, kk in enumerate(ids):
                        last = v.tensor_tensor(dst_s[:, kk, :], kfb[:, i, :], dq_t[:], ALU.mult)
                    ctx.inc(last, f"{tag}_q", 1)

        def emit_norm_amax(tag, nch, src_s, isrms):
            ao_use = ao_b if src_s.dtype == BF16 else aof
            for grp in range((nch + 3) // 4):
                @blk.scalar
                def _(sc, grp=grp, tag=tag, nch=nch):
                    ctx.wait(sc, f"{tag}_pre0")
                    if grp > 0:
                        ctx.wait(sc, f"{tag}_t")
                    ids = [grp * 4 + i for i in range(4) if grp * 4 + i < nch]
                    for i, kk in enumerate(ids):
                        last = sc.activation(sqf[:, i, :], src_s[:, kk, :], AF.Square)
                    ctx.inc(last, f"{tag}_s", 1)

                @blk.tensor
                def _(t, grp=grp, tag=tag, nch=nch):
                    ctx.wait(t, f"{tag}_s")
                    ids = [grp * 4 + i for i in range(4) if grp * 4 + i < nch]
                    for i, kk in enumerate(ids):
                        mm = t.matmul(pb[5][:, :], aof[:], sqf[:, i, :], start=(kk == 0), stop=(kk == nch - 1))
                    ctx.inc(mm, f"{tag}_t", 1)

            @blk.vector
            def _(v, tag=tag):
                ctx.wait(v, f"{tag}_t")
                ctx.inc(v.tensor_scalar(rstd_t[:], pb[5][:, :], 1.0 / (nch * 128.0), 1e-6, ALU.mult, ALU.add), f"{tag}_v", 1)

            @blk.scalar
            def _(sc, tag=tag):
                ctx.wait(sc, f"{tag}_v")
                ctx.inc(sc.activation(rstd_t[:], rstd_t[:], AF.Sqrt), f"{tag}_s2", 1)

            @blk.vector
            def _(v, tag=tag):
                ctx.wait(v, f"{tag}_s2")
                ctx.inc(v.memset(amax[:], 0.0), f"{tag}_v2", 1)

            for kk in range(nch):
                @blk.scalar
                def _(sc, kk=kk, tag=tag):
                    ctx.wait(sc, f"{tag}_v2")
                    if kk > 0:
                        ctx.wait(sc, f"{tag}_tt")
                    ctx.inc(sc.activation(xa_s[:], src_s[:, kk, :], AF.Abs), f"{tag}_as", 1)

                @blk.tensor
                def _(t, kk=kk, tag=tag):
                    ctx.wait(t, f"{tag}_as")
                    if kk > 0:
                        ctx.wait(t, f"{tag}_av")
                    for ts in range(4):
                        mm = t.transpose(pvt[:, ts, :], xa_s[:, ts * 128:(ts + 1) * 128], id_b[:])
                    ctx.inc(mm, f"{tag}_tt", 1)

                @blk.vector
                def _(v, kk=kk, tag=tag):
                    ctx.wait(v, f"{tag}_tt")
                    for ts in range(4):
                        v.tensor_reduce(tmax[:, ts:ts + 1], pvt[:, ts, :], AX.X, ALU.max)
                    for ts in range(4):
                        last = v.tensor_tensor(amax[:, ts:ts + 1], amax[:, ts:ts + 1], tmax[:, ts:ts + 1], ALU.max)
                    ctx.inc(last, f"{tag}_av", 1)

            @blk.tensor
            def _(t, tag=tag):
                ctx.wait(t, f"{tag}_av")
                for ts in range(4):
                    mm = t.transpose(pb[4][0:1, ts * 128:(ts + 1) * 128], amax[:, ts:ts + 1], idf_s[:])
                ctx.inc(mm, f"{tag}_t2", 1)

            @blk.vector
            def _(v, tag=tag):
                ctx.wait(v, f"{tag}_t2")
                inst = v.tensor_scalar_max(rowa[0:1, :], pb[4][0:1, 0:512], 1e-5)
                ctx.vsync(v, inst)
                ctx.inc(v.reciprocal(rowa[0:1, :], rowa[0:1, :]), f"{tag}_v3", 1)

            @blk.tensor
            def _(t, tag=tag):
                ctx.wait(t, f"{tag}_v3")
                ctx.inc(t.matmul(pb[4][:, :], aof[0:1, :], rowa[0:1, :], start=True, stop=True), f"{tag}_t3", 1)

            @blk.vector
            def _(v, tag=tag):
                ctx.wait(v, f"{tag}_t3")
                inst = v.tensor_scalar_mul(s127_t[:], pb[4][:, :], 127.0)
                ctx.vsync(v, inst)
                inst = v.reciprocal(t1f[:, 0, :], s127_t[:])
                ctx.vsync(v, inst)
                v.reciprocal(rstd_t[:], rstd_t[:])
                ctx.inc(v.tensor_tensor(dq_t[:], rstd_t[:], t1f[:, 0, :], ALU.mult), f"{tag}_pre", 1)

        for st in range(NST):
            t0 = st * ST

            @blk.gpsimd
            def _(g, t0=t0, st=st):
                ctx.wait(g, "i_v2")
                if st > 0:
                    ctx.wait(g, "out_st")
                ctx.inc(g.dma_start(out=att_s[:], in_=att.rearrange("(c p) s -> p c s", p=128)[:, :, t0:t0 + ST]), "in_ld", 16)
                ctx.inc(g.dma_start(out=x_s[:], in_=x2.rearrange("(c p) s -> p c s", p=128)[:, :, t0:t0 + ST]), "in_ld", 16)

            for m in range(16):
                @blk.gpsimd
                def _(g, m=m):
                    ctx.wait(g, "in_ld")
                    if m >= 2:
                        ctx.wait(g, "woq_t")
                    ctx.inc(g.dma_start(out=wo_b[m % 2][:], in_=wo[m]), "woq_ld", 16)

                @blk.tensor
                def _(t, m=m):
                    ctx.wait(t, "woq_ld")
                    if m >= 2:
                        ctx.wait(t, "woq_v")
                    for k in range(16):
                        mm = t.matmul(pb[m % 2][:, :], wo_b[m % 2][:, k, :], att_s[:, k, :],
                                      start=(k == 0), stop=(k == 15))
                    ctx.inc(mm, "woq_t", 1)

                @blk.vector
                def _(v, m=m):
                    ctx.wait(v, "woq_t")
                    ctx.inc(v.tensor_tensor(x_s[:, m, :], x_s[:, m, :], pb[m % 2][:, :], ALU.add), "woq_v", 1)

            @blk.vector
            def _(v):
                ctx.wait(v, "woq_v")
                ctx.inc(v.memset(sqf[:, 0, 0:1], 0.0), "q1_pre0", 1)
            emit_norm_amax("q1", 16, x_s, True)
            emit_quant("q1", 16, x_s, kq_s)

            for m in range(44):
                @blk.sync
                def _(s, m=m):
                    ctx.wait(s, "q1_q")
                    if m >= 2:
                        ctx.wait(s, "gu_t")
                    ctx.inc(s.dma_start(out=wt_b[m % 2][:], in_=t1T[m]), "gu_ld", 16)
                    ctx.inc(s.dma_start(out=wu_b[m % 2][:], in_=t2T[m]), "gu_ld", 16)

                @blk.tensor
                def _(t, m=m):
                    ctx.wait(t, "gu_ld")
                    if m >= 1:
                        ctx.wait(t, "h_v")
                    for k in range(16):
                        t.matmul(pb[2][:, :], wt_b[m % 2][:, k, :], kq_s[:, k, :], start=(k == 0), stop=(k == 15))
                    for k in range(16):
                        mm = t.matmul(pb[3][:, :], wu_b[m % 2][:, k, :], kq_s[:, k, :], start=(k == 0), stop=(k == 15))
                    ctx.inc(mm, "gu_t", 1)

                @blk.scalar
                def _(sc, m=m):
                    ctx.wait(sc, "gu_t")
                    sc.activation(gt_s[:], pb[2][:, :], AF.Silu, scale=mcol[:, 0:1])
                    ctx.inc(sc.activation(ut_s[:], pb[3][:, :], AF.Copy, scale=mcol[:, 1:2]), "gu_s", 1)

                @blk.vector
                def _(v, m=m):
                    ctx.wait(v, "gu_s")
                    ctx.inc(v.tensor_tensor(h_s[:, m, :], gt_s[:], ut_s[:], ALU.mult), "h_v", 1)

            @blk.vector
            def _(v):
                ctx.wait(v, "h_v")
                ctx.inc(v.memset(sqf[:, 0, 0:1], 0.0), "q2_pre0", 1)
            emit_norm_amax("q2", 44, h_s, True)
            emit_quant("q2", 44, h_s, h_s)

            for m in range(16):
                @blk.sync
                def _(s, m=m):
                    ctx.wait(s, "q2_q")
                    if m >= 2:
                        ctx.wait(s, "dn_t")
                    ctx.inc(s.dma_start(out=wd_b[m % 2][:], in_=t3T[m]), "dn_ld", 16)

                @blk.tensor
                def _(t, m=m):
                    ctx.wait(t, "dn_ld")
                    if m >= 2:
                        ctx.wait(t, "dn_v")
                    for k in range(44):
                        mm = t.matmul(pb[m % 2][:, :], wd_b[m % 2][:, k, :], h_s[:, k, :],
                                      start=(k == 0), stop=(k == 43))
                    ctx.inc(mm, "dn_t", 1)

                @blk.vector
                def _(v, m=m):
                    ctx.wait(v, "dn_t")
                    ctx.inc(v.scalar_tensor_tensor(x_s[:, m, :], pb[m % 2][:, :], mcol[:, 2:3],
                                                   x_s[:, m, :], ALU.mult, ALU.add), "dn_v", 1)

            @blk.sync
            def _(s, st=st):
                ctx.wait(s, "dn_v")
                ctx.inc(s.dma_start(out=x4_o[st], in_=x_s[:]), "out_st", 16)

        @blk.gpsimd
        def _(g):
            ctx.wait(g, "out_st")
    return nc


def build_l3l4():
    TOK = 1024
    ST = 512
    NST = 2
    SFULL = 4096
    NKT = SFULL // 128
    LN8 = math.log(8.0)
    NELEM = float(INTER * D)
    nc = bass.Bass("TRN2", target_bir_lowering=False)
    ln2 = nc.dram_tensor("ln2", [8, 128, 16, 512], BF16, kind="ExternalInput").ap()
    x2 = nc.dram_tensor("x2", [NST, 128, 16, ST], F32, kind="ExternalInput").ap()
    wkT = nc.dram_tensor("wkT", [2, 128, 16, 1024], F32, kind="ExternalInput").ap()
    wvT = nc.dram_tensor("wvT", [2, 128, 16, 1024], F32, kind="ExternalInput").ap()
    wqT = nc.dram_tensor("wqT", [2, 128, 16, 1024], F32, kind="ExternalInput").ap()
    lq1 = nc.dram_tensor("lq1", [1, DH1], F32, kind="ExternalInput").ap()
    lk1 = nc.dram_tensor("lk1", [1, DH1], F32, kind="ExternalInput").ap()
    lq2 = nc.dram_tensor("lq2", [1, DH1], F32, kind="ExternalInput").ap()
    lk2 = nc.dram_tensor("lk2", [1, DH1], F32, kind="ExternalInput").ap()
    gnw = nc.dram_tensor("gnw", [128, 16, 128], F32, kind="ExternalInput").ap()
    gnb = nc.dram_tensor("gnb", [128, 16, 128], F32, kind="ExternalInput").ap()
    wo = nc.dram_tensor("wo", [16, 128, 16, 128], F32, kind="ExternalInput").ap()
    t1T = nc.dram_tensor("t1T", [44, 128, 16, 128], BF16, kind="ExternalInput").ap()
    t2T = nc.dram_tensor("t2T", [44, 128, 16, 128], BF16, kind="ExternalInput").ap()
    t3T = nc.dram_tensor("t3T", [16, 128, 44, 128], BF16, kind="ExternalInput").ap()
    wsums = nc.dram_tensor("wsums", [8, 3], F32, kind="ExternalInput").ap()
    allones = nc.dram_tensor("allones", [128, 128], F32, kind="ExternalInput").ap()
    ident = nc.dram_tensor("ident", [128, 128], F32, kind="ExternalInput").ap()
    x4_o = nc.dram_tensor("x4", [NST, 128, 16, ST], F32, kind="ExternalOutput").ap()
    kf_d = nc.dram_tensor("kf_d", [D, SFULL], BF16, kind="Internal").ap()
    v_qd = nc.dram_tensor("v_qd", [4, 128, 32, 512], BF16, kind="Internal").ap()
    qf_d = nc.dram_tensor("qf_d", [D, TOK], BF16, kind="Internal").ap()
    att_t = nc.dram_tensor("att_t", [NST, 128, 16, ST], BF16, kind="Internal").ap()

    es = contextlib.ExitStack()
    with es:
        ctx = Ctx(nc, es)
        ab = ctx.sb("ab", [128, 32768], BF16)
        cw = ctx.sb("cw", [128, 19456], BF16)
        att_s = ctx.sb("att_s", [128, 16, 512], BF16)
        x_s = ctx.sb("x_s", [128, 16, ST], F32)
        wph = [ab[:, 0:16384].rearrange("p (c q) -> p c q", c=16),
               cw[:, 0:16384].rearrange("p (c q) -> p c q", c=16)]
        xt_b = [ab[:, 16384:24576].rearrange("p (c q) -> p c q", c=16),
                att_s[:, :, :]]
        kf_h = ab[:, 0:8192].rearrange("p (b s) -> p b s", b=2)
        qf_h = ab[:, 8192:10240].rearrange("p (b s) -> p b s", b=2)
        vt_quad = ab[:, 10240:26624].rearrange("p (k f) -> p k f", k=NKT)
        h_s = ab[:, 0:22528].rearrange("p (c q) -> p c q", c=44)
        kq_s = ab[:, 22528:30720].rearrange("p (c q) -> p c q", c=16)
        wt_b = [cw[:, 0:2048].rearrange("p (c q) -> p c q", c=16),
                cw[:, 2048:4096].rearrange("p (c q) -> p c q", c=16)]
        wu_b = [cw[:, 4096:6144].rearrange("p (c q) -> p c q", c=16),
                cw[:, 6144:8192].rearrange("p (c q) -> p c q", c=16)]
        wd_b = [cw[:, 8192:13824].rearrange("p (c q) -> p c q", c=44),
                cw[:, 13824:19456].rearrange("p (c q) -> p c q", c=44)]
        wo_b = [ctx.sb(f"wo_b{i}", [128, 16, 128], BF16) for i in range(2)]
        kout = ctx.sb("kout", [128, 2, 512], BF16)
        vout = ctx.sb("vout", [128, 2, 1024], BF16)
        gnw_s = ctx.sb("gnw_s", [128, 16, 128], BF16)
        gnb_s = ctx.sb("gnb_s", [128, 16, 128], BF16)
        lt = ctx.sb("lt", [1, 4 * DH1], F32)
        nrm = ctx.sb("nrm", [1, 64], F32)
        lamv = ctx.sb("lamv", [128, 1], F32)
        negc = ctx.sb("negc", [128, 16], F32)
        maxk = ctx.sb("maxk", [128, 16], F32)
        maxq = ctx.sb("maxq", [128, 16], F32)
        redt = ctx.sb("redt", [128, 2], F32)
        p_sb = ctx.sb("p_sb", [128, 2, 1024], BF16)
        attc = ctx.sb("attc", [128, 4, 128], BF16)
        att1 = ctx.sb("att1", [128, 4, 128], F32)
        stat = ctx.sb("stat", [128, 16], F32)
        st2 = ctx.sb("st2", [128, 16], F32)
        rsum = ctx.sb("rsum", [1, 1024], F32)
        xa_s = ctx.sb("xa_s", [128, ST], BF16)
        gt_s = ctx.sb("gt_s", [128, ST], BF16)
        ut_s = ctx.sb("ut_s", [128, ST], BF16)
        sdump = xa_s[:, :]
        atstg = gt_s[:].rearrange("p (a q) -> p a q", a=4)
        id_b = ctx.sb("id_b", [128, 128], BF16)
        idf_s = ctx.sb("idf_s", [128, 128], F32)
        aof = ctx.sb("aof", [128, 128], F32)
        ao_b = ctx.sb("ao_b", [128, 128], BF16)
        ws_s = ctx.sb("ws_s", [8, 3], F32)
        mrow = ctx.sb("mrow", [1, 16], F32)
        mcol = ctx.sb("mcol", [128, 3], F32)
        sqf = ctx.sb("sqf", [128, 4, ST], F32)
        rstd_t = ctx.sb("rstd_t", [128, ST], F32)
        att2 = rstd_t[:].rearrange("p (a q) -> p a q", a=4)
        s127_t = sqf[:, 0, :]
        dq_t = sqf[:, 1, :]
        cbf = sqf[:, 2, :]
        amax = ctx.sb("amax", [128, 4], F32)
        tmax = ctx.sb("tmax", [128, 4], F32)
        rowa = ctx.sb("rowa", [1, ST], F32)
        t1f = ctx.sb("t1f", [128, 1, ST], F32)
        kib = ctx.sb("kib", [128, 1, ST], I32)
        kfb = ctx.sb("kfb", [128, 1, ST], F32)
        psc_ev = ctx.ps("psc_ev", [128, 1024])
        psc_od = ctx.ps("psc_od", [128, 1024])
        pv4 = ctx.ps("pv4", [128, 2048])
        pb0 = psc_ev[:, 0:512]
        pb1 = psc_ev[:, 512:1024]
        l4pb = [pb0, pb1,
                pv4[:, 0:512], pv4[:, 512:1024],
                pv4[:, 1024:1536], pv4[:, 1536:2048]]

        def pso(sl, a):
            off = (sl * 2 + a) * 128
            return pv4[:, off:off + 128]
        prow = pv4[0:1, 1024:2048]

        blk = es.enter_context(nc.Block())

        @blk.gpsimd
        def _(g):
            for dst, src_ in [(aof, allones), (ao_b, allones), (id_b, ident),
                              (idf_s, ident), (ws_s, wsums),
                              (gnw_s, gnw), (gnb_s, gnb)]:
                ctx.inc(g.dma_start(out=dst[:], in_=src_), "c_ld", 16)
            for i, src_ in enumerate([lq1, lk1, lq2, lk2]):
                ctx.inc(g.dma_start(out=lt[0:1, i * DH1:(i + 1) * DH1], in_=src_), "c_ld", 16)

        @blk.vector
        def _(v):
            ctx.wait(v, "c_ld")
            v.tensor_tensor(lt[0:1, 0:DH1], lt[0:1, 0:DH1], lt[0:1, DH1:2 * DH1], ALU.mult)
            inst = v.tensor_tensor(lt[0:1, 2 * DH1:3 * DH1], lt[0:1, 2 * DH1:3 * DH1], lt[0:1, 3 * DH1:4 * DH1], ALU.mult)
            ctx.vsync(v, inst)
            v.tensor_reduce(nrm[0:1, 48:49], lt[0:1, 0:DH1], AX.X, ALU.add)
            inst = v.tensor_reduce(nrm[0:1, 49:50], lt[0:1, 2 * DH1:3 * DH1], AX.X, ALU.add)
            ctx.vsync(v, inst)
            v.tensor_scalar_mul(gnw_s[:], gnw_s[:], 1.0 - LAMBDA_INIT)
            v.tensor_scalar_mul(gnb_s[:], gnb_s[:], 1.0 - LAMBDA_INIT)
            v.memset(maxk[:], 0.0)
            ctx.inc(v.memset(maxq[:], 0.0), "l_v", 1)

        @blk.scalar
        def _(sc):
            ctx.wait(sc, "l_v")
            ctx.inc(sc.activation(nrm[0:1, 48:50], nrm[0:1, 48:50], AF.Exp), "l_s", 1)

        @blk.vector
        def _(v):
            ctx.wait(v, "l_s")
            inst = v.tensor_tensor(nrm[0:1, 50:51], nrm[0:1, 48:49], nrm[0:1, 49:50], ALU.subtract)
            ctx.vsync(v, inst)
            ctx.inc(v.tensor_scalar(nrm[0:1, 51:52], nrm[0:1, 50:51], LAMBDA_INIT, -1.0, ALU.add, ALU.mult), "l_v2", 1)

        @blk.tensor
        def _(t):
            ctx.wait(t, "l_v2")
            ctx.inc(t.matmul(pb0[:, 0:1], aof[0:1, :], nrm[0:1, 51:52], start=True, stop=True), "l_t", 1)

        @blk.vector
        def _(v):
            ctx.wait(v, "l_t")
            ctx.inc(v.tensor_copy(lamv[:], pb0[:, 0:1]), "l_v3", 1)

        @blk.tensor
        def _(t):
            ctx.wait(t, "l_v3")
            ctx.inc(t.matmul(pb1[0:1, 0:3], aof[0:8, 0:1], ws_s[:], start=True, stop=True), "i_t", 1)

        @blk.vector
        def _(v):
            ctx.wait(v, "i_t")
            ctx.inc(v.tensor_scalar_mul(mrow[0:1, 0:3], pb1[0:1, 0:3], 1.0 / NELEM), "i_v", 1)

        @blk.tensor
        def _(t):
            ctx.wait(t, "i_v")
            ctx.inc(t.matmul(pb1[:, 0:3], aof[0:1, :], mrow[0:1, 0:3], start=True, stop=True), "i_t2", 1)

        @blk.vector
        def _(v):
            ctx.wait(v, "i_v2" if False else "i_t2")
            ctx.inc(v.tensor_copy(mcol[:], pb1[:, 0:3]), "i_v2", 1)

        agd = {"wph": [0, 0], "xt": [0, 0], "ps": [0, 0, 0, 0],
               "psx": [0, 0], "kst": [0, 0], "vst": [0, 0]}

        def emit_proj(kind, W, nblocks):
            for half in range(2):
                @blk.gpsimd
                def _(g, half=half, W=W):
                    ctx.wait(g, "i_v2")
                    ctx.wait_val(g, "a_t", agd["wph"][half % 2])
                    ctx.inc(g.dma_start(
                        out=wph[half % 2][:, :, 0:1024], in_=W[half]),
                        "aw_ld", 16)

                for n in range(nblocks):
                    @blk.sync
                    def _(s, n=n):
                        ctx.wait(s, "i_v2")
                        ctx.wait_val(s, "a_t", agd["xt"][n % 2])
                        ctx.inc(s.dma_start(
                            out=xt_b[n % 2][:, :, :], in_=ln2[n]),
                            "xt_ld", 16)

                    if kind in ("k", "q"):
                        for ich in range(8):
                            @blk.tensor
                            def _(t, n=n, half=half, ich=ich):
                                ctx.wait(t, "aw_ld", "xt_ld")
                                if ctx.semv.get("a_t", 0) == 0:
                                    ctx.wait(t, "i_v2", "l_v3")
                                ctx.wait_val(t, "a_sc", agd["ps"][ich % 2])
                                ctx.wait_val(t, "a_mx", agd["psx"][ich % 2])
                                dstp = (pb0 if ich % 2 == 0 else pb1)
                                for kc in range(16):
                                    mm = t.matmul(dstp[:, :], wph[half % 2][:, kc, ich * 128:(ich + 1) * 128],
                                                  xt_b[n % 2][:, kc, :], start=(kc == 0), stop=(kc == 15))
                                ctx.inc(mm, "a_t", 1)

                            @blk.scalar
                            def _(sc, ich=ich):
                                ctx.wait(sc, "a_t")
                                ctx.wait_val(sc, "a_ev", agd["kst"][ich % 2])
                                ctx.inc(sc.activation(kout[:, ich % 2, :],
                                                      (pb0 if ich % 2 == 0 else pb1)[:, :], AF.Copy), "a_sc", 1)

                            @blk.vector
                            def _(v, half=half, ich=ich, kind=kind):
                                ctx.wait(v, "a_t")
                                col = half * 8 + ich
                                dst = maxk if kind == "k" else maxq
                                inst = v.tensor_reduce(redt[:, ich % 2:ich % 2 + 1],
                                                       (pb0 if ich % 2 == 0 else pb1)[:, :], AX.X, ALU.max,
                                                       apply_absolute_value=True)
                                ctx.vsync(v, inst)
                                ctx.inc(v.tensor_tensor(dst[:, col:col + 1], dst[:, col:col + 1],
                                                        redt[:, ich % 2:ich % 2 + 1], ALU.max), "a_mx", 1)

                            agd["ps"][ich % 2] = ctx.semv["a_sc"]
                            agd["psx"][ich % 2] = ctx.semv["a_mx"]

                            @blk.sync
                            def _(s, n=n, half=half, ich=ich, kind=kind):
                                ctx.wait(s, "a_sc")
                                dd = kf_d if kind == "k" else qf_d
                                ctx.inc(s.dma_start(
                                    out=dd[(half * 8 + ich) * 128:(half * 8 + ich + 1) * 128,
                                           n * 512:(n + 1) * 512],
                                    in_=kout[:, ich % 2, :]), "a_ev", 16)
                            agd["kst"][ich % 2] = ctx.semv["a_ev"]
                    else:
                        for ts in range(4):
                            @blk.tensor
                            def _(t, n=n, half=half, ts=ts):
                                ctx.wait(t, "aw_ld", "xt_ld")
                                ctx.wait_val(t, "a_sc", agd["ps"][2 + ts % 2])
                                for vh in range(2):
                                    dstp = pv4[:, (ts % 2) * 1024 + vh * 512:(ts % 2) * 1024 + (vh + 1) * 512]
                                    for kc in range(16):
                                        mm = t.matmul(dstp, xt_b[n % 2][:, kc, ts * 128:(ts + 1) * 128],
                                                      wph[half % 2][:, kc, vh * 512:(vh + 1) * 512],
                                                      start=(kc == 0), stop=(kc == 15))
                                ctx.inc(mm, "a_t", 1)

                            @blk.scalar
                            def _(sc, ts=ts):
                                ctx.wait(sc, "a_t")
                                ctx.wait_val(sc, "a_ev", agd["vst"][ts % 2])
                                ctx.inc(sc.activation(vout[:, ts % 2, :],
                                                      pv4[:, (ts % 2) * 1024:(ts % 2 + 1) * 1024], AF.Copy), "a_sc", 1)
                            agd["ps"][2 + ts % 2] = ctx.semv["a_sc"]

                            @blk.sync
                            def _(s, n=n, half=half, ts=ts):
                                ctx.wait(s, "a_sc")
                                for j in range(2):
                                    ctx.inc(s.dma_start(
                                        out=v_qd[half * 2 + j, :, n * 4 + ts, :],
                                        in_=vout[:, ts % 2, j * 512:(j + 1) * 512]), "a_ev", 16)
                            agd["vst"][ts % 2] = ctx.semv["a_ev"]
                    agd["xt"][n % 2] = ctx.semv["a_t"]
                agd["wph"][half % 2] = ctx.semv["a_t"]

        emit_proj("k", wkT, 8)
        emit_proj("v", wvT, 8)
        emit_proj("q", wqT, 2)

        @blk.tensor
        def _(t):
            ctx.wait(t, "a_mx", "a_t", "a_sc")
            for j in range(16):
                mm = t.transpose(pv4[0:1, j * 128:(j + 1) * 128], maxk[:, j:j + 1], idf_s[:])
            ctx.inc(mm, "n_t", 1)

        @blk.vector
        def _(v):
            ctx.wait(v, "n_t")
            for j in range(16):
                last = v.tensor_reduce(nrm[0:1, j:j + 1], pv4[0:1, j * 128:(j + 1) * 128], AX.X, ALU.max)
            ctx.inc(last, "n_v", 1)

        @blk.tensor
        def _(t):
            ctx.wait(t, "n_v")
            for j in range(16):
                mm = t.transpose(pv4[0:1, j * 128:(j + 1) * 128], maxq[:, j:j + 1], idf_s[:])
            ctx.inc(mm, "n_t2", 1)

        @blk.vector
        def _(v):
            ctx.wait(v, "n_t2")
            for j in range(16):
                last = v.tensor_reduce(nrm[0:1, 16 + j:17 + j], pv4[0:1, j * 128:(j + 1) * 128], AX.X, ALU.max)
            ctx.inc(last, "n_v2", 1)

        @blk.scalar
        def _(sc):
            ctx.wait(sc, "n_v2")
            ctx.inc(sc.activation(nrm[0:1, 0:32], nrm[0:1, 0:32], AF.Ln), "n_s", 1)

        @blk.vector
        def _(v):
            ctx.wait(v, "n_s")
            inst = v.tensor_tensor(nrm[0:1, 32:48], nrm[0:1, 0:16], nrm[0:1, 16:32], ALU.add)
            ctx.vsync(v, inst)
            ctx.inc(v.tensor_scalar(nrm[0:1, 32:48], nrm[0:1, 32:48], LN8 + math.log(64.0), -1.0, ALU.add, ALU.mult), "n_v3", 1)

        @blk.tensor
        def _(t):
            ctx.wait(t, "n_v3")
            ctx.inc(t.matmul(pb0[:, 0:16], aof[0:1, :], nrm[0:1, 32:48], start=True, stop=True), "n_t3", 1)

        @blk.vector
        def _(v):
            ctx.wait(v, "n_t3")
            ctx.inc(v.tensor_copy(negc[:], pb0[:, 0:16]), "n_v4", 1)

        for hh in range(16):
            buf = hh % 2

            @blk.gpsimd
            def _(g, hh=hh, buf=buf):
                ctx.wait(g, "n_v4", "a_ev")
                if hh >= 2:
                    ctx.wait_val(g, "g_t", 2 * (hh - 1))
                ctx.inc(g.dma_start(out=kf_h[:, buf, :], in_=kf_d[hh * 128:(hh + 1) * 128, :]), "kv_ld", 16)
                ctx.inc(g.dma_start(out=qf_h[:, buf, :], in_=qf_d[hh * 128:(hh + 1) * 128, :]), "kv_ld", 16)
                if hh % 4 == 0:
                    ctx.wait_val(g, "g_t", 2 * hh)
                    ctx.inc(g.dma_start(out=vt_quad[:, :, :], in_=v_qd[hh // 4]), "kv_ld", 16)

            for qb in range(2):
                sb0 = ctx.semv.get("b_t", 0)

                def emit_score(kt, hh=hh, buf=buf, qb=qb):
                    @blk.tensor
                    def _(t, kt=kt):
                        ctx.wait(t, "kv_ld")
                        if hh == 0 and qb == 0 and kt == 0:
                            ctx.wait(t, "n_v4")
                        if kt < 2:
                            if ctx.semv.get("g_v", 0) > 0:
                                ctx.wait(t, "g_v")
                        else:
                            ctx.wait(t, "b_s")
                        psc = psc_ev if kt % 2 == 0 else psc_od
                        ksl = slice(kt * 128, (kt + 1) * 128)
                        qsl = slice(qb * 512, (qb + 1) * 512)
                        t.matmul(psc[:, 0:512], kf_h[0:64, buf, ksl], qf_h[0:64, buf, qsl],
                                 start=True, stop=True, tile_position=(0, 0))
                        ctx.inc(t.matmul(psc[:, 512:1024], kf_h[64:128, buf, ksl], qf_h[64:128, buf, qsl],
                                         start=True, stop=True, tile_position=(64, 0)), "b_t", 1)

                emit_score(0)
                emit_score(1)
                for kt in range(NKT):
                    @blk.scalar
                    def _(sc, hh=hh, kt=kt, sb0=sb0):
                        ctx.wait_val(sc, "b_t", sb0 + kt + 1)
                        psc = psc_ev if kt % 2 == 0 else psc_od
                        ctx.inc(sc.activation(p_sb[:, kt % 2, :], psc[:, :], AF.Exp,
                                              scale=0.125, bias=negc[:, hh:hh + 1]), "b_s", 1)

                    @blk.tensor
                    def _(t, hh=hh, kt=kt):
                        ctx.wait(t, "b_s")
                        hq = hh % 4
                        for sl in range(4):
                            for a in range(2):
                                t.matmul(pso(sl, a), p_sb[:, kt % 2, a * 512 + sl * 128:a * 512 + (sl + 1) * 128],
                                         vt_quad[:, kt, hq * 128:(hq + 1) * 128],
                                         start=(kt == 0), stop=(kt == NKT - 1))
                        for a in range(2):
                            mm = t.matmul(pv4[0:1, 1024 + a * 512:1024 + (a + 1) * 512],
                                          ao_b[:, 0:1], p_sb[:, kt % 2, a * 512:(a + 1) * 512],
                                          start=(kt == 0), stop=(kt == NKT - 1))
                        if kt == NKT - 1:
                            ctx.inc(mm, "b_t2", 1)
                        else:
                            ctx.inc(mm, "b_av", 1)

                    if kt + 2 < NKT:
                        emit_score(kt + 2)

                @blk.vector
                def _(v):
                    ctx.wait(v, "b_t2")
                    inst = v.tensor_copy(rsum[0:1, :], prow)
                    ctx.vsync(v, inst)
                    ctx.inc(v.reciprocal(rsum[0:1, :], rsum[0:1, :]), "g_r", 1)

                @blk.tensor
                def _(t):
                    ctx.wait(t, "g_r")
                    for j in range(8):
                        mm = t.transpose(pb0[:, j:j + 1], rsum[0:1, j * 128:(j + 1) * 128],
                                         idf_s[0:1, 0:1])
                    ctx.inc(mm, "g_tt", 1)

                @blk.vector
                def _(v, hh=hh, qb=qb):
                    ctx.wait(v, "g_tt")
                    inst = v.tensor_copy(stat[:, 0:8], pb0[:, 0:8])
                    ctx.vsync(v, inst)
                    for sl in range(4):
                        v.tensor_scalar_mul(att1[:, sl, :], pso(sl, 0), stat[:, sl:sl + 1])
                        v.tensor_scalar_mul(att2[:, sl, :], pso(sl, 1), stat[:, 4 + sl:5 + sl])
                    for sl in range(4):
                        last = v.scalar_tensor_tensor(attc[:, sl, :], att2[:, sl, :], lamv[:, 0:1],
                                                      att1[:, sl, :], ALU.mult, ALU.add)
                    ctx.inc(last, "g_v", 1)

                @blk.vector
                def _(v):
                    ctx.wait(v, "g_v")
                    for sl in range(4):
                        last = v.tensor_reduce(stat[:, 8 + sl:9 + sl], attc[:, sl, :], AX.X, ALU.add)
                    ctx.inc(last, "g_v2", 1)

                @blk.vector
                def _(v):
                    ctx.wait(v, "g_v2")
                    for sl in range(4):
                        v.tensor_tensor(att1[:, sl, :], attc[:, sl, :], attc[:, sl, :], ALU.mult)
                    for sl in range(4):
                        last = v.tensor_reduce(stat[:, 12 + sl:13 + sl], att1[:, sl, :], AX.X, ALU.add)
                    ctx.inc(last, "g_s", 1)

                @blk.vector
                def _(v):
                    ctx.wait(v, "g_s")
                    v.tensor_scalar_mul(st2[:, 0:4], stat[:, 8:12], 1.0 / 128.0)
                    inst = v.tensor_scalar(st2[:, 4:8], stat[:, 12:16], 1.0 / 128.0, 1e-5, ALU.mult, ALU.add)
                    ctx.vsync(v, inst)
                    inst = v.tensor_tensor(st2[:, 8:12], st2[:, 0:4], st2[:, 0:4], ALU.mult)
                    ctx.vsync(v, inst)
                    ctx.inc(v.tensor_tensor(st2[:, 4:8], st2[:, 4:8], st2[:, 8:12], ALU.subtract), "g_v3", 1)

                @blk.scalar
                def _(sc):
                    ctx.wait(sc, "g_v3")
                    ctx.inc(sc.activation(st2[:, 4:8], st2[:, 4:8], AF.Sqrt), "g_s2", 1)

                @blk.vector
                def _(v, hh=hh):
                    ctx.wait(v, "g_s2")
                    inst = v.reciprocal(st2[:, 4:8], st2[:, 4:8])
                    ctx.vsync(v, inst)
                    for sl in range(4):
                        v.tensor_single_scalar(attc[:, sl, :], attc[:, sl, :], st2[:, sl:sl + 1], ALU.subtract)
                    for sl in range(4):
                        v.tensor_scalar_mul(attc[:, sl, :], attc[:, sl, :], st2[:, 4 + sl:5 + sl])
                    for sl in range(4):
                        v.tensor_tensor(att1[:, sl, :], attc[:, sl, :], gnw_s[:, hh, :], ALU.mult)
                    for sl in range(4):
                        last = v.tensor_tensor(att1[:, sl, :], att1[:, sl, :], gnb_s[:, hh, :], ALU.add)
                    ctx.inc(last, "g_v4", 1)

                @blk.tensor
                def _(t, hh=hh, qb=qb):
                    ctx.wait(t, "g_v4")
                    if ctx.semv.get("g_v5", 0) > 0:
                        ctx.wait(t, "g_v5")
                    for sl in range(4):
                        mm = t.transpose(pv4[:, 1024 + sl * 128:1024 + (sl + 1) * 128],
                                         att1[:, sl, :], idf_s[:])
                    ctx.inc(mm, "g_t", 1)

                @blk.vector
                def _(v):
                    ctx.wait(v, "g_t")
                    if ctx.semv.get("at_st", 0) > 0:
                        ctx.wait(v, "at_st")
                    ctx.inc(v.tensor_copy(
                        atstg[:], pv4[:, 1024:1536].rearrange("p (a q) -> p a q", a=4)), "g_v5", 1)

                @blk.sync
                def _(s, hh=hh, qb=qb):
                    ctx.wait(s, "g_v5")
                    ctx.inc(s.dma_start(
                        out=att_t[qb, :, hh, :],
                        in_=atstg[:].rearrange("p a q -> p (a q)")), "at_st", 16)

        pb = l4pb

        def emit_quant(tag, nch, src_s, dst_s):
            for kk in range(nch):
                @blk.vector
                def _(v, kk=kk, tag=tag):
                    ctx.wait(v, f"{tag}_pre")
                    v.tensor_tensor(t1f[:, 0, :], src_s[:, kk, :], s127_t[:], ALU.mult)
                    v.tensor_copy(kib[:, 0, :], t1f[:, 0, :])
                    v.tensor_copy(kfb[:, 0, :], kib[:, 0, :])
                    v.tensor_tensor(t1f[:, 0, :], t1f[:, 0, :], kfb[:, 0, :], ALU.subtract)
                    v.scalar_tensor_tensor(cbf[:], t1f[:, 0, :], 0.5, kfb[:, 0, :], ALU.is_gt, ALU.add)
                    v.scalar_tensor_tensor(kfb[:, 0, :], t1f[:, 0, :], -0.5, cbf[:], ALU.is_lt, ALU.subtract)
                    last = v.tensor_tensor(dst_s[:, kk, :], kfb[:, 0, :], dq_t[:], ALU.mult)
                    ctx.inc(last, f"{tag}_q", 1)

        def emit_norm_amax(tag, nch, src_s):
            for grp in range((nch + 3) // 4):
                @blk.scalar
                def _(sc, grp=grp, tag=tag, nch=nch):
                    ctx.wait(sc, f"{tag}_pre0")
                    if grp > 0:
                        ctx.wait(sc, f"{tag}_t")
                    ids = [grp * 4 + i for i in range(4) if grp * 4 + i < nch]
                    for i, kk in enumerate(ids):
                        last = sc.activation(sqf[:, i, :], src_s[:, kk, :], AF.Square)
                    ctx.inc(last, f"{tag}_s", 1)

                @blk.tensor
                def _(t, grp=grp, tag=tag, nch=nch):
                    ctx.wait(t, f"{tag}_s")
                    ids = [grp * 4 + i for i in range(4) if grp * 4 + i < nch]
                    for i, kk in enumerate(ids):
                        mm = t.matmul(pb[5], aof[:], sqf[:, i, :], start=(kk == 0), stop=(kk == nch - 1))
                    ctx.inc(mm, f"{tag}_t", 1)

            @blk.vector
            def _(v, tag=tag, nch=nch):
                ctx.wait(v, f"{tag}_t")
                ctx.inc(v.tensor_scalar(rstd_t[:], pb[5], 1.0 / (nch * 128.0), 1e-6, ALU.mult, ALU.add), f"{tag}_v", 1)

            @blk.scalar
            def _(sc, tag=tag):
                ctx.wait(sc, f"{tag}_v")
                ctx.inc(sc.activation(rstd_t[:], rstd_t[:], AF.Sqrt), f"{tag}_s2", 1)

            @blk.vector
            def _(v, tag=tag):
                ctx.wait(v, f"{tag}_s2")
                ctx.inc(v.memset(amax[:], 0.0), f"{tag}_v2", 1)

            for kk in range(nch):
                @blk.scalar
                def _(sc, kk=kk, tag=tag):
                    ctx.wait(sc, f"{tag}_v2")
                    if kk > 0:
                        ctx.wait(sc, f"{tag}_tt")
                    ctx.inc(sc.activation(sqf[:, 3, :], src_s[:, kk, :], AF.Abs), f"{tag}_as", 1)

                @blk.tensor
                def _(t, kk=kk, tag=tag):
                    ctx.wait(t, f"{tag}_as")
                    if kk > 0:
                        ctx.wait(t, f"{tag}_av")
                    for ts in range(4):
                        mm = t.transpose(psc_od[:, ts * 128:(ts + 1) * 128],
                                         sqf[:, 3, ts * 128:(ts + 1) * 128], idf_s[:])
                    ctx.inc(mm, f"{tag}_tt", 1)

                @blk.vector
                def _(v, kk=kk, tag=tag):
                    ctx.wait(v, f"{tag}_tt")
                    for ts in range(4):
                        v.tensor_reduce(tmax[:, ts:ts + 1], psc_od[:, ts * 128:(ts + 1) * 128], AX.X, ALU.max)
                    for ts in range(4):
                        last = v.tensor_tensor(amax[:, ts:ts + 1], amax[:, ts:ts + 1], tmax[:, ts:ts + 1], ALU.max)
                    ctx.inc(last, f"{tag}_av", 1)

            @blk.tensor
            def _(t, tag=tag):
                ctx.wait(t, f"{tag}_av")
                for ts in range(4):
                    mm = t.transpose(pb[4][0:1, ts * 128:(ts + 1) * 128], amax[:, ts:ts + 1], idf_s[:])
                ctx.inc(mm, f"{tag}_t2", 1)

            @blk.vector
            def _(v, tag=tag):
                ctx.wait(v, f"{tag}_t2")
                inst = v.tensor_scalar_max(rowa[0:1, :], pb[4][0:1, 0:512], 1e-5)
                ctx.vsync(v, inst)
                ctx.inc(v.reciprocal(rowa[0:1, :], rowa[0:1, :]), f"{tag}_v3", 1)

            @blk.tensor
            def _(t, tag=tag):
                ctx.wait(t, f"{tag}_v3")
                ctx.inc(t.matmul(pb[4], aof[0:1, :], rowa[0:1, :], start=True, stop=True), f"{tag}_t3", 1)

            @blk.vector
            def _(v, tag=tag):
                ctx.wait(v, f"{tag}_t3")
                inst = v.tensor_scalar_mul(s127_t[:], pb[4], 127.0)
                ctx.vsync(v, inst)
                inst = v.reciprocal(t1f[:, 0, :], s127_t[:])
                ctx.vsync(v, inst)
                v.reciprocal(rstd_t[:], rstd_t[:])
                ctx.inc(v.tensor_tensor(dq_t[:], rstd_t[:], t1f[:, 0, :], ALU.mult), f"{tag}_pre", 1)

        for st in range(NST):
            t0 = st * ST

            @blk.gpsimd
            def _(g, t0=t0, st=st):
                ctx.wait(g, "at_st", "i_v2")
                if st > 0:
                    ctx.wait(g, "out_st")
                ctx.inc(g.dma_start(out=att_s[:], in_=att_t[st]), "in_ld", 16)
                ctx.inc(g.dma_start(out=x_s[:], in_=x2[st]), "in_ld", 16)

            for m in range(16):
                @blk.gpsimd
                def _(g, m=m):
                    ctx.wait(g, "in_ld")
                    ctx.wait_val(g, "woq_t", ctx.semv.get("woq_t", 0) - 1)
                    ctx.inc(g.dma_start(out=wo_b[m % 2][:], in_=wo[m]), "woq_ld", 16)

                @blk.tensor
                def _(t, m=m):
                    ctx.wait(t, "woq_ld")
                    ctx.wait_val(t, "woq_v", ctx.semv.get("woq_v", 0) - 1)
                    for k in range(16):
                        mm = t.matmul(pb[m % 2], wo_b[m % 2][:, k, :], att_s[:, k, :],
                                      start=(k == 0), stop=(k == 15))
                    ctx.inc(mm, "woq_t", 1)

                @blk.vector
                def _(v, m=m):
                    ctx.wait(v, "woq_t")
                    ctx.inc(v.tensor_tensor(x_s[:, m, :], x_s[:, m, :], pb[m % 2], ALU.add), "woq_v", 1)

            @blk.vector
            def _(v):
                ctx.wait(v, "woq_v")
                ctx.inc(v.memset(sqf[:, 0, 0:1], 0.0), "q1_pre0", 1)
            emit_norm_amax("q1", 16, x_s)
            emit_quant("q1", 16, x_s, kq_s)

            for m in range(44):
                @blk.sync
                def _(s, m=m):
                    ctx.wait(s, "q1_q")
                    ctx.wait_val(s, "gu_t", ctx.semv.get("gu_t", 0) - 1)
                    ctx.inc(s.dma_start(out=wt_b[m % 2][:], in_=t1T[m]), "gu_ld", 16)
                    ctx.inc(s.dma_start(out=wu_b[m % 2][:], in_=t2T[m]), "gu_ld", 16)

                @blk.tensor
                def _(t, m=m):
                    ctx.wait(t, "gu_ld")
                    ctx.wait_val(t, "h_v", ctx.semv.get("h_v", 0) - 1)
                    pg, pu = pb[2 + 2 * (m % 2)], pb[3 + 2 * (m % 2)]
                    for k in range(16):
                        t.matmul(pg, wt_b[m % 2][:, k, :], kq_s[:, k, :], start=(k == 0), stop=(k == 15))
                    for k in range(16):
                        mm = t.matmul(pu, wu_b[m % 2][:, k, :], kq_s[:, k, :], start=(k == 0), stop=(k == 15))
                    ctx.inc(mm, "gu_t", 1)

                @blk.scalar
                def _(sc, m=m):
                    ctx.wait(sc, "gu_t")
                    ctx.wait(sc, "h_v")
                    ctx.inc(sc.activation(gt_s[:], pb[2 + 2 * (m % 2)], AF.Silu, scale=mcol[:, 0:1]), "gu_s", 1)

                @blk.vector
                def _(v, m=m):
                    ctx.wait(v, "gu_s")
                    v.tensor_scalar_mul(ut_s[:], pb[3 + 2 * (m % 2)], mcol[:, 1:2])
                    ctx.inc(v.tensor_tensor(h_s[:, m, :], gt_s[:], ut_s[:], ALU.mult), "h_v", 1)

            @blk.vector
            def _(v):
                ctx.wait(v, "h_v")
                ctx.inc(v.memset(sqf[:, 0, 0:1], 0.0), "q2_pre0", 1)
            emit_norm_amax("q2", 44, h_s)
            emit_quant("q2", 44, h_s, h_s)

            for m in range(16):
                @blk.sync
                def _(s, m=m):
                    ctx.wait(s, "q2_q")
                    ctx.wait_val(s, "dn_t", ctx.semv.get("dn_t", 0) - 1)
                    ctx.inc(s.dma_start(out=wd_b[m % 2][:], in_=t3T[m]), "dn_ld", 16)

                @blk.tensor
                def _(t, m=m):
                    ctx.wait(t, "dn_ld")
                    ctx.wait_val(t, "dn_v", ctx.semv.get("dn_v", 0) - 1)
                    for k in range(44):
                        mm = t.matmul(pb[m % 2], wd_b[m % 2][:, k, :], h_s[:, k, :],
                                      start=(k == 0), stop=(k == 43))
                    ctx.inc(mm, "dn_t", 1)

                @blk.vector
                def _(v, m=m):
                    ctx.wait(v, "dn_t")
                    ctx.inc(v.scalar_tensor_tensor(x_s[:, m, :], pb[m % 2], mcol[:, 2:3],
                                                   x_s[:, m, :], ALU.mult, ALU.add), "dn_v", 1)

            @blk.sync
            def _(s, st=st):
                ctx.wait(s, "dn_v")
                ctx.inc(s.dma_start(out=x4_o[st], in_=x_s[:]), "out_st", 16)

        @blk.gpsimd
        def _(g):
            ctx.wait(g, "out_st")
    return nc


def prep_l0attn(inputs):
    cst = _consts()
    x = np.asarray(inputs["L0_attn_hidden_states"])
    wqkv = np.asarray(inputs["L0_attn_W_qkv"])
    wz = np.asarray(inputs["L0_attn_W_z"])
    wb = np.asarray(inputs["L0_attn_W_b"])
    wa = np.asarray(inputs["L0_attn_W_a"])
    cw = np.asarray(inputs["L0_attn_conv_w"])
    nw = np.asarray(inputs["L0_attn_norm_w"])
    alog = np.asarray(inputs["L0_attn_A_log"])
    dtb = np.asarray(inputs["L0_attn_dt_bias"])
    in_maps = []
    for c in range(8):
        b, q = c // 4, c % 4
        qs = slice(4 * q * 128, (4 * q + 4) * 128)
        vs = slice(8 * q * 128, (8 * q + 8) * 128)
        hv = slice(8 * q, 8 * q + 8)
        chan = np.concatenate([
            np.arange(qs.start, qs.stop),
            2048 + np.arange(qs.start, qs.stop),
            4096 + np.arange(vs.start, vs.stop)])
        cwT = np.ascontiguousarray(
            cw[chan, :].reshape(16, 128, KTAP).transpose(1, 2, 0)).astype(np.float32)
        m = {
            "x": tile_fm(x[b].T.astype(np.float32), 16, 512),
            "wqT": tile_fm(wqkv[qs, :].T, 16, 512)[0],
            "wkT": tile_fm(wqkv[2048 + qs.start:2048 + qs.stop, :].T, 16, 512)[0],
            "wvT": tile_fm(wqkv[4096 + vs.start:4096 + vs.stop, :].T, 16, 1024)[0],
            "wzT": tile_fm(wz[vs, :].T, 16, 512),
            "wbaT": tile_fm(
                np.concatenate([wb[hv, :], wa[hv, :]], axis=0).T, 16, 16)[0],
            "cwT": cwT,
            "nwrep": np.tile(nw, 8)[None, :].astype(np.float32),
            "alog": np.ascontiguousarray(alog[hv])[None, :].astype(np.float32),
            "dtb": np.ascontiguousarray(dtb[hv])[None, :].astype(np.float32),
            "ident": cst["ident"], "allones": cst["allones"],
            "mus8": cst["mus8"], "mui8": cst["mui8"],
            "trilones": cst["trilones"], "onehot8": cst["onehot8"],
        }
        in_maps.append(m)
    return in_maps


def run_l0attn(inputs):
    nc = build_l0attn()
    in_maps = prep_l0attn(inputs)
    res = run_bass_kernel_spmd(nc, in_maps, core_ids=list(range(8)))
    og = np.zeros((B, S, VAL_DIM), BF)
    for c in range(8):
        b, q = c // 4, c % 4
        og[b, :, 8 * q * 128:(8 * q + 8) * 128] = res.results[c]["og"]
    return og


def prep_w(inputs):
    cst = _consts()
    w1 = np.asarray(inputs["L1_ffn_w1"])
    w2 = np.asarray(inputs["L1_ffn_w2"])
    w3 = np.asarray(inputs["L1_ffn_w3"])
    RW1, RW3 = INTER // 8, D // 8
    maps = []
    for c in range(8):
        maps.append({
            "w1": np.ascontiguousarray(w1[c * RW1:(c + 1) * RW1, :]),
            "w2": np.ascontiguousarray(w2[c * RW1:(c + 1) * RW1, :]),
            "w3": np.ascontiguousarray(w3[c * RW3:(c + 1) * RW3, :]),
            "allones": cst["allones"],
        })
    return maps


def run_pipeline(inputs):
    cst = _consts()
    ncw = build_wsum()
    mw = prep_w(inputs)
    rw = run_bass_kernel_spmd(ncw, mw, core_ids=list(range(8)))
    wsums = np.stack([rw.results[c]["wsums"][0] for c in range(8)])

    nct = build_wtern()
    for c in range(8):
        mw[c]["wsums"] = wsums
    rt = run_bass_kernel_spmd(nct, mw, core_ids=list(range(8)))
    t1 = np.concatenate([rt.results[c]["t1"] for c in range(8)], axis=0)
    t2 = np.concatenate([rt.results[c]["t2"] for c in range(8)], axis=0)
    t3 = np.concatenate([rt.results[c]["t3"] for c in range(8)], axis=0)

    nc1 = build_l0attn()
    m1 = prep_l0attn(inputs)
    r1 = run_bass_kernel_spmd(nc1, m1, core_ids=list(range(8)))
    og = [r1.results[c]["og"] for c in range(8)]

    x = np.asarray(inputs["L0_attn_hidden_states"])
    wout = np.asarray(inputs["L0_attn_W_out"])
    gw = np.asarray(inputs["L0_ffn_gate_w"]); uw = np.asarray(inputs["L0_ffn_up_w"])
    dw = np.asarray(inputs["L0_ffn_down_w"])
    gs = np.asarray(inputs["L0_ffn_gate_s"]); us = np.asarray(inputs["L0_ffn_up_s"])
    ds = np.asarray(inputs["L0_ffn_down_s"])
    nc2 = build_l2()
    m2 = []
    woT = tile_w(np.ascontiguousarray(wout.T), 32)
    gwT = tile_w(np.ascontiguousarray(gw.T), 16)
    uwT = tile_w(np.ascontiguousarray(uw.T), 16)
    dwT = tile_w(np.ascontiguousarray(dw.T), 44)
    gsc = np.ascontiguousarray(gs.reshape(44, 128).T).astype(np.float32)
    usc = np.ascontiguousarray(us.reshape(44, 128).T).astype(np.float32)
    dsc = np.ascontiguousarray(ds.reshape(16, 128).T).astype(np.float32)
    for c in range(8):
        b, q = c // 4, c % 4
        tsl = slice(q * 1024, (q + 1) * 1024)
        og_fm = np.concatenate([og[b * 4 + j][tsl, :].T for j in range(4)], axis=0)
        m2.append({
            "og": tile_fm(og_fm, 32, 512),
            "x0": tile_fm(x[b, tsl, :].T.astype(np.float32), 16, 512),
            "woT": woT, "gwT": gwT, "uwT": uwT, "dwT": dwT,
            "gsc": gsc, "usc": usc, "dsc": dsc, "allones": cst["allones"],
        })
    r2 = run_bass_kernel_spmd(nc2, m2, core_ids=list(range(8)))
    x2 = [r2.results[c]["x2"] for c in range(8)]
    ln2 = [r2.results[c]["ln2"] for c in range(8)]

    wq1 = np.asarray(inputs["L1_attn_Wq"]); wk1 = np.asarray(inputs["L1_attn_Wk"])
    wv1 = np.asarray(inputs["L1_attn_Wv"])
    gnw = np.asarray(inputs["L1_attn_gn_weight"]); gnb = np.asarray(inputs["L1_attn_gn_bias"])
    wo1 = np.asarray(inputs["L1_attn_Wo"])
    nc34 = build_l3l4()
    t1T = tile_w(np.ascontiguousarray(t1.T), 16)
    t2T = tile_w(np.ascontiguousarray(t2.T), 16)
    t3T = tile_w(np.ascontiguousarray(t3.T), 44)
    wqf = tile_fm(wq1.astype(np.float32), 16, 1024)
    wkf = tile_fm(wk1.astype(np.float32), 16, 1024)
    wvf = tile_fm(wv1.astype(np.float32), 16, 1024)
    wof = tile_w(wo1.astype(np.float32), 16)
    gnwb = np.ascontiguousarray(np.broadcast_to(
        gnw.reshape(16, 128)[None, :, :], (128, 16, 128))).astype(np.float32)
    gnbb = np.ascontiguousarray(np.broadcast_to(
        gnb.reshape(16, 128)[None, :, :], (128, 16, 128))).astype(np.float32)
    m34 = []
    for c in range(8):
        b, q = c // 4, c % 4
        order = [2 * q, 2 * q + 1] + [g for g in range(8) if g // 2 != q]
        ln2r = np.stack([ln2[b * 4 + g // 2][g % 2] for g in order])
        m34.append({
            "ln2": np.ascontiguousarray(ln2r),
            "x2": x2[c],
            "wkT": wkf, "wvT": wvf, "wqT": wqf,
            "lq1": inputs["L1_attn_lambda_q1"][None, :].astype(np.float32),
            "lk1": inputs["L1_attn_lambda_k1"][None, :].astype(np.float32),
            "lq2": inputs["L1_attn_lambda_q2"][None, :].astype(np.float32),
            "lk2": inputs["L1_attn_lambda_k2"][None, :].astype(np.float32),
            "gnw": gnwb, "gnb": gnbb,
            "wo": wof,
            "t1T": t1T, "t2T": t2T, "t3T": t3T,
            "wsums": wsums,
            "allones": cst["allones"], "ident": cst["ident"],
        })
    r34 = run_bass_kernel_spmd(nc34, m34, core_ids=list(range(8)))
    out = np.zeros((B, S, D), np.float32)
    for c in range(8):
        b, q = c // 4, c % 4
        x4 = r34.results[c]["x4"]
        for st in range(2):
            out[b, q * 1024 + st * 512:q * 1024 + (st + 1) * 512, :] = \
                x4[st].transpose(1, 0, 2).reshape(D, 512).T
    return out


def kernel(**inputs):
    return run_pipeline(inputs)
